# version 22
# baseline (speedup 1.0000x reference)
import math
from contextlib import ExitStack

import numpy as np
import ml_dtypes
import concourse.bass as bass
import concourse.mybir as mybir
from concourse.bass_utils import run_bass_kernel_spmd

F32 = mybir.dt.float32
BF16 = mybir.dt.bfloat16
AF = mybir.ActivationFunctionType
ALU = mybir.AluOpType
AX = mybir.AxisListType

NCORES = 8
S = 8192
D = 2048
NH = 16
HD = 128
ROWS = S // NCORES
FT = D // 128
NKT = S // 128
QCH = 512
NQC = S // QCH
MAGIC = float(np.float32(1.5 * 2 ** 23))
SCL_Q = 1.0 / math.sqrt(HD)
SLOPES = [2.0 ** (-k / 2) for k in range(NH)]
LN_EPS = 1e-5
RMS_EPS = 1e-5
Q_EPS = 1e-8
FF0 = 4096
FF1 = 5632
BF = ml_dtypes.bfloat16


def heads_of(c):
    return [c, c + 8]


def pack_lhsT(A):
    A = np.ascontiguousarray(A)
    K, M = A.shape
    return np.ascontiguousarray(
        A.reshape(K // 128, 128, M // 128, 128).transpose(2, 1, 0, 3).reshape(M // 128, 128, K))



def _prep(inputs):
    g = {}
    x = np.ascontiguousarray(np.asarray(inputs["L0_attn_x"], np.float32).reshape(S, D))
    Wqkv = np.asarray(inputs["L0_attn_Wqkv"], np.float32)
    bqkv = np.asarray(inputs["L0_attn_bqkv"], np.float32)
    Wo0 = np.asarray(inputs["L0_attn_Wo"], np.float32)
    bo0 = np.asarray(inputs["L0_attn_bo"], np.float32)
    Wg = np.asarray(inputs["L0_ffn_Wg"], np.float32)
    W0a = np.asarray(inputs["L0_ffn_W0a"], np.float32)
    b0a = np.asarray(inputs["L0_ffn_b0a"], np.float32)
    W0b = np.asarray(inputs["L0_ffn_W0b"], np.float32)
    b0b = np.asarray(inputs["L0_ffn_b0b"], np.float32)
    W1a = np.asarray(inputs["L0_ffn_W1a"], np.float32)
    b1a = np.asarray(inputs["L0_ffn_b1a"], np.float32)
    W1b = np.asarray(inputs["L0_ffn_W1b"], np.float32)
    b1b = np.asarray(inputs["L0_ffn_b1b"], np.float32)
    ld = np.asarray(inputs["L1_attn_log_decay"], np.float32)
    Wq1 = np.asarray(inputs["L1_attn_Wq"], np.float32)
    Wk1 = np.asarray(inputs["L1_attn_Wk"], np.float32)
    Wv1 = np.asarray(inputs["L1_attn_Wv"], np.float32)
    Wo1 = np.asarray(inputs["L1_attn_Wo"], np.float32)
    Wg1 = np.asarray(inputs["L1_attn_Wg1"], np.float32)
    Wg2 = np.asarray(inputs["L1_attn_Wg2"], np.float32)
    nw = np.asarray(inputs["L1_attn_norm_w"], np.float32)
    w1 = np.asarray(inputs["L1_ffn_w1"], np.float32)
    w2 = np.asarray(inputs["L1_ffn_w2"], np.float32)
    w3 = np.asarray(inputs["L1_ffn_w3"], np.float32)

    for c in range(NCORES):
        g[f"xfm_{c}"] = np.ascontiguousarray(x[c * ROWS:(c + 1) * ROWS].T)

    for c in range(NCORES):
        hs = heads_of(c)
        qcols = np.concatenate([Wqkv[:, 0 * D + h * HD:0 * D + (h + 1) * HD] for h in hs], 1)
        kcols = np.concatenate([Wqkv[:, 1 * D + h * HD:1 * D + (h + 1) * HD] for h in hs], 1)
        vcols = np.concatenate([Wqkv[:, 2 * D + h * HD:2 * D + (h + 1) * HD] for h in hs], 1)
        g[f"wqp_{c}"] = pack_lhsT(qcols)
        g[f"wkp_{c}"] = pack_lhsT(kcols)
        g[f"wv_{c}"] = np.ascontiguousarray(vcols)
        bq = np.concatenate([bqkv[0 * D + h * HD:0 * D + (h + 1) * HD] for h in hs])
        bk = np.concatenate([bqkv[1 * D + h * HD:1 * D + (h + 1) * HD] for h in hs])
        bv = np.concatenate([bqkv[2 * D + h * HD:2 * D + (h + 1) * HD] for h in hs])
        g[f"bq_{c}"] = np.ascontiguousarray(bq.reshape(2, 128).T)
        g[f"bk_{c}"] = np.ascontiguousarray(bk.reshape(2, 128).T)
        g[f"bv_{c}"] = np.ascontiguousarray(bv.reshape(1, 256))
        sj = np.zeros((2, 128, NKT), np.float32)
        msh = np.zeros((1, 16), np.float32)
        for i_h, h in enumerate(hs):
            sl = np.float32(SLOPES[h])
            kk = np.arange(128, dtype=np.float64)[:, None]
            jj = np.arange(NKT, dtype=np.float64)[None, :]
            sj[i_h] = (sl * (jj * 128 + kk)).astype(np.float32)
            if i_h == 1:
                msh[0, i_h * 8:(i_h + 1) * 8] = sl * np.arange(8) * 1024.0
        g[f"sjtab_{c}"] = sj
        g[f"mshtab_{c}"] = msh
        g[f"negtab_{c}"] = (-np.float64(SLOPES[hs[0]]) *
                            np.arange(S, dtype=np.float64)).astype(BF).reshape(1, S)
    dm = np.zeros((8, 128, 1024), np.float32)
    kk = np.arange(128)[:, None]
    qq = np.arange(1024)[None, :]
    for t in range(8):
        dm[t] = (qq - kk >= 128 * t).astype(np.float32)
    g["diagmask"] = dm.astype(BF)

    g["wo0p"] = pack_lhsT(Wo0)
    order = [h for r in range(NCORES) for h in (r, r + 8)]
    g["wo0pg"] = pack_lhsT(Wo0.reshape(NH, HD, D)[order].reshape(D, D))
    g["bo0c"] = np.ascontiguousarray(bo0.reshape(FT, 128).T)
    g["w0ap"] = pack_lhsT(np.ascontiguousarray(W0a.T))
    g["w1ap"] = pack_lhsT(np.ascontiguousarray(W1a.T))
    g["w0bp"] = pack_lhsT(np.ascontiguousarray(W0b.T))
    g["w1bp"] = pack_lhsT(np.ascontiguousarray(W1b.T))
    g["b0ac"] = np.ascontiguousarray(b0a.reshape(32, 128).T)
    g["b1ac"] = np.ascontiguousarray(b1a.reshape(32, 128).T)
    g["b0bc"] = np.ascontiguousarray(b0b.reshape(FT, 128).T)
    g["b1bc"] = np.ascontiguousarray(b1b.reshape(FT, 128).T)
    g["wgt"] = np.ascontiguousarray(Wg.T.reshape(FT, 128, 2))

    for c in range(NCORES):
        hs = heads_of(c)
        g[f"wq1p_{c}"] = pack_lhsT(np.concatenate([Wq1[:, h * HD:(h + 1) * HD] for h in hs], 1))
        g[f"wk1p_{c}"] = pack_lhsT(np.concatenate([Wk1[:, h * HD:(h + 1) * HD] for h in hs], 1))
        g[f"wv1_{c}"] = np.ascontiguousarray(
            np.concatenate([Wv1[:, h * HD:(h + 1) * HD] for h in hs], 1))
        g[f"ld_{c}"] = np.ascontiguousarray(ld[hs].reshape(1, 2))
        g[f"wg2_{c}"] = np.ascontiguousarray(
            np.concatenate([Wg2[:, h * HD:(h + 1) * HD] for h in hs],
                           1)).astype(BF)
    g["wg1p"] = pack_lhsT(Wg1)

    g["wo1p"] = pack_lhsT(Wo1)
    g["nwc"] = np.ascontiguousarray(nw.reshape(FT, 128).T)
    nq = FF1 * D // NCORES // 128
    w1tp = pack_lhsT(np.ascontiguousarray(w1.T)).reshape(NCORES, 128, nq)
    w2tp = pack_lhsT(np.ascontiguousarray(w2.T)).reshape(NCORES, 128, nq)
    w3tp = pack_lhsT(np.ascontiguousarray(w3.T)).reshape(NCORES, 128, nq)
    for c in range(NCORES):
        g[f"w1s_{c}"] = np.ascontiguousarray(w1tp[c])
        g[f"w2s_{c}"] = np.ascontiguousarray(w2tp[c])
        g[f"w3s_{c}"] = np.ascontiguousarray(w3tp[c])
    g["identb"] = np.eye(128, dtype=BF)
    return g



class Sem:
    def __init__(self, stack, nc, name):
        self.h = stack.enter_context(nc.semaphore(name))
        self.n = 0

    def bump(self, k=1):
        self.n += k
        return self.n


class Prog:
    ENGS = ["sync", "gpsimd", "scalar", "vector", "tensor"]
    SELF = ("vector", "scalar", "gpsimd")

    def __init__(self, nc, stack):
        self.ops = {e: [] for e in self.ENGS}
        self.selfs = {e: Sem(stack, nc, f"ss_{e}") for e in self.SELF}

    @property
    def vs(self):
        return self.selfs["vector"]

    @property
    def ss(self):
        return self.selfs["scalar"]

    @property
    def gs(self):
        return self.selfs["gpsimd"]

    def op(self, eng, fn, waits=(), inc=None, inck=1, ordered=True, serial=True):
        ss = self.selfs.get(eng) if ordered else None
        assert ss is None or inc is None, "one sem update per instruction"
        if inc is not None and inck == 16 and serial:
            waits = list(waits) + [(inc, inc.n)]
        pre = ss.n if ss is not None else 0
        tgt = None
        if ss is not None:
            tgt = ss.bump(1)
        elif inc is not None:
            tgt = inc.bump(inck)

        def f(e, fn=fn, waits=tuple(waits), ss=ss, pre=pre, inc=inc, inck=inck):
            for s, v in waits:
                e.wait_ge(s.h, v)
            if ss is not None and pre > 0:
                e.wait_ge(ss.h, pre)
            ins = fn(e)
            if ss is not None:
                ins.then_inc(ss.h, 1)
            elif inc is not None:
                ins.then_inc(inc.h, inck)
        self.ops[eng].append(f)
        return tgt

    def run(self, nc):
        with nc.Block() as block:
            for e in self.ENGS:
                if self.ops[e]:
                    getattr(block, e)(lambda eng, ops=tuple(self.ops[e]): [f(eng) for f in ops])



def mk_pool(nc, stack):
    pAB = stack.enter_context(nc.psum_tensor("g_pAB", [128, 1024], F32))
    pCD = stack.enter_context(nc.psum_tensor("g_pCD", [128, 1024], F32))
    pOP = stack.enter_context(nc.psum_tensor("g_pOP", [128, 1024], F32))
    pD1 = stack.enter_context(nc.psum_tensor("g_pD1", [128, 512], F32))
    ptr = stack.enter_context(nc.psum_tensor("g_ptr", [128, 1024], BF16))
    return {
        "pAB": pAB[:, :], "pCD": pCD[:, :], "pOP": pOP[:, :],
        "A0": pAB[:, 0:512], "A1": pAB[:, 512:1024],
        "C0": pCD[:, 0:512], "C1": pCD[:, 512:1024],
        "O0": pOP[:, 0:512], "O1": pOP[:, 512:1024],
        "pD1": pD1[0:1, :], "pDf": pD1[:, :], "ptr": ptr[:, :],
        "pD2v": pOP[0:1, 512:1024], "pGv": pCD[0:1, 512:1024],
    }


class Env:

    def __init__(self, ctx, prefix, psmap):
        self.ctx = ctx
        self.prefix = prefix
        self.psmap = psmap
        if ctx is None:
            self.nc = bass.Bass("TRN2", target_bir_lowering=False)
            self._own_stack = ExitStack()
            self.gst = self._own_stack
            self.P = None
        else:
            self.nc = ctx["nc"]
            self.P = ctx["P"]
            self.gst = ctx["gst"]
        self._scope = None
        self._pool = None

    def dram(self, name, shape, dt, kind):
        if self.ctx is None:
            k = {"in": "ExternalInput", "out": "ExternalOutput",
                 "int": "Internal"}[kind]
            return self.nc.dram_tensor(name, shape, dt, kind=k)
        return self.ctx["ext"][self.prefix + name]

    def scope(self):
        self._scope = ExitStack()
        if self.ctx is None:
            self._own_stack.__enter__()
            self.P = Prog(self.nc, self._own_stack)
        return self._scope

    def sb(self, name, shape, dt):
        return self._scope.enter_context(
            self.nc.sbuf_tensor(self.prefix + name, shape, dt))

    def ps(self, name, shape=None, dt=F32):
        if self.ctx is None:
            if self._pool is None:
                self._pool = mk_pool(self.nc, self._scope)
            return self._pool[self.psmap[name]]
        return self.ctx["pss"][self.psmap[name]]

    def sem(self, name):
        sm = Sem(self.gst, self.nc, self.prefix + name)
        if self.ctx is not None:
            self.ctx["all"].append(sm)
        return sm

    def finish(self):
        self._scope.close()
        if self.ctx is None:
            self.P.run(self.nc)
            self._own_stack.close()
            return self.nc
        return None


def barrier(P, sems):
    snap = [(s, s.n) for s in sems if s.n > 0]
    for eng in Prog.ENGS:
        P.op(eng, lambda e: e.nop(), waits=snap, ordered=False)



def emit_ln_fm(P, bufs, pe, src, dst, ncols, after=(), gate=None,
               norm_after=()):
    ones128f, ones1f = bufs["ones128f"], bufs["ones1f"]
    sq, mu_s, rstd_s = bufs["lnsq"], bufs["ln_mu"], bufs["ln_rstd"]
    mu_r, e2_r, var_r, rstd_r = bufs["ln_r1"], bufs["ln_r2"], bufs["ln_r3"], bufs["ln_r4"]
    pS, pQ, pB1, pB2 = bufs["ln_pS"], bufs["ln_pQ"], bufs["ln_pB1"], bufs["ln_pB2"]
    nch = ncols // 512
    prev_norm_done = P.vs.n
    prev_stats = P.vs.n
    pq_abs = {}
    dg_abs = {}
    last = P.vs.n
    for ci in range(nch):
        first = list(after) if ci == 0 else []
        for kt in range(FT):
            s_in = src[:, kt * ncols + ci * 512: kt * ncols + ci * 512 + 512]
            sl = kt % 2
            s_sq = sq[:, sl * 512:(sl + 1) * 512]
            w = list(first) + ([(P.vs, prev_stats)] if kt == 0 else [])
            P.op("tensor", lambda e, r=s_in, kt=kt: e.matmul(
                pS[:, :], ones128f[:, :], r, start=(kt == 0), stop=(kt == FT - 1)),
                waits=w)
            idx = ci * FT + kt
            w = list(first) + ([(P.vs, prev_norm_done)] if kt == 0 else []) + (
                [(pe, pq_abs[idx - 2])] if idx - 2 in pq_abs else [])
            sc_sq = P.op("scalar", lambda e, a=s_in, b=s_sq: e.activation(b, a, AF.Square),
                         waits=w)
            pq_abs[idx] = P.op("tensor", lambda e, r=s_sq, kt=kt: e.matmul(
                pQ[:, :], ones128f[:, :], r, start=(kt == 0), stop=(kt == FT - 1)),
                waits=[(P.ss, sc_sq)] + ([(P.vs, prev_stats)] if kt == 0 else []),
                inc=pe)
        pe_sums = pq_abs[ci * FT + FT - 1]
        P.op("vector", lambda e: e.tensor_scalar_mul(mu_r[:, :], pS[:, :], 1.0 / D),
             waits=[(pe, pe_sums)])
        P.op("vector", lambda e: e.tensor_scalar_mul(e2_r[:, :], pQ[:, :], 1.0 / D))
        P.op("vector", lambda e: e.tensor_mul(var_r[:, :], mu_r[:, :], mu_r[:, :]))
        P.op("vector", lambda e: e.tensor_sub(var_r[:, :], e2_r[:, :], var_r[:, :]))
        P.op("vector", lambda e: e.tensor_scalar_add(var_r[:, :], var_r[:, :], LN_EPS))
        ve_recip = P.op("vector", lambda e: e.reciprocal(rstd_r[:, :], var_r[:, :]))
        prev_stats = ve_recip
        sc_rstd = P.op("scalar", lambda e: e.activation(rstd_r[:, :], rstd_r[:, :], AF.Sqrt),
                       waits=[(P.vs, ve_recip)])
        P.op("tensor", lambda e: e.matmul(pB1[:, :], ones1f[:, :], mu_r[:, :],
                                          start=True, stop=True),
             waits=[(P.vs, ve_recip)])
        pe_bc = P.op("tensor", lambda e: e.matmul(pB2[:, :], ones1f[:, :], rstd_r[:, :],
                                                  start=True, stop=True),
                     waits=[(P.ss, sc_rstd)], inc=pe)
        P.op("vector", lambda e: e.tensor_copy(mu_s[:, :], pB1[:, :]), waits=[(pe, pe_bc)])
        P.op("vector", lambda e: e.tensor_copy(rstd_s[:, :], pB2[:, :]))
        for kt in range(FT):
            sl = kt % 2
            s_in = src[:, kt * ncols + ci * 512: kt * ncols + ci * 512 + 512]
            s_tmp = sq[:, sl * 512:(sl + 1) * 512]
            s_out = dst[:, kt * ncols + ci * 512: kt * ncols + ci * 512 + 512]
            idx = ci * FT + kt
            w = [(pe, dg_abs[idx - 2])] if (gate and idx - 2 in dg_abs) else []
            if idx == 0:
                w = w + list(norm_after)
            t1 = P.op("vector", lambda e, a=s_in, t=s_tmp: e.tensor_sub(t, a, mu_s[:, :]),
                      waits=w)
            last = P.op("vector",
                        lambda e, t=s_tmp, o=s_out: e.tensor_mul(o, t, rstd_s[:, :]))
            if gate:
                dg_abs[idx] = P.op("tensor", lambda e, t=s_tmp, kt=kt, ci=ci: e.matmul(
                    gate["pG"][0:1, :], gate["wd"][:, kt:kt + 1], t,
                    start=(kt == 0), stop=(kt == FT - 1)),
                    waits=[(P.vs, t1)] + ([(P.vs, gate.get("_mev", -1))] if (kt == 0 and "_mev" in gate) else []),
                    inc=pe)
        if gate:
            gm = dg_abs[ci * FT + FT - 1]
            moff = gate.get("moff", 0)
            gate["_mev"] = P.op("vector", lambda e, ci=ci, moff=moff: e.tensor_scalar(
                gate["mrow"][:, moff + ci * 512: moff + (ci + 1) * 512],
                gate["pG"][0:1, :], 0.0, None,
                ALU.is_ge), waits=[(pe, gm)])
            last = gate["_mev"]
        prev_norm_done = last
    return P.vs, last



def build_l1(ctx=None):
    env = Env(ctx, "p1_", {"ln_pS": "pD1", "ln_pQ": "pD2v", "ln_pB1": "C0",
                           "ln_pB2": "A0", "pG": "pGv"})
    nc = env.nc
    xfm = env.dram("xfm", [D, ROWS], F32, "in")
    wgt = env.dram("wgt", [FT, 128, 2], F32, "in")
    xn0ch = [env.dram("xn0c0", [D, 512], BF16, "out"),
             env.dram("xn0c1", [D, 512], BF16, "out")]
    maskc = env.dram("maskc", [1, ROWS], F32, "out")
    with env.scope():
        sb, ps = env.sb, env.ps
        bufs = {
            "ones128f": sb("ones128f", [128, 1], F32),
            "ones1f": sb("ones1f", [1, 128], F32),
            "lnsq": sb("lnsq", [128, 2 * 512], F32),
            "ln_mu": sb("ln_mu", [128, 512], F32),
            "ln_rstd": sb("ln_rstd", [128, 512], F32),
            "ln_r1": sb("ln_r1", [1, 512], F32),
            "ln_r2": sb("ln_r2", [1, 512], F32),
            "ln_r3": sb("ln_r3", [1, 512], F32),
            "ln_r4": sb("ln_r4", [1, 512], F32),
            "ln_pS": ps("ln_pS", [1, 512]),
            "ln_pQ": ps("ln_pQ", [1, 512]),
            "ln_pB1": ps("ln_pB1", [128, 512]),
            "ln_pB2": ps("ln_pB2", [128, 512]),
        }
        resc = [sb(f"resc{i}", [128, FT * 512], F32) for i in range(2)]
        xnb = sb("xnb", [128, FT * 512], BF16)
        wg_s = sb("wg_s", [128, FT * 2], F32)
        wd_s = sb("wd_s", [128, FT], F32)
        mrow = sb("mrow", [1, ROWS], F32)
        pG = ps("pG", [1, 512])
        P = env.P
        pe = env.sem("pe")
        ld = env.sem("ld")
        lds = [env.sem("ldr0"), env.sem("ldr1")]
        stv = env.sem("stv")
        P.op("gpsimd", lambda e: e.memset(bufs["ones128f"][:, :], 1.0))
        const_ok = P.op("gpsimd", lambda e: e.memset(bufs["ones1f"][:, :], 1.0))
        ld_w = P.op("sync", lambda e: e.dma_start(
            out=wg_s.ap().rearrange("p (t e) -> p t e", e=2),
            in_=wgt.ap().rearrange("t p e -> p t e")), inc=ld, inck=16)
        ldcv = []
        for ch in range(2):
            ldcv.append(P.op("sync", lambda e, ch=ch: e.dma_start(
                out=resc[ch].ap().rearrange("p (t n) -> p t n", n=512),
                in_=xfm[:, ch * 512:(ch + 1) * 512].rearrange(
                    "(t p) n -> p t n", p=128)),
                inc=lds[ch], inck=16, serial=False))
        P.op("vector", lambda e: e.tensor_sub(
            wd_s.ap().rearrange("p (t o) -> p t o", o=1),
            wg_s.ap().rearrange("p (t e) -> p t e", e=2)[:, :, 0:1],
            wg_s.ap().rearrange("p (t e) -> p t e", e=2)[:, :, 1:2]),
            waits=[(ld, ld_w)])
        gate = {"wd": wd_s, "pG": pG}
        xn0_st = []
        st_prev = 0
        gate["mrow"] = mrow
        for ch in range(2):
            gate["moff"] = ch * 512
            vsem, done = emit_ln_fm(
                P, bufs, pe, resc[ch], xnb, 512,
                after=[(lds[ch], ldcv[ch]), (P.gs, const_ok)], gate=gate,
                norm_after=([(stv, st_prev)] if ch else []))
            P.op("sync", lambda e, ch=ch: e.dma_start(
                out=xn0ch[ch].ap().rearrange("(t p) n -> p t n", p=128),
                in_=xnb.ap().rearrange("p (t n) -> p t n", n=512)),
                waits=[(vsem, done)], inc=stv, inck=16)
            st_prev = stv.n
            xn0_st.append((stv, stv.n))
        if ctx is not None:
            ctx["xn0_st"] = xn0_st
        stx = P.op("sync", lambda e: e.dma_start(out=maskc[:, :], in_=mrow[:, :]),
                   waits=[(P.vs, P.vs.n)], inc=stv, inck=16)
        P.op("sync", lambda e: e.nop(), waits=[(stv, stx)])
    return env.finish()



def build_l2(ctx=None):
    env = Env(ctx, "p2_", {"pAB": "pAB", "pCD": "pCD", "pOP": "pOP",
                           "pD": "pD1", "pDf": "pDf", "A0": "A0", "A1": "A1",
                           "C0": "C0", "C1": "C1", "O0": "O0", "O1": "O1"})
    nc = env.nc
    wqp = env.dram("wqp", [2, 128, D], F32, "in")
    wkp = env.dram("wkp", [2, 128, D], F32, "in")
    wv = env.dram("wv", [D, 256], F32, "in")
    bq = env.dram("bq", [128, 2], F32, "in")
    bk = env.dram("bk", [128, 2], F32, "in")
    bv = env.dram("bv", [1, 256], F32, "in")
    sjtab = env.dram("sjtab", [2, 128, NKT], F32, "in")
    mshtab = env.dram("mshtab", [1, 16], F32, "in")
    negtab = env.dram("negtab", [1, S], BF16, "in")
    diagmask = env.dram("diagmask", [8, 128, 1024], BF16, "in")
    if ctx is None:
        xn0full = env.dram("xn0full", [D, S], BF16, "in")
        o0c = env.dram("o0c", [256, S], BF16, "out")
        def xn_src(tc):
            return xn0full[:, tc * 512:(tc + 1) * 512].rearrange(
                "(t p) n -> p t n", p=128)
        def o0_dst(hh, ci):
            return o0c[hh * 128:(hh + 1) * 128, ci * 1024:(ci + 1) * 1024]
    else:
        xn_src = ctx["xn0_src"]
        o0_dst = ctx["o0_dst"]
    with env.scope():
        sb, ps = env.sb, env.ps
        xnb = [sb(f"xnb{i}", [128, FT * 512], BF16) for i in range(2)]
        qf = sb("qf", [128, 2 * S], BF16)
        kf = sb("kf", [128, 2 * S], BF16)
        vt = sb("vt", [128, NKT * 256], BF16)
        wq_s = sb("wq_s", [128, 2 * D], BF16)
        wk_s = sb("wk_s", [128, 2 * D], BF16)
        wv_s = sb("wv_s", [128, FT * 256], BF16)
        bq_s = sb("bq_s", [128, 2], F32)
        bqs = sb("bqs", [128, 2], F32)
        bk_s = sb("bk_s", [128, 2], F32)
        bv_s = sb("bv_s", [1, 256], F32)
        bvt = sb("bvt", [128, 256], F32)
        sj_s = sb("sj_s", [128, 2 * NKT], F32)
        msh_s = sb("msh_s", [1, 16], F32)
        msh2 = sb("msh2", [1, 16], F32)
        ngt_c = [sb(f"ngt_c{i}", [1, 1024], BF16) for i in range(2)]
        ones1b = sb("ones1b", [1, 128], BF16)
        mctab = sb("mctab", [1, 32], F32)
        mc2 = sb("mc2", [1, 16], F32)
        mcol = sb("mcol", [128, 1], F32)
        biasc = sb("biasc", [128, NKT], F32)
        sacc = [sb(f"sacc{i}", [128, 1024], F32) for i in range(2)]
        ones128f = sb("ones128f", [128, 1], F32)
        dg_s = sb("dg_s", [128, 8 * 1024], BF16)
        px = [sb(f"px{i}", [128, 1024], BF16) for i in range(3)]
        pm = [sb(f"pm{i}", [128, 1024], BF16) for i in range(3)]
        clampb = [sb(f"clampb{i}", [128, 1024], F32) for i in range(2)]
        s2b = sb("s2b", [128, 2 * 512], BF16)
        kmrow = sb("kmrow", [1, 16], F32)
        km2 = sb("km2", [1, 1], F32)
        mrow = [clampb[0][0:1, 0:512], clampb[1][0:1, 0:512]]
        obuf = px

        ones128b = sb("ones128b", [128, 1], BF16)
        ones2b = sb("ones2b", [2, 128], BF16)
        ones1f = sb("ones1f", [1, 128], F32)
        o128th = sb("o128th", [128, 128], BF16)
        ngbF = s2b
        pAB = ps("pAB")
        pCD = ps("pCD")
        pOP = ps("pOP")
        pD = ps("pD")
        pDfull = ps("pDf")
        bA0, bA1 = ps("A0"), ps("A1")
        bC0, bC1 = ps("C0"), ps("C1")
        bO0, bO1 = ps("O0"), ps("O1")

        P = env.P
        ld = env.sem("ld")
        ldxs = [env.sem("ldx0"), env.sem("ldx1")]
        auxd = env.sem("auxd")
        peQ = env.sem("peQ")
        peS = env.sem("peS")
        peP = env.sem("peP")
        peO = env.sem("peO")
        peD = env.sem("peD")
        peB = env.sem("peB")
        stv = env.sem("stv")

        P.op("gpsimd", lambda e: e.memset(ones128b[:, :], 1.0))
        P.op("gpsimd", lambda e: e.memset(o128th[:, :], 1.0 / 128.0))
        P.op("gpsimd", lambda e: e.memset(ones1b[:, :], 1.0))
        P.op("gpsimd", lambda e: e.memset(ones128f[:, :], 1.0))
        g_const = P.op("gpsimd", lambda e: e.memset(ones1f[:, :], 1.0))
        def cdma(dst, src):
            return P.op("gpsimd", lambda e, d=dst, s=src: e.dma_start(out=d, in_=s),
                        inc=ld, inck=16, ordered=False, serial=False)
        cdma(wq_s.ap().rearrange("p (m k) -> p m k", k=D),
             wqp.ap().rearrange("m p k -> p m k"))
        cdma(wk_s.ap().rearrange("p (m k) -> p m k", k=D),
             wkp.ap().rearrange("m p k -> p m k"))
        cdma(wv_s.ap().rearrange("p (t f) -> p t f", f=256),
             wv.ap().rearrange("(t p) f -> p t f", p=128))
        cdma(bq_s[:, :], bq[:, :])
        cdma(bk_s[:, :], bk[:, :])
        cdma(bv_s[:, :], bv[:, :])
        cdma(sj_s.ap().rearrange("p (m k) -> p m k", k=NKT),
             sjtab.ap().rearrange("m p k -> p m k"))
        cdma(msh_s[:, :], mshtab[:, :])
        ld_const = cdma(dg_s.ap().rearrange("p (m k) -> p m k", k=1024),
                        diagmask.ap().rearrange("m p k -> p m k"))
        if ctx is not None and "post_warm" in ctx:
            ctx["post_warm"]([(ld, ld_const), (ldxs[0], 16), (ldxs[1], 16)])
        vs_bqs = P.op("vector", lambda e: e.tensor_scalar_mul(bqs[:, :], bq_s[:, :], SCL_Q),
                      waits=[(ld, ld_const)])
        pe_bv = P.op("tensor", lambda e: e.matmul(bO1[:, 0:256], ones1f[:, :], bv_s[:, :],
                                                  start=True, stop=True),
                     waits=[(ld, ld_const), (P.gs, g_const)], inc=peB)
        vs_bvt = P.op("vector", lambda e: e.tensor_copy(bvt[:, :], bO1[:, 0:256]),
                      waits=[(peB, pe_bv)])

        last_v_evac = vs_bvt
        last_k_evac = None
        ev_bank = {}
        qkv_pe = {}
        for tc in range(FT):
            sl_ = tc % 2
            ldw0 = [(peQ, qkv_pe.get(tc - 2, 0))]
            if ctx is not None and tc < 2:
                ldw0.append(ctx["xn0_join"][tc % 2])
            ldx_v = P.op("sync", lambda e, tc=tc, sl_=sl_: e.dma_start(
                out=xnb[sl_].ap().rearrange("p (t n) -> p t n", n=512),
                in_=xn_src(tc)),
                waits=ldw0, inc=ldxs[sl_], inck=16,
                serial=False)
            qkbanks = {("q", 0): bA0, ("q", 1): bA1, ("k", 0): bC0, ("k", 1): bC1}
            for which, wsb, dst, bias in (("q", wq_s, qf, bqs),
                                          ("k", wk_s, kf, bk_s)):
                for hh in range(2):
                    bank = qkbanks[(which, hh)]
                    ev_prev = ev_bank.get((which, hh))
                    for kt in range(FT):
                        w = []
                        if kt == 0:
                            w.append((ldxs[sl_], ldx_v))
                            if ev_prev is not None:
                                w.append((P.ss, ev_prev))
                            else:
                                w.append((P.vs, vs_bvt))
                        P.op("tensor", lambda e, hh=hh, kt=kt, wsb=wsb, bank=bank,
                             sl_=sl_: e.matmul(
                            bank[:, :], wsb[:, hh * D + kt * 128: hh * D + kt * 128 + 128],
                            xnb[sl_][:, kt * 512:(kt + 1) * 512],
                            start=(kt == 0), stop=(kt == FT - 1)),
                            waits=w, inc=peQ if kt == FT - 1 else None)
                    grp = peQ.n
                    ev_bank[(which, hh)] = P.op(
                        "scalar",
                        lambda e, hh=hh, tc=tc, dst=dst, bias=bias, bank=bank: e.activation(
                            dst[:, hh * S + tc * 512: hh * S + tc * 512 + 512],
                            bank[:, :], AF.Identity, bias=bias[:, hh:hh + 1],
                            scale=(SCL_Q if dst is qf else 1.0)),
                        waits=[(peQ, grp), (ld, ld_const)] + ([(P.vs, vs_bqs)] if dst is qf else []))
            last_k_evac = ev_bank[("k", 1)]
            for stt in range(4):
                for kt in range(FT):
                    w = []
                    if kt == 0:
                        w.append((ldxs[sl_], ldx_v))
                        w.append((P.vs, last_v_evac))
                    P.op("tensor", lambda e, stt=stt, kt=kt, sl_=sl_: e.matmul(
                        bO0[:, 0:256],
                        xnb[sl_][:, kt * 512 + stt * 128: kt * 512 + stt * 128 + 128],
                        wv_s[:, kt * 256:(kt + 1) * 256],
                        start=(kt == 0), stop=(kt == FT - 1)),
                        waits=w, inc=peQ if kt == FT - 1 else None)
                grpv = peQ.n
                tt = tc * 4 + stt
                last_v_evac = P.op("vector", lambda e, tt=tt: e.scalar_tensor_tensor(
                    vt[:, tt * 256:(tt + 1) * 256], bO0[:, 0:256], 1.0, bvt[:, :],
                    ALU.mult, ALU.add), waits=[(peQ, grpv)])
            qkv_pe[tc] = peQ.n
        qkv_done_s = last_k_evac
        qkv_done_v = last_v_evac

        vb_ms = {}
        for hh in range(2):
            for tc in range(FT):
                sqs = P.op("scalar", lambda e, hh=hh, tc=tc: e.activation(
                    s2b[:, (tc % 2) * 512:(tc % 2) * 512 + 512],
                    kf[:, hh * S + tc * 512: hh * S + tc * 512 + 512], AF.Square),
                    waits=[(P.ss, qkv_done_s), (P.vs, qkv_done_v)] if tc == 0 else
                          [(peD, peD.n)])
                dv = P.op("tensor", lambda e, tc=tc: e.matmul(
                    pD[:, :], ones128b[:, :], s2b[:, (tc % 2) * 512:(tc % 2) * 512 + 512],
                    start=True, stop=True), waits=[(P.ss, sqs), (P.vs, P.vs.n)],
                    inc=peD)
                P.op("vector", lambda e, tc=tc: e.reduce_max(
                    kmrow[:, tc:tc + 1], pD[:, :], axis=AX.X), waits=[(peD, dv)])
            vkm = P.op("vector", lambda e: e.reduce_max(km2[:, :], kmrow[:, :], axis=AX.X))
            for tc in range(FT):
                pp = tc % 2
                sqs = P.op("scalar", lambda e, hh=hh, tc=tc, pp=pp: e.activation(
                    s2b[:, pp * 512:pp * 512 + 512],
                    qf[:, hh * S + tc * 512: hh * S + tc * 512 + 512], AF.Square),
                    waits=[(peD, peD.n)])
                dv = P.op("tensor", lambda e, pp=pp: e.matmul(
                    pD[:, :], ones128b[:, :], s2b[:, pp * 512:pp * 512 + 512],
                    start=True, stop=True),
                    waits=[(P.ss, sqs), (P.vs, P.vs.n)], inc=peD)
                msw = [(peD, dv), (P.vs, vkm)]
                if (hh, tc - 2) in vb_ms:
                    msw.append((P.vs, vb_ms[(hh, tc - 2)]))
                ms = P.op("scalar", lambda e, pp=pp: e.activation(
                    mrow[pp][:, :], pD[:, :], AF.Sqrt, scale=km2[:, :]), waits=msw)
                vb_ms[(hh, tc)] = P.op("vector", lambda e, hh=hh, tc=tc, pp=pp: e.reduce_max(
                    mctab[:, hh * 16 + tc: hh * 16 + tc + 1], mrow[pp][:, :], axis=AX.X),
                    waits=[(P.ss, ms)])
        P.op("vector", lambda e: e.tensor_max(
            mc2.ap().rearrange("o (c t) -> o c t", t=1),
            mctab.ap().rearrange("o (c t) -> o c t", t=2)[:, :, 0:1],
            mctab.ap().rearrange("o (c t) -> o c t", t=2)[:, :, 1:2]))
        mhat_done = P.op("vector", lambda e: e.tensor_add(msh2[:, :], msh_s[:, :],
                                                          mc2[:, :]),
                         waits=[(ld, ld_const)])

        QW = 1024
        NQW = S // QW
        prev_norm = P.vs.n
        prev_exp = P.ss.n
        prev_pv = peP.n
        prev_sum = peO.n
        spairs = [pAB, pCD]
        for hh in range(2):
            for ci in range(NQW):
                jmax = 8 * ci + 7
                q0 = ci * QW
                idx = hh * 8 + ci
                exp_ms = {}
                pv_ms = {}
                clamp_ms = {}
                sig_ms = {}
                ngl = P.op("gpsimd", lambda e, hh=hh, q0=q0, cb=idx % 2: e.dma_start(
                    out=ngt_c[cb][:, :], in_=negtab[0:1, q0:q0 + QW]),
                    waits=[(peS, peS.n)], inc=auxd, inck=16, ordered=False)
                ngb = ngt_c[idx % 2]
                if hh == 0:
                    ngf_ms = 0
                    for half in range(2):
                        bF = P.op("tensor", lambda e, half=half, ngb=ngb: e.matmul(
                            pDfull[:, :], ones1b[:, :],
                            ngb[:, half * 512:(half + 1) * 512],
                            start=True, stop=True),
                            waits=[(auxd, ngl), (peO, prev_sum),
                                   (P.vs, prev_norm if idx else mhat_done),
                                   (P.vs, ngf_ms)], inc=peB)
                        ngf_ms = P.op("vector", lambda e, half=half: e.tensor_copy(
                            ngbF[:, half * 512:(half + 1) * 512], pDfull[:, :]),
                            waits=[(peB, bF), (peS, peS.n)])
                bcv = P.op("tensor", lambda e, idx=idx: e.matmul(
                    pAB[:, 0:1], ones1f[:, :], msh2[:, idx:idx + 1],
                    start=True, stop=True),
                    waits=[(P.vs, mhat_done if idx == 0 else prev_norm),
                           (P.ss, prev_exp), (peO, prev_sum)], inc=peB)
                vmc = P.op("vector", lambda e: e.tensor_copy(mcol[:, :], pAB[:, 0:1]),
                           waits=[(peB, bcv)])
                vbias = P.op("vector", lambda e, hh=hh, jmax=jmax: e.tensor_scalar(
                    biasc[:, 0:jmax + 1], sj_s[:, hh * NKT: hh * NKT + jmax + 1],
                    mcol[:, :], None, ALU.subtract), waits=[(P.ss, prev_exp)])

                def emit_pv(j):
                    rhsP, pready = pvq[j]
                    w = [pready]
                    if j == 0:
                        w.append((P.vs, prev_norm))
                    pv_ms[j] = P.op("tensor", lambda e, hh=hh, j=j, rhsP=rhsP,
                                    jmax=jmax: e.matmul(
                        pOP[:, 0:512], vt[:, j * 256 + hh * 128: j * 256 + hh * 128 + 128],
                        rhsP[:, 0:512], start=(j == 0), stop=(j == jmax)), waits=w)
                    pv_ms[j] = P.op("tensor", lambda e, hh=hh, j=j, rhsP=rhsP,
                                    jmax=jmax: e.matmul(
                        pOP[:, 512:1024],
                        vt[:, j * 256 + hh * 128: j * 256 + hh * 128 + 128],
                        rhsP[:, 512:1024], start=(j == 0), stop=(j == jmax)),
                        inc=peP)
                    if j == 0:
                        v0 = P.op("vector", lambda e, rhsP=rhsP: e.tensor_copy(
                            sacc[0][:, 0:512], rhsP[:, 0:512]),
                            waits=[pready, (peO, prev_sum), (peB, peB.n)])
                        g0 = P.op("gpsimd", lambda e, rhsP=rhsP: e.tensor_copy(
                            sacc[0][:, 512:1024], rhsP[:, 512:1024]),
                            waits=[pready, (peO, prev_sum), (peB, peB.n)])
                        sig_ms[j] = (v0, g0)
                    else:
                        v0 = P.op("vector", lambda e, j=j, rhsP=rhsP: e.tensor_add(
                            sacc[j % 2][:, 0:512], sacc[(j + 1) % 2][:, 0:512],
                            rhsP[:, 0:512]), waits=[pready])
                        g0 = P.op("gpsimd", lambda e, j=j, rhsP=rhsP: e.tensor_add(
                            sacc[j % 2][:, 512:1024], sacc[(j + 1) % 2][:, 512:1024],
                            rhsP[:, 512:1024]), waits=[pready])
                        sig_ms[j] = (v0, g0)

                pvq = {}
                for j in range(jmax + 1):
                    pair = spairs[j % 2]
                    t = j - 8 * ci
                    diag = t >= 0
                    w = []
                    if j <= 1:
                        w += [(P.ss, prev_exp), (P.vs, vbias)]
                    else:
                        if (j - 2) in clamp_ms:
                            w.append((P.vs, clamp_ms[j - 2]))
                        else:
                            w.append((P.ss, exp_ms[j - 2]))
                    if hh == 0:
                        w2 = [(P.vs, ngf_ms)] if j == 0 else []
                        P.op("tensor", lambda e, hh=hh, j=j, q0=q0, pair=pair: e.matmul(
                            pair[:, 0:512], kf[:, hh * S + j * 128: hh * S + j * 128 + 128],
                            qf[:, hh * S + q0: hh * S + q0 + 512],
                            start=True, stop=False), waits=w)
                        P.op("tensor", lambda e, hh=hh, j=j, q0=q0, pair=pair: e.matmul(
                            pair[:, 512:1024],
                            kf[:, hh * S + j * 128: hh * S + j * 128 + 128],
                            qf[:, hh * S + q0 + 512: hh * S + q0 + 1024],
                            start=True, stop=False))
                        P.op("tensor", lambda e, pair=pair: e.matmul(
                            pair[:, 0:512], o128th[:, :], ngbF[:, 0:512],
                            start=False, stop=True), waits=w2)
                        sv = P.op("tensor", lambda e, pair=pair: e.matmul(
                            pair[:, 512:1024], o128th[:, :], ngbF[:, 512:1024],
                            start=False, stop=True), inc=peS)
                    else:
                        P.op("tensor", lambda e, hh=hh, j=j, q0=q0, pair=pair: e.matmul(
                            pair[:, 0:512], kf[:, hh * S + j * 128: hh * S + j * 128 + 128],
                            qf[:, hh * S + q0: hh * S + q0 + 512],
                            start=True, stop=True), waits=w)
                        sv = P.op("tensor", lambda e, hh=hh, j=j, q0=q0, pair=pair: e.matmul(
                            pair[:, 512:1024],
                            kf[:, hh * S + j * 128: hh * S + j * 128 + 128],
                            qf[:, hh * S + q0 + 512: hh * S + q0 + 1024],
                            start=True, stop=True), inc=peS)
                    if diag:
                        cw = [(peS, sv), (P.vs, vbias)]
                        if (j - 2) in exp_ms:
                            cw.append((P.ss, exp_ms[j - 2]))
                        else:
                            cw.append((P.ss, prev_exp))
                        if j <= 1:
                            cw.append((P.vs, prev_norm))
                        cm = P.op("vector", lambda e, j=j, pair=pair: e.tensor_scalar(
                            clampb[j % 2][:, :], pair[:, :],
                            biasc[:, j:j + 1], 55.0, ALU.add, ALU.min), waits=cw)
                        clamp_ms[j] = cm
                        src = clampb[j % 2]
                        ew = [(P.vs, cm)]
                    else:
                        src = pair
                        ew = [(peS, sv), (P.vs, vbias)]
                    if (j - 3) in pv_ms:
                        ew.append((peP, pv_ms[j - 3]))
                        ew.append((P.vs, sig_ms[j - 3][0]))
                        ew.append((P.gs, sig_ms[j - 3][1]))
                    elif j <= 2:
                        ew.append((peP, prev_pv))
                        ew.append((P.vs, prev_norm))
                        ew.append((P.gs, P.gs.n))
                        ew.append((stv, stv.n))
                    ex = P.op("scalar", lambda e, j=j, src=src, diag=diag: e.activation(
                        px[j % 3][:, :], src[:, :], AF.Exp,
                        bias=(0.0 if diag else biasc[:, j:j + 1])),
                        waits=ew)
                    exp_ms[j] = ex
                    if diag:
                        mw = [(P.ss, ex)]
                        if (j - 3) in pv_ms:
                            mw.append((peP, pv_ms[j - 3]))
                            mw.append((P.gs, sig_ms[j - 3][1]))
                            mw.append((P.vs, sig_ms[j - 3][0]))
                        elif j <= 2:
                            mw.append((peP, prev_pv))
                            mw.append((P.vs, prev_norm))
                            mw.append((P.gs, P.gs.n))
                        mm = P.op("vector", lambda e, t=t, j=j: e.tensor_mul(
                            pm[j % 3][:, :], px[j % 3][:, :],
                            dg_s[:, t * 1024:(t + 1) * 1024]),
                            waits=mw)
                        pvq[j] = (pm[j % 3], (P.vs, mm))
                    else:
                        pvq[j] = (px[j % 3], (P.ss, ex))
                    if j >= 2:
                        emit_pv(j - 2)
                emit_pv(jmax - 1)
                emit_pv(jmax)
                rin = sacc[0][0:1, :]
                rinb = clampb[0]
                s0 = P.op("tensor", lambda e, jmax=jmax: e.matmul(
                    pD[:, :], ones128f[:, :], sacc[jmax % 2][:, 0:512],
                    start=True, stop=True),
                    waits=[(P.vs, sig_ms[jmax][0]), (P.gs, sig_ms[jmax][1])], inc=peO)
                P.op("vector", lambda e, rin=rin: e.reciprocal(rin[:, 0:512], pD[:, :]),
                     waits=[(peO, s0)])
                prev_sum = P.op("tensor", lambda e, jmax=jmax: e.matmul(
                    pD[:, :], ones128f[:, :], sacc[jmax % 2][:, 512:1024],
                    start=True, stop=True), waits=[(P.vs, P.vs.n)], inc=peO)
                vr = P.op("vector", lambda e, rin=rin: e.reciprocal(
                    rin[:, 512:1024], pD[:, :]), waits=[(peO, prev_sum)])
                bc1 = P.op("tensor", lambda e, rin=rin: e.matmul(
                    pAB[:, 0:512], ones1f[:, :], rin[:, 0:512], start=True, stop=True),
                    waits=[(P.vs, vr), (P.ss, exp_ms[jmax])])
                bc = P.op("tensor", lambda e, rin=rin: e.matmul(
                    pAB[:, 512:1024], ones1f[:, :], rin[:, 512:1024],
                    start=True, stop=True), inc=peB)
                P.op("vector", lambda e, rinb=rinb: e.tensor_copy(rinb[:, :], pAB[:, :]),
                     waits=[(peB, bc)])
                ob = obuf[1]
                nm = P.op("vector", lambda e, ob=ob, rinb=rinb: e.tensor_mul(
                    ob[:, :], pOP[:, :], rinb[:, :]),
                    waits=[(peP, pv_ms[jmax]), (stv, stv.n), (P.ss, exp_ms[jmax])])
                prev_norm = nm
                prev_exp = exp_ms[jmax]
                prev_pv = pv_ms[jmax]
                P.op("sync", lambda e, hh=hh, ci=ci, ob=ob: e.dma_start(
                    out=o0_dst(hh, ci), in_=ob[:, :]),
                    waits=[(P.vs, nm)], inc=stv, inck=16)
        P.op("sync", lambda e: e.nop(), waits=[(stv, stv.n)])
    return env.finish()



def build_l3(ctx=None):
    env = Env(ctx, "p3_", {"pA": "A0", "pB": "A1", "pC": "O0", "pE": "C0",
                           "pD": "pD1", "pD2": "pD2v"})
    nc = env.nc
    xfm = env.dram("xfm", [D, ROWS], F32, "in")
    xn0lh = [env.dram("xn0loc0", [D, 512], BF16, "in"),
             env.dram("xn0loc1", [D, 512], BF16, "in")]
    maskr = env.dram("maskr", [1, ROWS], F32, "in")
    wo0p = env.dram("wo0p", [FT, 128, D], F32, "in")
    bo0c = env.dram("bo0c", [128, FT], F32, "in")
    w0ap = env.dram("w0ap", [32, 128, D], F32, "in")
    w1ap = env.dram("w1ap", [32, 128, D], F32, "in")
    w0bp = env.dram("w0bp", [FT, 128, FF0], F32, "in")
    w1bp = env.dram("w1bp", [FT, 128, FF0], F32, "in")
    b0ac = env.dram("b0ac", [128, 32], F32, "in")
    b1ac = env.dram("b1ac", [128, 32], F32, "in")
    b0bc = env.dram("b0bc", [128, FT], F32, "in")
    b1bc = env.dram("b1bc", [128, FT], F32, "in")
    x1fm = env.dram("x1fm", [D, ROWS], F32, "out")
    xn1ch = [env.dram("xn1c0", [D, 512], BF16, "out"),
             env.dram("xn1c1", [D, 512], BF16, "out")]
    def xn1_dst(tch):
        return xn1ch[tch].ap().rearrange("(t p) n -> p t n", p=128)
    if ctx is None:
        o0loc = env.dram("o0loc", [D, ROWS], BF16, "in")
        def o0_src(t0):
            return o0loc[:, t0:t0 + 512].rearrange("(t p) n -> p t n", p=128)
    else:
        o0_src = ctx["o0_src"]
    with env.scope():
        sb, ps = env.sb, env.ps
        res = sb("res", [128, FT * 512], F32)
        xn = sb("xn", [128, FT * 512], BF16)
        o0 = sb("o0", [128, FT * 512], BF16)
        h0 = sb("h0", [128, 32 * 512], BF16)
        h1 = sb("h1", [128, 32 * 512], BF16)
        wbufA = [sb(f"wbufA{i}", [128, D], BF16) for i in range(2)]
        wbufB0 = sb("wbufB0", [128, FF0], BF16)
        wbufB1 = sb("wbufB1", [128, FF0], BF16)
        wbufO = sb("wbufO", [128, D], BF16)
        b0a_s = sb("b0a_s", [128, 32], F32)
        b1a_s = sb("b1a_s", [128, 32], F32)
        b0b_s = sb("b0b_s", [128, FT], F32)
        b1b_s = sb("b1b_s", [128, FT], F32)
        bo0_s = sb("bo0_s", [128, FT], F32)
        mr_s = sb("mr_s", [1, ROWS], F32)
        mbc = sb("mbc", [128, 512], F32)
        s0 = sb("s0", [128, 512], F32)
        s1 = sb("s1", [128, 512], F32)
        s2 = sb("s2", [128, 512], F32)
        d_ = sb("d_", [128, 512], F32)
        e2 = sb("e2", [128, 512], F32)
        x1b = [sb(f"x1b{i}", [128, 512], F32) for i in range(2)]
        xn1b = xn
        bufs = {
            "ones128f": sb("ones128f", [128, 1], F32),
            "ones1f": sb("ones1f", [1, 128], F32),
            "lnsq": sb("lnsq", [128, 2 * 512], F32),
            "ln_mu": sb("ln_mu", [128, 512], F32),
            "ln_rstd": sb("ln_rstd", [128, 512], F32),
            "ln_r1": sb("ln_r1", [1, 512], F32),
            "ln_r2": sb("ln_r2", [1, 512], F32),
            "ln_r3": sb("ln_r3", [1, 512], F32),
            "ln_r4": sb("ln_r4", [1, 512], F32),
        }
        pA = ps("pA", [128, 512])
        pB = ps("pB", [128, 512])
        pC = ps("pC", [128, 512])
        pE = ps("pE", [128, 512])
        pD = ps("pD", [1, 512])
        pD2 = ps("pD2", [1, 512])
        bufs.update({"ln_pS": pD, "ln_pQ": pD2, "ln_pB1": pE, "ln_pB2": pA})

        P = env.P
        ld = env.sem("ld")
        wsa = [env.sem("wsa0"), env.sem("wsa1")]
        wso = env.sem("wso")
        wsb0 = env.sem("wsb0")
        wsb1 = env.sem("wsb1")
        peH = env.sem("peH")
        peM = env.sem("peM")
        peB3 = env.sem("peB3")
        pe = env.sem("pe")
        stx = env.sem("stx")
        stn = env.sem("stn")

        P.op("gpsimd", lambda e: e.memset(bufs["ones128f"][:, :], 1.0), ordered=False)
        g_const = P.op("gpsimd", lambda e: e.memset(bufs["ones1f"][:, :], 1.0))
        def sdma(dst, src):
            return P.op("sync", lambda e, d=dst, s=src: e.dma_start(out=d, in_=s),
                        inc=ld, inck=16, serial=False)
        sdma(b0a_s[:, :], b0ac[:, :])
        sdma(b1a_s[:, :], b1ac[:, :])
        sdma(b0b_s[:, :], b0bc[:, :])
        sdma(b1b_s[:, :], b1bc[:, :])
        sdma(bo0_s[:, :], bo0c[:, :])
        ld_const = sdma(mr_s[:, :], maskr[:, :])

        last_x1 = 0
        for tch in range(2):
            t0 = tch * 512
            lw = [(peH, peH.n), (peM, peM.n), (P.vs, last_x1)]
            P.op("sync", lambda e, t0=t0: e.dma_start(
                out=res.ap().rearrange("p (t n) -> p t n", n=512),
                in_=xfm[:, t0:t0 + 512].rearrange("(t p) n -> p t n", p=128)),
                waits=lw, inc=ld, inck=16, serial=False)
            ld_rx = P.op("sync", lambda e, tch=tch: e.dma_start(
                out=xn.ap().rearrange("p (t n) -> p t n", n=512),
                in_=xn0lh[tch].ap().rearrange("(t p) n -> p t n", p=128)),
                inc=ld, inck=16, serial=False)
            ld_chunk = P.op("sync", lambda e, t0=t0: e.dma_start(
                out=o0.ap().rearrange("p (t n) -> p t n", n=512),
                in_=o0_src(t0)),
                waits=([ctx["o0_join"]] if ctx is not None else []),
                inc=ld, inck=16, serial=False)
            hgrp = {}
            peH_start = peH.n
            for ex, (wpk, hbuf, bcol, actf) in enumerate(
                    ((w0ap, h0, b0a_s, AF.Gelu), (w1ap, h1, b1a_s, AF.Relu))):
                for mc in range(32):
                    gi = ex * 32 + mc
                    wl = P.op("gpsimd", lambda e, wpk=wpk, mc=mc, gi=gi: e.dma_start(
                        out=wbufA[gi % 2][:, :], in_=wpk[mc]),
                        waits=[(peH, hgrp.get(gi - 2, peH_start))],
                        inc=wsa[gi % 2], inck=16, ordered=False, serial=False)
                    bank = pB if gi % 2 else pA
                    for kt in range(FT):
                        w = [(wsa[gi % 2], wl), (ld, ld_rx)] if kt == 0 else []
                        if kt == 0:
                            w.append((P.ss, ev_h[gi - 2]) if gi >= 2
                                     else (P.ss, P.ss.n))
                        P.op("tensor", lambda e, gi=gi, kt=kt, bank=bank: e.matmul(
                            bank[:, :], wbufA[gi % 2][:, kt * 128:(kt + 1) * 128],
                            xn[:, kt * 512:(kt + 1) * 512],
                            start=(kt == 0), stop=(kt == FT - 1)),
                            waits=w, inc=peH if kt == FT - 1 else None)
                    hgrp[gi] = peH.n
                    if gi == 0:
                        ev_h = {}
                    ev_h[gi] = P.op("scalar", lambda e, mc=mc, hbuf=hbuf, bcol=bcol,
                                    actf=actf, bank=bank: e.activation(
                        hbuf[:, mc * 512:(mc + 1) * 512], bank[:, :], actf,
                        bias=bcol[:, mc:mc + 1]),
                        waits=[(peH, hgrp[gi]), (ld, ld_chunk)])
            ha_done = ev_h[63]
            bcv = P.op("tensor", lambda e, t0=t0: e.matmul(
                pE[:, :], bufs["ones1f"][:, :], mr_s[:, t0:t0 + 512],
                start=True, stop=True),
                waits=[(ld, ld_chunk), (P.gs, g_const), (P.vs, last_x1)], inc=peB3)
            mb = P.op("vector", lambda e: e.tensor_copy(mbc[:, :], pE[:, :]),
                      waits=[(peB3, bcv)])
            for mc in range(FT):
                wlo = P.op("gpsimd", lambda e, mc=mc: e.dma_start(
                    out=wbufO[:, :], in_=wo0p[mc]), waits=[(peM, peM.n)],
                    inc=wso, inck=16, ordered=False, serial=False)
                wl0 = P.op("gpsimd", lambda e, mc=mc: e.dma_start(
                    out=wbufB0[:, :], in_=w0bp[mc]), waits=[(peM, peM.n)],
                    inc=wsb0, inck=16, ordered=False, serial=False)
                wl1 = P.op("gpsimd", lambda e, mc=mc: e.dma_start(
                    out=wbufB1[:, :], in_=w1bp[mc]), waits=[(peM, peM.n)],
                    inc=wsb1, inck=16, ordered=False, serial=False)
                for kt in range(FT):
                    w = [(wso, wlo), (ld, ld_chunk)] if kt == 0 else []
                    if kt == 0:
                        w.append((P.ss, ev_bl.get("s2", ha_done) if mc else ha_done))
                    P.op("tensor", lambda e, kt=kt: e.matmul(
                        pC[:, :], wbufO[:, kt * 128:(kt + 1) * 128],
                        o0[:, kt * 512:(kt + 1) * 512],
                        start=(kt == 0), stop=(kt == FT - 1)),
                        waits=w, inc=peM if kt == FT - 1 else None)
                g_ao = peM.n
                for bank, wb, wlx, hbuf, evk, wsx in ((pA, wbufB0, wl0, h0, "s0", wsb0),
                                                       (pB, wbufB1, wl1, h1, "s1", wsb1)):
                    for ft in range(32):
                        w = [(wsx, wlx)] if ft == 0 else []
                        if ft == 0:
                            w.append((P.ss, ev_bl.get(evk, ha_done) if mc else ha_done))
                        P.op("tensor", lambda e, ft=ft, bank=bank, wb=wb, hbuf=hbuf: e.matmul(
                            bank[:, :], wb[:, ft * 128:(ft + 1) * 128],
                            hbuf[:, ft * 512:(ft + 1) * 512],
                            start=(ft == 0), stop=(ft == 31)),
                            waits=w, inc=peM if ft == 31 else None)
                g_out1 = peM.n
                if mc == 0:
                    ev_bl = {}
                ev_bl["s2"] = P.op("scalar", lambda e, mc=mc: e.activation(
                    s2[:, :], pC[:, :], AF.Identity, bias=bo0_s[:, mc:mc + 1]),
                    waits=[(peM, g_ao), (P.vs, last_x1)])
                ev_bl["s0"] = P.op("scalar", lambda e, mc=mc: e.activation(
                    s0[:, :], pA[:, :], AF.Identity, bias=b0b_s[:, mc:mc + 1]),
                    waits=[(peM, g_out1)])
                ev_bl["s1"] = P.op("scalar", lambda e, mc=mc: e.activation(
                    s1[:, :], pB[:, :], AF.Identity, bias=b1b_s[:, mc:mc + 1]))
                xb = x1b[mc % 2]
                P.op("vector", lambda e: e.tensor_sub(d_[:, :], s0[:, :], s1[:, :]),
                     waits=[(P.ss, ev_bl["s1"]), (P.vs, mb)])
                P.op("vector", lambda e: e.tensor_mul(e2[:, :], d_[:, :], mbc[:, :]))
                P.op("vector", lambda e: e.tensor_add(d_[:, :], s1[:, :], e2[:, :]))
                P.op("vector", lambda e: e.tensor_add(e2[:, :], d_[:, :], s2[:, :]))
                nx = P.op("vector", lambda e, mc=mc, xb=xb: e.tensor_add(
                    xb[:, :], e2[:, :], res[:, mc * 512:(mc + 1) * 512]),
                    waits=[(stx, stx.n)])
                last_x1 = nx
                P.op("sync", lambda e, mc=mc, t0=t0, xb=xb: e.dma_start(
                    out=x1fm[mc * 128:(mc + 1) * 128, t0:t0 + 512], in_=xb[:, :]),
                    waits=[(P.vs, nx)], inc=stx, inck=16)
        xn1_st = []
        xrl1 = h0.ap().bitcast(F32)
        st_mark = 0
        for tch in range(2):
            t0 = tch * 512
            src_buf = res.ap() if tch == 0 else xrl1
            lw = [(stx, stx.n)] + ([(pe, pe.n)] if tch else [])
            ldr = P.op("sync", lambda e, t0=t0, src_buf=src_buf: e.dma_start(
                out=src_buf.rearrange("p (t n) -> p t n", n=512),
                in_=x1fm[:, t0:t0 + 512].rearrange("(t p) n -> p t n", p=128)),
                waits=lw, inc=ld, inck=16, serial=False)
            vsem, done = emit_ln_fm(
                P, bufs, pe, src_buf, xn1b, 512, after=[(ld, ldr)],
                norm_after=([(stn, st_mark)] if tch else []))
            P.op("sync", lambda e, tch=tch: e.dma_start(
                out=xn1_dst(tch),
                in_=xn1b.ap().rearrange("p (t n) -> p t n", n=512)),
                waits=[(vsem, done)], inc=stn, inck=16)
            st_mark = stn.n
            xn1_st.append((stn, stn.n))
        if ctx is not None:
            ctx["xn1_st"] = xn1_st
        P.op("sync", lambda e: e.nop(), waits=[(stn, stn.n), (stx, stx.n)])
    return env.finish()



def build_l4(ctx=None):
    env = Env(ctx, "p4_", {"pA0": "A0", "pA1": "A1", "pC0": "C0", "pC1": "C1",
                           "pO0": "O0", "pO1": "O1", "pSc": "pDf", "ptr": "ptr"})
    nc = env.nc
    wq1p = env.dram("wq1p", [2, 128, D], F32, "in")
    wk1p = env.dram("wk1p", [2, 128, D], F32, "in")
    wv1 = env.dram("wv1", [D, 256], F32, "in")
    ldc = env.dram("ldc", [1, 2], F32, "in")
    wg1p = env.dram("wg1p", [1, 128, D], F32, "in")
    wg2c = env.dram("wg2c", [128, 256], BF16, "in")
    identb = env.dram("identb", [128, 128], BF16, "in")
    if ctx is None:
        xn1full = env.dram("xn1full", [D, S], BF16, "in")
        o1c = env.dram("o1c", [S, 256], BF16, "out")
        def xn_src(tc):
            return xn1full[:, tc * 512:(tc + 1) * 512].rearrange(
                "(t p) n -> p t n", p=128)
        def o1_dst(t):
            return o1c[t * 128:(t + 1) * 128, :]
    else:
        xn_src = ctx["xn1_src"]
        o1_dst = ctx["o1_dst"]
    with env.scope():
        sb, ps = env.sb, env.ps
        xnb = [sb(f"xnb{i}", [128, FT * 512], BF16) for i in range(2)]
        q1f = sb("q1f", [128, 2 * S], BF16)
        k1f = sb("k1f", [128, 2 * S], BF16)
        v1t = sb("v1t", [128, NKT * 256], BF16)
        sgr = sb("sgr", [128, 8 * 256], F32)
        g1tmp = sb("g1tmp", [128, 2 * 512], BF16)
        sigt = [sb(f"sigt{i}", [128, 512], F32) for i in range(2)]
        pre4 = ctx.get("pre4") if ctx is not None else None
        wq_s = sb("wq_s", [128, 2 * D], BF16)
        wk_s = sb("wk_s", [128, 2 * D], BF16)
        wv_s = sb("wv_s", [128, FT * 256], BF16)
        if pre4 is not None:
            wg1_s, wg2_s, ld_s, id_s = pre4[0]
        else:
            wg1_s = sb("wg1_s", [128, D], BF16)
            wg2_s = sb("wg2_s", [128, 256], BF16)
            ld_s = sb("ld_s", [1, 2], F32)
            id_s = sb("id_s", [128, 128], BF16)
        ldcol = sb("ldcol", [128, 2], F32)
        qdcol = sb("qdcol", [128, 2], F32)
        kdcol = sb("kdcol", [128, 2], F32)
        cdcol = sb("cdcol", [128, 2], F32)
        iot = sb("iot", [128, 128], F32)
        iotm = sb("iotm", [128, 128], F32)
        dts = [sb(f"dts{i}", [128, 128], BF16) for i in range(2)]
        S_sb = [[sb(f"Ssb{h}_{i}", [128, 128], F32) for i in range(2)] for h in range(2)]
        S_bf = [sb(f"Sbf{h}", [128, 128], BF16) for h in range(2)]
        ktb = [sb(f"ktb{h}", [128, 128], BF16) for h in range(2)]
        scb = [sb(f"scb{h}", [128, 128], BF16) for h in range(2)]
        ob = [sb(f"ob{h}", [128, 128], F32) for h in range(2)]
        tib = sb("tib", [128, 128], F32)
        ogr = sb("ogr", [128, 8 * 128], F32)
        o2b = sb("o2b", [128, 128], BF16)
        msbr = sb("msbr", [128, 8], F32)
        msb2r = sb("msb2r", [128, 8], F32)
        lnb8 = sb("lnb8", [128, 8], F32)
        rstd8 = sb("rstd8", [128, 8], F32)
        onb = sb("onb", [128, 4 * 256], BF16)
        ones1f = sb("ones1f", [1, 128], F32)
        pA = [ps("pA0", [128, 512]), ps("pA1", [128, 512])]
        pC = [ps("pC0", [128, 512]), ps("pC1", [128, 512])]
        pO0 = ps("pO0", [128, 512])
        pO1 = ps("pO1", [128, 512])
        pSc = ps("pSc", [128, 512])
        ptr = ps("ptr", [128, 1024], BF16)

        P = env.P
        ldxs = [env.sem("ldx0"), env.sem("ldx1")]
        peQ = env.sem("peQ")
        peL = env.sem("peL")
        peB4 = env.sem("peB4")
        stvs = [env.sem(f"stv{i}") for i in range(4)]

        g_const = P.op("gpsimd", lambda e: e.memset(ones1f[:, :], 1.0))
        ld = env.sem("ld")
        def cdma(dst, src):
            return P.op("gpsimd", lambda e, d=dst, s=src: e.dma_start(
                out=d, in_=s), inc=ld, inck=16, ordered=False, serial=False)
        cdma(wq_s.ap().rearrange("p (m k) -> p m k", k=D),
             wq1p.ap().rearrange("m p k -> p m k"))
        cdma(wk_s.ap().rearrange("p (m k) -> p m k", k=D),
             wk1p.ap().rearrange("m p k -> p m k"))
        cdma(wv_s.ap().rearrange("p (t f) -> p t f", f=256),
             wv1.ap().rearrange("(t p) f -> p t f", p=128))
        if pre4 is not None:
            ldc_sem, ld_const = pre4[1], pre4[2]
        else:
            cdma(wg1_s[:, :], wg1p[0])
            cdma(wg2_s[:, :], wg2c[:, :])
            cdma(ld_s[:, :], ldc[:, :])
            ld_const = cdma(id_s[:, :], identb[:, :])
            ldc_sem = ld
        ld_big = ld.n

        g_iot = P.op("gpsimd", lambda e: e.iota(
            iot[:, :], pattern=[[1, 128]], base=0, channel_multiplier=-1,
            allow_small_or_imprecise_dtypes=True))
        dts_ms = {}
        s0init = {}
        for hh in range(2):
            bc = P.op("tensor", lambda e, hh=hh: e.matmul(
                pA[0][:, 0:1], ones1f[:, :], ld_s[:, hh:hh + 1], start=True, stop=True),
                waits=[(ldc_sem, ld_const), (P.gs, g_const)] +
                      ([(P.vs, ldc_ms)] if hh else []), inc=peB4)
            ldc_ms = P.op("vector", lambda e, hh=hh: e.tensor_copy(
                ldcol[:, hh:hh + 1], pA[0][:, 0:1]), waits=[(peB4, bc)])
            tm1 = P.op("vector", lambda e, hh=hh: e.tensor_scalar_mul(
                iotm[:, :], iot[:, :], ldcol[:, hh:hh + 1]),
                waits=[(P.gs, g_iot)] +
                      ([(P.gs, g_asel), (P.ss, sq3)] if hh else []))
            g_asel = P.op("gpsimd", lambda e: e.affine_select(
                iotm[:, :], iotm[:, :], pattern=[[1, 128]], compare_op=ALU.is_ge,
                fill=-100.0, base=0, channel_multiplier=-1), waits=[(P.vs, tm1)])
            dts_ms[hh] = P.op("scalar", lambda e, hh=hh: e.activation(
                dts[hh][:, :], iotm[:, :], AF.Exp), waits=[(P.gs, g_asel)])
            qi = P.op("gpsimd", lambda e, hh=hh: e.iota(
                iotm[:, 0:1], pattern=[[1, 1]], base=1, channel_multiplier=1,
                allow_small_or_imprecise_dtypes=True), waits=[(P.ss, dts_ms[hh])])
            t2 = P.op("vector", lambda e, hh=hh: e.tensor_mul(
                iotm[:, 1:2], iotm[:, 0:1], ldcol[:, hh:hh + 1]), waits=[(P.gs, qi)])
            sq1 = P.op("scalar", lambda e, hh=hh: e.activation(
                qdcol[:, hh:hh + 1], iotm[:, 1:2], AF.Exp), waits=[(P.vs, t2)])
            ki = P.op("gpsimd", lambda e, hh=hh: e.iota(
                iotm[:, 2:3], pattern=[[1, 1]], base=127, channel_multiplier=-1,
                allow_small_or_imprecise_dtypes=True))
            t3 = P.op("vector", lambda e, hh=hh: e.tensor_mul(
                iotm[:, 3:4], iotm[:, 2:3], ldcol[:, hh:hh + 1]), waits=[(P.gs, ki)])
            sq2 = P.op("scalar", lambda e, hh=hh: e.activation(
                kdcol[:, hh:hh + 1], iotm[:, 3:4], AF.Exp), waits=[(P.vs, t3)])
            t4 = P.op("vector", lambda e, hh=hh: e.tensor_scalar_mul(
                iotm[:, 4:5], ldcol[:, hh:hh + 1], 128.0))
            sq3 = P.op("scalar", lambda e, hh=hh: e.activation(
                cdcol[:, hh:hh + 1], iotm[:, 4:5], AF.Exp), waits=[(P.vs, t4)])
            s0init[hh] = P.op("gpsimd", lambda e, hh=hh: e.memset(S_sb[hh][0][:, :], 0.0))
        dec_done_s = P.ss.n
        dec_done_v = P.vs.n

        bankw = {"A0": [(P.vs, ldc_ms)]}
        sig_war = {}
        sgr_war = {}
        g1_war = {}
        sig_slot = [0]
        PB = {"A0": pA[0], "A1": pA[1], "C0": pC[0], "C1": pC[1],
              "O0": pO0, "O1": pO1, "Sc": pSc}

        def silu_evac(bkey, grp, dst, ncols):
            sl2 = sig_slot[0] % 2
            sig_slot[0] += 1
            s_ms = P.op("scalar", lambda e, bkey=bkey, sl2=sl2, ncols=ncols:
                        e.activation(sigt[sl2][:, 0:ncols], PB[bkey][:, 0:ncols],
                                     AF.Sigmoid),
                        waits=[(peQ, grp)] + sig_war.pop(sl2, []))
            m_ms = P.op("vector", lambda e, bkey=bkey, sl2=sl2, dst=dst, ncols=ncols:
                        e.tensor_mul(dst, sigt[sl2][:, 0:ncols],
                                     PB[bkey][:, 0:ncols]),
                        waits=[(P.ss, s_ms)])
            sig_war[sl2] = [(P.vs, m_ms)]
            bankw[bkey] = [(P.ss, s_ms), (P.vs, m_ms)]
            return m_ms

        ev_q = {}
        ev_k = {}
        ev_v = {}
        sg_ms = {}
        sup_ms = {}
        cast_ms = {}
        ointer_ms = {}
        ointra_ms = {}
        kv_ms = {}
        scm_ms = {}
        om_ms = {}
        og_ms = {}
        sqa_ms = {}
        ktb_ms = {}
        trp_ms = {}
        st_ms = {}
        qkv_pe = {}
        msn_prev = [None]
        lv_prev = None
        on_last_prev = None
        sqa_q = []

        def emit_trp(u):
            t, hh = u // 2, u % 2
            tcu = t // 4
            trp_ms[u] = P.op("tensor", lambda e, t=t, hh=hh: e.transpose(
                ptr[:, 0:128], k1f[:, hh * S + t * 128: hh * S + t * 128 + 128],
                id_s[:, :]),
                waits=[(P.vs, ev_k[(tcu, hh)]), (ldc_sem, ld_const)] +
                      bankw.pop("ptr", []), inc=peL)

        def emit_ktb(u):
            t, hh = u // 2, u % 2
            w = [(peL, trp_ms[u])]
            if (t - 1, hh) in kv_ms:
                w.append((peL, kv_ms[(t - 1, hh)]))
            if t == 0:
                w.append((P.ss, dec_done_s))
            ktb_ms[u] = P.op("vector", lambda e, hh=hh: e.tensor_scalar_mul(
                ktb[hh][:, :], ptr[:, 0:128], kdcol[:, hh:hh + 1]), waits=w)
            bankw["ptr"] = [(P.vs, ktb_ms[u])]

        def emit_sqa(u):
            w = [(P.gs, og_ms[u])]
            if msn_prev[0] is not None:
                w.append((P.gs, msn_prev[0]))
            sqa_ms[u] = P.op("scalar", lambda e, u=u: e.activation(
                o2b[:, :], ogr[:, (u % 8) * 128:(u % 8) * 128 + 128], AF.Square,
                accum_out=msbr[:, u % 8:u % 8 + 1]), waits=w)

        for tc in range(FT):
            sl_ = tc % 2
            ldw = [(peQ, qkv_pe.get(tc - 2, 0))]
            if ctx is not None and tc < 2:
                ldw.append(ctx["xn1_join"][tc % 2])
            ldx_v = P.op("sync", lambda e, tc=tc, sl_=sl_: e.dma_start(
                out=xnb[sl_].ap().rearrange("p (t n) -> p t n", n=512),
                in_=xn_src(tc)),
                waits=ldw, inc=ldxs[sl_], inck=16,
                serial=False)
            first = ([(P.ss, dec_done_s), (P.vs, dec_done_v), (ld, ld_big)]
                     if tc == 0 else [])

            def qk_group(bkey, wsb, hh):
                for kt in range(FT):
                    w = []
                    if kt == 0:
                        w = [(ldxs[sl_], ldx_v)] + bankw.pop(bkey, []) + first
                    P.op("tensor", lambda e, hh=hh, kt=kt, wsb=wsb, bkey=bkey,
                         sl_=sl_: e.matmul(
                        PB[bkey][:, :],
                        wsb[:, hh * D + kt * 128: hh * D + kt * 128 + 128],
                        xnb[sl_][:, kt * 512:(kt + 1) * 512],
                        start=(kt == 0), stop=(kt == FT - 1)),
                        waits=w, inc=peQ if kt == FT - 1 else None)
                return peQ.n

            def v_group(stt):
                for kt in range(FT):
                    w = []
                    if kt == 0:
                        w = [(ldxs[sl_], ldx_v)] + bankw.pop("O0", []) + first
                    P.op("tensor", lambda e, stt=stt, kt=kt, sl_=sl_: e.matmul(
                        pO0[:, 0:256],
                        xnb[sl_][:, kt * 512 + stt * 128: kt * 512 + stt * 128 + 128],
                        wv_s[:, kt * 256:(kt + 1) * 256],
                        start=(kt == 0), stop=(kt == FT - 1)),
                        waits=w, inc=peQ if kt == FT - 1 else None)
                return peQ.n

            g = qk_group("A0", wq_s, 0)
            ev_q[(tc, 0)] = silu_evac(
                "A0", g, q1f[:, 0 * S + tc * 512: 0 * S + tc * 512 + 512], 512)
            g = v_group(0)
            ev_v[tc * 4 + 0] = silu_evac(
                "O0", g, v1t[:, (tc * 4 + 0) * 256:(tc * 4 + 0) * 256 + 256], 256)
            g = qk_group("A1", wq_s, 1)
            ev_q[(tc, 1)] = silu_evac(
                "A1", g, q1f[:, 1 * S + tc * 512: 1 * S + tc * 512 + 512], 512)
            g = v_group(1)
            ev_v[tc * 4 + 1] = silu_evac(
                "O0", g, v1t[:, (tc * 4 + 1) * 256:(tc * 4 + 1) * 256 + 256], 256)
            g = qk_group("C0", wk_s, 0)
            ev_k[(tc, 0)] = silu_evac(
                "C0", g, k1f[:, 0 * S + tc * 512: 0 * S + tc * 512 + 512], 512)
            g = v_group(2)
            ev_v[tc * 4 + 2] = silu_evac(
                "O0", g, v1t[:, (tc * 4 + 2) * 256:(tc * 4 + 2) * 256 + 256], 256)
            g = qk_group("C1", wk_s, 1)
            ev_k[(tc, 1)] = silu_evac(
                "C1", g, k1f[:, 1 * S + tc * 512: 1 * S + tc * 512 + 512], 512)
            g = v_group(3)
            ev_v[tc * 4 + 3] = silu_evac(
                "O0", g, v1t[:, (tc * 4 + 3) * 256:(tc * 4 + 3) * 256 + 256], 256)
            for kt in range(FT):
                w = []
                if kt == 0:
                    w = [(ldxs[sl_], ldx_v)] + bankw.pop("C0", [])
                P.op("tensor", lambda e, kt=kt, sl_=sl_: e.matmul(
                    pC[0][:, :], wg1_s[:, kt * 128:(kt + 1) * 128],
                    xnb[sl_][:, kt * 512:(kt + 1) * 512],
                    start=(kt == 0), stop=(kt == FT - 1)),
                    waits=w, inc=peQ if kt == FT - 1 else None)
            g_g1 = peQ.n
            g1c_ms = P.op("scalar", lambda e, sl_=sl_: e.activation(
                g1tmp[:, sl_ * 512: sl_ * 512 + 512], pC[0][:, :], AF.Copy),
                waits=[(peQ, g_g1)] + g1_war.pop(sl_, []))
            bankw["C0"] = [(P.ss, g1c_ms)]
            qkv_pe[tc] = peQ.n

            emit_trp(8 * tc)
            emit_ktb(8 * tc)
            for t in range(tc * 4, tc * 4 + 4):
                st = t % 4
                P.op("tensor", lambda e, sl_=sl_, st=st: e.matmul(
                    pC[1][:, 0:256], g1tmp[:, sl_ * 512 + st * 128: sl_ * 512 + st * 128 + 128],
                    wg2_s[:, :], start=True, stop=True),
                    waits=[(P.ss, g1c_ms)] + bankw.pop("C1", []), inc=peQ)
                g_g2 = peQ.n
                g1_war.setdefault(sl_, []).append((peQ, g_g2))
                sg_ms[t] = P.op("scalar", lambda e, t=t: e.activation(
                    sgr[:, (t % 8) * 256:(t % 8) * 256 + 256], pC[1][:, 0:256],
                    AF.Sigmoid), waits=[(peQ, g_g2)] + sgr_war.pop(t % 8, []))
                bankw["C1"] = [(P.ss, sg_ms[t])]
                for hh in range(2):
                    cw = ([(P.vs, sup_ms[(t - 1, hh)])] if t else
                          [(P.gs, s0init[hh])])
                    if (t - 1, hh) in ointer_ms:
                        cw.append((peL, ointer_ms[(t - 1, hh)]))
                    cast_ms[(t, hh)] = P.op("scalar", lambda e, hh=hh, t=t:
                                            e.activation(
                        S_bf[hh][:, :], S_sb[hh][t % 2][:, :], AF.Copy), waits=cw)
                while sqa_q and sqa_q[0] < 2 * t - 1:
                    emit_sqa(sqa_q.pop(0))
                for hh in range(2):
                    u = 2 * t + hh
                    w = [(P.ss, cast_ms[(t, hh)]), (P.vs, ev_q[(tc, hh)])] + \
                        bankw.pop("A%d" % hh, [])
                    ointer_ms[(t, hh)] = P.op("tensor", lambda e, hh=hh, t=t:
                                              e.matmul(
                        pA[hh][:, 0:128], q1f[:, hh * S + t * 128: hh * S + t * 128 + 128],
                        S_bf[hh][:, :], start=True, stop=True), waits=w, inc=peL)
                    w = [(P.vs, ktb_ms[u]), (P.vs, ev_v[t])] + \
                        bankw.pop("O1", [])
                    kv_ms[(t, hh)] = P.op("tensor", lambda e, hh=hh, t=t: e.matmul(
                        pO1[:, 0:128], ktb[hh][:, :],
                        v1t[:, t * 256 + hh * 128: t * 256 + hh * 128 + 128],
                        start=True, stop=True), waits=w, inc=peL)
                    w = [(P.vs, ev_k[(tc, hh)]), (P.vs, ev_q[(tc, hh)])] + \
                        bankw.pop("Sc", [])
                    sc_v = P.op("tensor", lambda e, hh=hh, t=t: e.matmul(
                        pSc[:, 0:128], k1f[:, hh * S + t * 128: hh * S + t * 128 + 128],
                        q1f[:, hh * S + t * 128: hh * S + t * 128 + 128],
                        start=True, stop=True), waits=w, inc=peL)
                    w = [(peL, sc_v), (P.ss, dts_ms[hh])]
                    if (t - 1, hh) in ointra_ms:
                        w.append((peL, ointra_ms[(t - 1, hh)]))
                    scm_ms[u] = P.op("vector", lambda e, hh=hh: e.tensor_mul(
                        scb[hh][:, :], pSc[:, 0:128], dts[hh][:, :]), waits=w)
                    bankw["Sc"] = [(P.vs, scm_ms[u])]
                    w = [(P.vs, scm_ms[u]), (P.vs, ev_v[t])] + bankw.pop("Sc", [])
                    ointra_ms[(t, hh)] = P.op("tensor", lambda e, hh=hh, t=t:
                                              e.matmul(
                        pSc[:, 128:256], scb[hh][:, :],
                        v1t[:, t * 256 + hh * 128: t * 256 + hh * 128 + 128],
                        start=True, stop=True), waits=w, inc=peL)
                    if u + 1 < 8 * (tc + 1):
                        emit_trp(u + 1)
                        emit_ktb(u + 1)
                    w = [(peL, ointer_ms[(t, hh)])]
                    if t == 0:
                        w.append((P.ss, dec_done_s))
                    ti_ms = P.op("vector", lambda e, hh=hh: e.tensor_scalar_mul(
                        tib[:, :], pA[hh][:, 0:128], qdcol[:, hh:hh + 1]),
                        waits=w)
                    bankw["A%d" % hh] = [(P.vs, ti_ms)]
                    w = [(peL, ointra_ms[(t, hh)])]
                    if (t - 1, hh) in om_ms:
                        w.append((P.gs, og_ms[2 * (t - 1) + hh]))
                    om_ms[(t, hh)] = P.op("vector", lambda e, hh=hh:
                                          e.tensor_add(
                        ob[hh][:, :], tib[:, :], pSc[:, 128:256]), waits=w)
                    bankw["Sc"] = [(P.vs, om_ms[(t, hh)])]
                    w = [(peL, kv_ms[(t, hh)]), (P.ss, cast_ms[(t, hh)])]
                    if t == 0:
                        w.append((P.ss, dec_done_s))
                    sup_ms[(t, hh)] = P.op("vector", lambda e, hh=hh, t=t:
                                           e.scalar_tensor_tensor(
                        S_sb[hh][(t + 1) % 2][:, :], S_sb[hh][t % 2][:, :],
                        cdcol[:, hh:hh + 1], pO1[:, 0:128], ALU.mult, ALU.add),
                        waits=w)
                    bankw["O1"] = [(P.vs, sup_ms[(t, hh)])]
                    w = [(P.vs, om_ms[(t, hh)]), (P.ss, sg_ms[t])]
                    if u - 8 in sqa_ms:
                        w.append((P.ss, sqa_ms[u - 8]))
                    og_ms[u] = P.op("gpsimd", lambda e, hh=hh, t=t, u=u:
                                    e.tensor_mul(
                        ogr[:, (u % 8) * 128:(u % 8) * 128 + 128], ob[hh][:, :],
                        sgr[:, (t % 8) * 256 + hh * 128:(t % 8) * 256 + hh * 128 + 128]),
                        waits=w)
                    sqa_q.append(u)
                sgr_war[t % 8] = [(P.gs, og_ms[2 * t + 1])]

            while sqa_q:
                emit_sqa(sqa_q.pop(0))
            w = [(P.ss, sqa_ms[8 * tc + 7])]
            if lv_prev is not None:
                w.append((P.ss, lv_prev))
            msn_prev[0] = P.op("gpsimd", lambda e: e.tensor_scalar(
                msb2r[:, :], msbr[:, :], 1.0 / HD, RMS_EPS, ALU.mult, ALU.add),
                waits=w)
            lv_prev = P.op("scalar", lambda e: e.activation(
                lnb8[:, :], msb2r[:, :], AF.Ln), waits=[(P.gs, msn_prev[0])])
            w = []
            if on_last_prev is not None:
                w.append((P.gs, on_last_prev))
            rst_ms = P.op("scalar", lambda e: e.activation(
                rstd8[:, :], lnb8[:, :], AF.Exp, scale=-0.5), waits=w)
            for t in range(tc * 4, tc * 4 + 4):
                for hh in range(2):
                    u = 2 * t + hh
                    w = [(P.ss, rst_ms)]
                    if t - 4 in st_ms:
                        w.append((stvs[t % 4], st_ms[t - 4]))
                    on_last_prev = P.op("gpsimd", lambda e, t=t, hh=hh, u=u:
                                        e.tensor_scalar_mul(
                        onb[:, (t % 4) * 256 + hh * 128:(t % 4) * 256 + hh * 128 + 128],
                        ogr[:, (u % 8) * 128:(u % 8) * 128 + 128],
                        rstd8[:, u % 8:u % 8 + 1]), waits=w)
                P.op("gpsimd", lambda e, t=t: e.dma_start(
                    out=o1_dst(t),
                    in_=onb[:, (t % 4) * 256:(t % 4) * 256 + 256]),
                    waits=[(P.gs, on_last_prev)],
                    inc=stvs[t % 4], inck=16, ordered=False, serial=False)
                st_ms[t] = stvs[t % 4].n
        P.op("sync", lambda e: e.nop(),
             waits=[(sv, sv.n) for sv in stvs if sv.n > 0])
    return env.finish()



QN = FF1 * D
QSH = QN // NCORES // 128


def build_l5(ctx=None):
    env = Env(ctx, "p5_", {"pD": "O0", "pE": "A1"})
    nc = env.nc
    wins = [env.dram(f"w{i}s", [128, QSH], F32, "in") for i in (1, 2, 3)]
    wouts = [env.dram(f"w{i}q", [128, QSH], BF16, "out") for i in (1, 2, 3)]
    arin = env.dram("arin", [1, 4], F32, "int")
    arout = env.dram("arout", [1, 4], F32, "int")
    with env.scope():
        sb, ps = env.sb, env.ps
        pre5 = ctx.get("pre5") if ctx is not None else None
        if pre5 is not None:
            wbuf = [pre5[0][0], pre5[0][1], sb("wbuf2", [128, QSH], F32)]
        else:
            wbuf = [sb(f"wbuf{i}", [128, QSH], F32) for i in range(3)]
        qtmp = sb("qtmp", [128, QSH], F32)
        absb = qtmp.ap().rearrange("p n -> p n") and qtmp
        qout = sb("qout", [128, QSH], BF16)
        asum = sb("asum", [128, 1], F32)
        gsum = sb("gsum", [1, 4], F32)
        gtot = sb("gtot", [1, 4], F32)
        rg = sb("rg", [1, 4], F32)
        rgc = sb("rgc", [128, 4], F32)
        ones128f = sb("ones128f", [128, 1], F32)
        ones1f = sb("ones1f", [1, 128], F32)
        pD = ps("pD", [128, 512])[0:1, :]
        pE = ps("pE", [128, 512])

        P = env.P
        ld = env.sem("ld")
        pe = env.sem("pe")
        ccs = env.sem("ccs")
        stv = env.sem("stv")
        ard = env.sem("ard")

        guard = ([(sm, sm.n) for sm in ctx["all"] if sm.n > 0] +
                 [(P.ss, P.ss.n), (P.vs, P.vs.n)]) if ctx is not None else []
        P.op("gpsimd", lambda e: e.memset(ones128f[:, :], 1.0), waits=guard)
        g_const = P.op("gpsimd", lambda e: e.memset(ones1f[:, :], 1.0))
        if pre5 is not None:
            lv3 = P.op("sync", lambda e: e.dma_start(
                out=wbuf[2][:, :], in_=wins[2][:, :]), inc=ld, inck=16,
                serial=False)
            ldw = [[(pre5[1], pre5[2])], [(pre5[1], pre5[2])], [(ld, lv3)]]
        else:
            lvs = []
            for i in range(3):
                lvs.append(P.op("sync", lambda e, i=i: e.dma_start(
                    out=wbuf[i][:, :], in_=wins[i][:, :]), inc=ld, inck=16,
                    serial=False))
            ldw = [[(ld, lvs[2])]] * 3
        prev_pe = 0
        for i in range(3):
            sa = P.op("scalar", lambda e, i=i: e.activation(
                absb[:, :], wbuf[i][:, :], AF.Abs, accum_out=asum[:, :]),
                waits=ldw[i] + [(pe, prev_pe)] + (guard if i == 0 else []))
            prev_pe = P.op("tensor", lambda e: e.matmul(
                pD[:, 0:1], asum[:, :], ones128f[:, :], start=True, stop=True),
                waits=[(P.ss, sa), (P.vs, P.vs.n), (P.gs, g_const)], inc=pe)
            P.op("vector", lambda e, i=i: e.tensor_copy(gsum[:, i:i + 1], pD[:, 0:1]),
                 waits=[(pe, prev_pe)])
        z = P.op("vector", lambda e: e.tensor_scalar_mul(gsum[:, 3:4], gsum[:, 0:1], 0.0))
        wr = P.op("gpsimd", lambda e: e.dma_start(out=arin[:, :], in_=gsum[:, :]),
                  waits=[(P.vs, z)], inc=ard, inck=16, ordered=False)
        cc = P.op("gpsimd", lambda e: e.collective_compute(
            "AllReduce", ALU.add, replica_groups=[list(range(NCORES))],
            ins=[arin.ap().opt()], outs=[arout.ap().opt()]),
            waits=[(ard, wr)], inc=ccs, inck=1, ordered=False)
        rd = P.op("gpsimd", lambda e: e.dma_start(out=gtot[:, :], in_=arout[:, :]),
                  waits=[(ccs, cc)], inc=ard, inck=16, ordered=False)
        P.op("vector", lambda e: e.tensor_scalar(
            rg[:, :], gtot[:, :], 1.0 / QN, Q_EPS, ALU.mult, ALU.add),
            waits=[(ard, rd)])
        vrg = P.op("vector", lambda e: e.reciprocal(rg[:, :], rg[:, :]))
        bc = P.op("tensor", lambda e: e.matmul(
            pE[:, 0:4], ones1f[:, :], rg[:, :], start=True, stop=True),
            waits=[(P.vs, vrg), (P.gs, g_const)], inc=pe)
        vb = P.op("vector", lambda e: e.tensor_copy(rgc[:, :], pE[:, 0:4]),
                  waits=[(pe, bc)])
        prev_st = 0
        for i in range(3):
            P.op("vector", lambda e, i=i: e.tensor_scalar(
                qtmp[:, :], wbuf[i][:, :], rgc[:, i:i + 1], MAGIC, ALU.mult, ALU.add),
                waits=[(P.vs, vb), (stv, prev_st)])
            P.op("vector", lambda e, i=i: e.tensor_scalar(
                wbuf[i][:, :], qtmp[:, :], -MAGIC, 1.0, ALU.add, ALU.min))
            qv = P.op("vector", lambda e, i=i: e.tensor_scalar_max(
                qout[:, :], wbuf[i][:, :], -1.0))
            prev_st = P.op("sync", lambda e, i=i: e.dma_start(
                out=wouts[i][:, :], in_=qout[:, :]), waits=[(P.vs, qv)],
                inc=stv, inck=16)
            if ctx is not None:
                ctx.setdefault("w5_st", []).append((stv, prev_st))
        P.op("sync", lambda e: e.nop(), waits=[(stv, prev_st)])
    return env.finish()



def build_l6(ctx=None):
    env = Env(ctx, "p6_", {"pA": "A0", "pB": "A1", "pC": "C0", "pD": "C1",
                           "pE": "O0"})
    nc = env.nc
    x1fm = env.dram("x1fm", [D, ROWS], F32, "in")
    xn1lh = [env.dram("xn1loc0", [D, 512], BF16, "in"),
             env.dram("xn1loc1", [D, 512], BF16, "in")]
    wo1p = env.dram("wo1p", [FT, 128, D], F32, "in")
    nwc = env.dram("nwc", [128, FT], F32, "in")
    outc = env.dram("outc", [D, ROWS], F32, "out")
    if ctx is None:
        o1loc = env.dram("o1loc", [ROWS, D], BF16, "in")
        w1qp = env.dram("w1qp", [44, 128, D], BF16, "in")
        w2qp = env.dram("w2qp", [44, 128, D], BF16, "in")
        w3qp = env.dram("w3qp", [FT, 128, FF1], BF16, "in")
        def o1_src(ft, t0):
            return o1loc[t0:t0 + 512, ft * 128:(ft + 1) * 128]
        def w12_blk(i, fc):
            return (w1qp if i == 0 else w2qp)[fc]
        def w3_blk(mc):
            return w3qp[mc]
    else:
        o1_src = ctx["o1_src"]
        w12_blk = ctx["w12_blk"]
        w3_blk = ctx["w3_blk"]
    with env.scope():
        sb, ps = env.sb, env.ps
        res = sb("res", [128, FT * 512], F32)
        xn = sb("xn", [128, FT * 512], BF16)
        o1b = sb("o1b", [128, FT * 512], BF16)
        hid = sb("hid", [128, 44 * 512], BF16)
        w12b = [sb(f"w12b{i}", [128, 2 * D], BF16) for i in range(2)]
        w3b = [sb(f"w3b{i}", [128, FF1], BF16) for i in range(2)]
        wofb = sb("wofb", [128, D], F32)
        wob = [sb(f"wob{i}", [128, D], BF16) for i in range(2)]
        nw_s = sb("nw_s", [128, FT], F32)
        s1b = sb("s1b", [128, 512], BF16)
        s2b = sb("s2b", [128, 512], BF16)
        e1b = sb("e1b", [128, 512], F32)
        e2b = sb("e2b", [128, 512], F32)
        t1b = sb("t1b", [128, 512], F32)
        outb = [sb(f"outb{i}", [128, 512], F32) for i in range(2)]
        pA = ps("pA", [128, 512])
        pB = ps("pB", [128, 512])
        pC = ps("pC", [128, 512])
        pD = ps("pD", [128, 512])
        pE = ps("pE", [128, 512])

        P = env.P
        ld = env.sem("ld")
        o1s = env.sem("o1s")
        w12s = [env.sem("w12s0"), env.sem("w12s1")]
        w3s = [env.sem("w3s0"), env.sem("w3s1")]
        wso = env.sem("wso")
        peF = env.sem("peF")
        stv = env.sem("stv")

        lw_nw = P.op("sync", lambda e: e.dma_start(out=nw_s[:, :], in_=nwc[:, :]),
                     inc=ld, inck=16, serial=False)
        last_blend = 0
        for tch in range(2):
            t0 = tch * 512
            P.op("sync", lambda e, t0=t0: e.dma_start(
                out=res.ap().rearrange("p (t n) -> p t n", n=512),
                in_=x1fm[:, t0:t0 + 512].rearrange("(t p) n -> p t n", p=128)),
                waits=[(peF, peF.n), (P.vs, last_blend), (P.ss, P.ss.n)],
                inc=ld, inck=16, serial=False)
            ld_rx = P.op("sync", lambda e, tch=tch: e.dma_start(
                out=xn.ap().rearrange("p (t n) -> p t n", n=512),
                in_=xn1lh[tch].ap().rearrange("(t p) n -> p t n", p=128)),
                inc=ld, inck=16, serial=False)
            for ft in range(FT):
                P.op("sync", lambda e, t0=t0, ft=ft: e.dma_start(
                    out=o1b[:, ft * 512:(ft + 1) * 512],
                    in_=o1_src(ft, t0), transpose=True),
                    waits=([ctx["o1_join"]] if ctx is not None and ft == 0 else []),
                    inc=ld, inck=16, serial=False)
            ld_chunk = ld.n
            mm_grp = {}
            peF_start = peF.n
            ev1 = {}
            ev2 = {}
            hv_ms = {}
            for fc in range(44):
                wla = P.op("gpsimd", lambda e, fc=fc: e.dma_start(
                    out=w12b[fc % 2][:, 0:D], in_=w12_blk(0, fc)),
                    waits=[(peF, mm_grp.get(fc - 2, peF_start))],
                    inc=w12s[fc % 2], inck=16, ordered=False, serial=False)
                wlb = P.op("gpsimd", lambda e, fc=fc: e.dma_start(
                    out=w12b[fc % 2][:, D:2 * D], in_=w12_blk(1, fc)),
                    inc=w12s[fc % 2], inck=16, ordered=False, serial=False)
                bs = pA if fc % 2 == 0 else pC
                bt = pB if fc % 2 == 0 else pD
                for kt in range(FT):
                    w = []
                    if kt == 0:
                        w = [(w12s[fc % 2], wlb), (ld, ld_rx),
                             (P.ss, ev1.get(fc - 2, P.ss.n if fc < 2 else 0))]
                    P.op("tensor", lambda e, fc=fc, kt=kt, bs=bs: e.matmul(
                        bs[:, :], w12b[fc % 2][:, kt * 128:(kt + 1) * 128],
                        xn[:, kt * 512:(kt + 1) * 512],
                        start=(kt == 0), stop=(kt == FT - 1)), waits=w)
                for kt in range(FT):
                    w = [(P.ss, ev2.get(fc - 2, P.ss.n if fc < 2 else 0))] if kt == 0 else []
                    P.op("tensor", lambda e, fc=fc, kt=kt, bt=bt: e.matmul(
                        bt[:, :], w12b[fc % 2][:, D + kt * 128: D + kt * 128 + 128],
                        xn[:, kt * 512:(kt + 1) * 512],
                        start=(kt == 0), stop=(kt == FT - 1)),
                        waits=w, inc=peF if kt == FT - 1 else None)
                mm_grp[fc] = peF.n
                ev1[fc] = P.op("scalar", lambda e, bs=bs: e.activation(
                    s1b[:, :], bs[:, :], AF.Sigmoid),
                    waits=[(peF, mm_grp[fc])] +
                          ([(P.vs, hv_ms[fc - 1])] if fc else [(P.vs, last_blend)]))
                ev2[fc] = P.op("scalar", lambda e, bt=bt: e.activation(
                    s2b[:, :], bt[:, :], AF.Copy))
                hv_ms[fc] = P.op("vector", lambda e, fc=fc: e.tensor_mul(
                    hid[:, fc * 512:(fc + 1) * 512], s1b[:, :], s2b[:, :]),
                    waits=[(P.ss, ev2[fc])])
            hid_done = hv_ms[43]
            ph1_sc = ev2[43]
            last_wosc = 0
            last_aoev = ph1_sc
            f_ev = {}
            f_grp = {}
            peF_start2 = peF.n
            for mc in range(FT):
                wlo = P.op("gpsimd", lambda e, mc=mc: e.dma_start(
                    out=wofb[:, :], in_=wo1p[mc]),
                    waits=[(P.vs, last_wosc)],
                    inc=wso, inck=16, ordered=False, serial=False)
                for kt in range(FT):
                    w = [(wso, wlo), (ld, ld_chunk)] if kt == 0 else []
                    if kt == 0 and not (tch == 0 and mc < 2):
                        w.append((peF, f_grp.get(mc - 2, 0) if mc >= 2 else peF.n))
                    P.op("vector", lambda e, kt=kt, mc=mc: e.tensor_scalar_mul(
                        wob[mc % 2][:, kt * 128:(kt + 1) * 128],
                        wofb[:, kt * 128:(kt + 1) * 128], nw_s[:, kt:kt + 1]),
                        waits=w)
                last_wosc = P.vs.n
                wl3 = P.op("gpsimd", lambda e, mc=mc: e.dma_start(
                    out=w3b[mc % 2][:, :], in_=w3_blk(mc)),
                    waits=[(peF, f_grp.get(mc - 2, peF_start2))],
                    inc=w3s[mc % 2], inck=16, ordered=False, serial=False)
                for kt in range(FT):
                    w = []
                    if kt == 0:
                        w = [(P.vs, last_wosc), (ld, ld_chunk), (P.ss, last_aoev)]
                    P.op("tensor", lambda e, kt=kt, mc=mc: e.matmul(
                        pE[:, :], wob[mc % 2][:, kt * 128:(kt + 1) * 128],
                        o1b[:, kt * 512:(kt + 1) * 512],
                        start=(kt == 0), stop=(kt == FT - 1)),
                        waits=w, inc=peF if kt == FT - 1 else None)
                g_ao = peF.n
                bf_ = pA if mc % 2 == 0 else pC
                for fc in range(44):
                    w = []
                    if fc == 0:
                        w = [(w3s[mc % 2], wl3), (P.vs, hid_done)]
                        w.append((P.ss, f_ev.get(mc - 2, ph1_sc)))
                    P.op("tensor", lambda e, fc=fc, mc=mc, bf_=bf_: e.matmul(
                        bf_[:, :], w3b[mc % 2][:, fc * 128:(fc + 1) * 128],
                        hid[:, fc * 512:(fc + 1) * 512],
                        start=(fc == 0), stop=(fc == 43)),
                        waits=w, inc=peF if fc == 43 else None)
                f_grp[mc] = peF.n
                last_aoev = P.op("scalar", lambda e, mc=mc: e.activation(
                    e2b[:, :], pE[:, :], AF.Copy),
                    waits=[(peF, g_ao)] + ([(P.vs, last_blend)] if (tch or mc) else []))
                f_ev[mc] = P.op("scalar", lambda e, mc=mc, bf_=bf_: e.activation(
                    e1b[:, :], bf_[:, :], AF.Copy), waits=[(peF, f_grp[mc])])
                ob = outb[mc % 2]
                P.op("vector", lambda e, mc=mc: e.tensor_add(
                    t1b[:, :], res[:, mc * 512:(mc + 1) * 512], e2b[:, :]),
                    waits=[(P.ss, f_ev[mc])])
                nx = P.op("vector", lambda e, ob=ob: e.tensor_add(
                    ob[:, :], t1b[:, :], e1b[:, :]), waits=[(stv, stv.n)])
                last_blend = nx
                P.op("sync", lambda e, mc=mc, t0=t0, ob=ob: e.dma_start(
                    out=outc[mc * 128:(mc + 1) * 128, t0:t0 + 512], in_=ob[:, :]),
                    waits=[(P.vs, nx)], inc=stv, inck=16)
        P.op("sync", lambda e: e.nop(), waits=[(stv, stv.n)])
    return env.finish()



def build_fused():
    nc = bass.Bass("TRN2", target_bir_lowering=False)
    ext = {}

    def din(name, shape, dt):
        ext[name] = nc.dram_tensor(name, shape, dt, kind="ExternalInput")

    def dint(name, shape, dt):
        ext[name] = nc.dram_tensor(name, shape, dt, kind="Internal")

    def dout(name, shape, dt):
        ext[name] = nc.dram_tensor(name, shape, dt, kind="ExternalOutput")

    din("p1_xfm", [D, ROWS], F32)
    din("p1_wgt", [FT, 128, 2], F32)
    dint("p1_xn0c0", [D, 512], BF16)
    dint("p1_xn0c1", [D, 512], BF16)
    dint("p1_maskc", [1, ROWS], F32)
    for nm, shp, dt in (("wqp", [2, 128, D], F32), ("wkp", [2, 128, D], F32),
                        ("wv", [D, 256], F32), ("bq", [128, 2], F32),
                        ("bk", [128, 2], F32), ("bv", [1, 256], F32),
                        ("sjtab", [2, 128, NKT], F32), ("mshtab", [1, 16], F32),
                        ("negtab", [1, S], BF16),
                        ("diagmask", [8, 128, 1024], BF16)):
        din("p2_" + nm, shp, dt)
    dint("xn0ag0", [NCORES * D, 512], BF16)
    dint("xn0ag1", [NCORES * D, 512], BF16)
    dint("o0li", [D, ROWS], BF16)
    dint("o0rc", [D, ROWS], BF16)
    ext["p3_xfm"] = ext["p1_xfm"]
    ext["p3_xn0loc0"] = ext["p1_xn0c0"]
    ext["p3_xn0loc1"] = ext["p1_xn0c1"]
    ext["p3_maskr"] = ext["p1_maskc"]
    for nm, shp in (("wo0p", [FT, 128, D]), ("bo0c", [128, FT]),
                    ("w0ap", [32, 128, D]), ("w1ap", [32, 128, D]),
                    ("w0bp", [FT, 128, FF0]), ("w1bp", [FT, 128, FF0]),
                    ("b0ac", [128, 32]), ("b1ac", [128, 32]),
                    ("b0bc", [128, FT]), ("b1bc", [128, FT])):
        din("p3_" + nm, shp, F32)
    dint("p3_x1fm", [D, ROWS], F32)
    dint("p3_xn1c0", [D, 512], BF16)
    dint("p3_xn1c1", [D, 512], BF16)
    dint("xn1ag0", [NCORES * D, 512], BF16)
    dint("xn1ag1", [NCORES * D, 512], BF16)
    for nm, shp, dt in (("wq1p", [2, 128, D], F32), ("wk1p", [2, 128, D], F32),
                        ("wv1", [D, 256], F32), ("ldc", [1, 2], F32),
                        ("wg1p", [1, 128, D], F32), ("wg2c", [128, 256], BF16),
                        ("identb", [128, 128], BF16)):
        din("p4_" + nm, shp, dt)
    dint("o1li", [S, 256], BF16)
    dint("o1rc", [S, 256], BF16)
    for i in (1, 2, 3):
        din(f"p5_w{i}s", [128, QSH], F32)
        dint(f"p5_w{i}q", [128, QSH], BF16)
        dint(f"w{i}qa", [NCORES * 128, QSH], BF16)
    dint("p5_arin", [1, 4], F32)
    dint("p5_arout", [1, 4], F32)
    ext["p6_x1fm"] = ext["p3_x1fm"]
    ext["p6_xn1loc0"] = ext["p3_xn1c0"]
    ext["p6_xn1loc1"] = ext["p3_xn1c1"]
    din("p6_wo1p", [FT, 128, D], F32)
    din("p6_nwc", [128, FT], F32)
    dout("p6_outc", [D, ROWS], F32)

    o0li, o0rc = ext["o0li"], ext["o0rc"]
    o1li, o1rc = ext["o1li"], ext["o1rc"]
    xn0agh = [ext["xn0ag0"], ext["xn0ag1"]]
    xn1agh = [ext["xn1ag0"], ext["xn1ag1"]]

    def xn0_src(tc):
        h, r = tc % 2, tc // 2
        return xn0agh[h][r * D:(r + 1) * D, :].rearrange(
            "(t p) n -> p t n", p=128)

    def xn1_src(tc):
        h, r = tc % 2, tc // 2
        return xn1agh[h][r * D:(r + 1) * D, :].rearrange(
            "(t p) n -> p t n", p=128)

    def o0_dst(hh, ci):
        rb = ci * 256 + hh * 128
        return o0li[rb:rb + 128, :]

    def o0_src(t0):
        return o0rc[:, t0:t0 + 512].rearrange("(t p) n -> p t n", p=128)

    def o1_dst(t):
        rb = (t // 8) * ROWS + (t % 8) * 128
        return o1li[rb:rb + 128, :]

    def o1_src(ft, t0):
        rb = (ft % 8) * ROWS + t0
        cb = (ft // 8) * 128
        return o1rc[rb:rb + 512, cb:cb + 128]

    def w12_blk(i, fc):
        h = ext["w1qa"] if i == 0 else ext["w2qa"]
        return bass.AP(h, fc * 128 * D, [[D, 128], [1, D]])

    def w3_blk(mc):
        return bass.AP(ext["w3qa"], mc * 128 * FF1, [[FF1, 128], [1, FF1]])

    with ExitStack() as gst:
        pss = mk_pool(nc, gst)
        P = Prog(nc, gst)
        ctx = {"nc": nc, "P": P, "gst": gst, "pss": pss, "ext": ext, "all": [],
               "xn0_src": xn0_src, "o0_dst": o0_dst, "o0_src": o0_src,
               "xn1_src": xn1_src, "o1_dst": o1_dst, "o1_src": o1_src,
               "w12_blk": w12_blk, "w3_blk": w3_blk}

        def bar():
            barrier(P, ctx["all"] + [P.vs, P.ss, P.gs])

        def cc(kind, op, ins, outs, waits=()):
            cs = Sem(gst, nc, f"cc{len(ctx.setdefault('_cc', []))}")
            ctx["_cc"].append(cs)
            P.op("gpsimd", lambda e: e.collective_compute(
                kind, op, replica_groups=[list(range(NCORES))],
                ins=[ins.ap().opt()], outs=[outs.ap().opt()]),
                waits=waits, inc=cs, inck=1, ordered=False)
            return cs

        def ccjoin(cs, engs=None):
            for eng in (engs or Prog.ENGS):
                P.op(eng, lambda e, cs=cs: e.nop(), waits=[(cs, 1)], ordered=False)

        pre5_stack = gst.enter_context(ExitStack())
        w5t = [pre5_stack.enter_context(
            nc.sbuf_tensor(f"pre5_w{i}", [128, QSH], F32)) for i in range(2)]
        ld5 = Sem(gst, nc, "pre5_ld")
        ctx["all"].append(ld5)
        for i in range(2):
            P.op("sync", lambda e, i=i: e.dma_start(
                out=w5t[i][:, :], in_=ext[f"p5_w{i + 1}s"][:, :]),
                inc=ld5, inck=16, ordered=False, serial=False)
        ctx["pre5"] = (w5t, ld5, ld5.n)
        build_l1(ctx)
        cc0a = cc("AllGather", ALU.bypass, ext["p1_xn0c0"], xn0agh[0],
                  waits=[ctx["xn0_st"][0]])
        build_l5(ctx)
        cc0b = cc("AllGather", ALU.bypass, ext["p1_xn0c1"], xn0agh[1],
                  waits=[ctx["xn0_st"][1]])
        ctx["xn0_join"] = [(cc0a, 1), (cc0b, 1)]
        pre5_stack.close()
        bar()
        ccw = []

        def post_warm(l2_warm):
            for i in (1, 2, 3):
                ccw.append(cc("AllGather", ALU.bypass, ext[f"p5_w{i}q"],
                              ext[f"w{i}qa"],
                              waits=[ctx["w5_st"][i - 1]] + l2_warm))
        ctx["post_warm"] = post_warm
        build_l2(ctx)
        bar()
        cc5 = cc("AllToAll", ALU.bypass, o0li, o0rc)
        ctx["o0_join"] = (cc5, 1)
        pre4_stack = gst.enter_context(ExitStack())
        p4t = {}
        for nm, shp, dt in (("wg1_s", [128, D], BF16),
                            ("wg2_s", [128, 256], BF16),
                            ("ld_s", [1, 2], F32),
                            ("id_s", [128, 128], BF16)):
            p4t[nm] = pre4_stack.enter_context(
                nc.sbuf_tensor("pre4_" + nm, shp, dt))
        ld4 = Sem(gst, nc, "pre4_ld")
        ctx["all"].append(ld4)
        def cdma4(dst, src):
            return P.op("gpsimd", lambda e, d=dst, s=src: e.dma_start(
                out=d, in_=s), inc=ld4, inck=16, ordered=False, serial=False)
        cdma4(p4t["wg1_s"][:, :], ext["p4_wg1p"][0])
        cdma4(p4t["wg2_s"][:, :], ext["p4_wg2c"][:, :])
        cdma4(p4t["ld_s"][:, :], ext["p4_ldc"][:, :])
        ld4_const = cdma4(p4t["id_s"][:, :], ext["p4_identb"][:, :])
        ctx["pre4"] = ((p4t["wg1_s"], p4t["wg2_s"], p4t["ld_s"], p4t["id_s"]),
                       ld4, ld4_const)
        build_l3(ctx)
        cc6h = [cc("AllGather", ALU.bypass, ext[f"p3_xn1c{h}"], xn1agh[h],
                   waits=[ctx["xn1_st"][h]]) for h in range(2)]
        ctx["xn1_join"] = [(cc6h[0], 1), (cc6h[1], 1)]
        bar()
        build_l4(ctx)
        pre4_stack.close()
        bar()
        cc7 = cc("AllToAll", ALU.bypass, o1li, o1rc)
        ctx["o1_join"] = (cc7, 1)
        for cs in ccw:
            ccjoin(cs, ["gpsimd"])
        build_l6(ctx)
        from concourse.aot_env import allow_ici_all_to_all
        with allow_ici_all_to_all():
            P.run(nc)
    return nc


def kernel_fused(inputs):
    g = _prep(inputs)
    nc = build_fused()
    maps = []
    for c in range(NCORES):
        m = {"p1_xfm": g[f"xfm_{c}"], "p1_wgt": g["wgt"]}
        for nm in ["wqp", "wkp", "wv", "bq", "bk", "bv", "sjtab", "mshtab", "negtab"]:
            m["p2_" + nm] = g[f"{nm}_{c}"]
        m["p2_diagmask"] = g["diagmask"]
        m["p3_wo0p"] = g["wo0pg"]
        for nm in ["bo0c", "w0ap", "w1ap", "w0bp", "w1bp",
                   "b0ac", "b1ac", "b0bc", "b1bc"]:
            m["p3_" + nm] = g[nm]
        m["p4_wq1p"] = g[f"wq1p_{c}"]
        m["p4_wk1p"] = g[f"wk1p_{c}"]
        m["p4_wv1"] = g[f"wv1_{c}"]
        m["p4_ldc"] = g[f"ld_{c}"]
        m["p4_wg1p"] = g["wg1p"]
        m["p4_wg2c"] = g[f"wg2_{c}"]
        m["p4_identb"] = g["identb"]
        for i in (1, 2, 3):
            m[f"p5_w{i}s"] = g[f"w{i}s_{c}"]
        m["p6_wo1p"] = g["wo1p"]
        m["p6_nwc"] = g["nwc"]
        maps.append(m)
    res = run_bass_kernel_spmd(nc, maps, core_ids=list(range(NCORES)))
    out = np.concatenate([np.asarray(res.results[c]["p6_outc"])
                          for c in range(NCORES)], 1)
    return np.ascontiguousarray(out.T).reshape(1, S, D)



def _run(nc, maps):
    return run_bass_kernel_spmd(nc, maps, core_ids=list(range(NCORES)))


def kernel(**inputs):
    return kernel_fused(inputs)


def kernel_unfused(**inputs):
    g = _prep(inputs)
    cores = list(range(NCORES))

    r1 = _run(build_l1(), [{"xfm": g[f"xfm_{c}"], "wgt": g["wgt"]} for c in cores])
    xn0c = [np.concatenate([np.asarray(r1.results[c]["xn0c0"]),
                            np.asarray(r1.results[c]["xn0c1"])], 1)
            for c in cores]
    maskc = [np.asarray(r1.results[c]["maskc"]) for c in cores]
    xn0full = np.ascontiguousarray(np.concatenate(xn0c, 1))

    r5 = _run(build_l5(), [{"w1s": g[f"w1s_{c}"], "w2s": g[f"w2s_{c}"],
                            "w3s": g[f"w3s_{c}"]} for c in cores])
    w1qp = np.ascontiguousarray(np.stack(
        [np.asarray(r5.results[c]["w1q"]) for c in cores]).reshape(44, 128, D))
    w2qp = np.ascontiguousarray(np.stack(
        [np.asarray(r5.results[c]["w2q"]) for c in cores]).reshape(44, 128, D))
    w3qp = np.ascontiguousarray(np.stack(
        [np.asarray(r5.results[c]["w3q"]) for c in cores]).reshape(FT, 128, FF1))

    maps2 = []
    for c in cores:
        maps2.append({"xn0full": xn0full, "wqp": g[f"wqp_{c}"], "wkp": g[f"wkp_{c}"],
                      "wv": g[f"wv_{c}"], "bq": g[f"bq_{c}"], "bk": g[f"bk_{c}"],
                      "bv": g[f"bv_{c}"], "sjtab": g[f"sjtab_{c}"],
                      "mshtab": g[f"mshtab_{c}"], "negtab": g[f"negtab_{c}"],
                      "diagmask": g["diagmask"]})
    r2 = _run(build_l2(), maps2)
    O0 = np.zeros((D, S), BF)
    for c in cores:
        oc = np.asarray(r2.results[c]["o0c"])
        for i_h, h in enumerate(heads_of(c)):
            O0[h * 128:(h + 1) * 128] = oc[i_h * 128:(i_h + 1) * 128]

    maps3 = []
    for c in cores:
        r0, r1_ = c * ROWS, (c + 1) * ROWS
        m = {"xfm": g[f"xfm_{c}"], "xn0loc0": xn0c[c][:, :512].copy(),
             "xn0loc1": xn0c[c][:, 512:].copy(),
             "o0loc": np.ascontiguousarray(O0[:, r0:r1_]), "maskr": maskc[c]}
        for nm in ["wo0p", "bo0c", "w0ap", "w1ap", "w0bp", "w1bp",
                   "b0ac", "b1ac", "b0bc", "b1bc"]:
            m[nm] = g[nm]
        maps3.append(m)
    r3 = _run(build_l3(), maps3)
    x1c = [np.asarray(r3.results[c]["x1fm"]) for c in cores]
    xn1c = [np.concatenate([np.asarray(r3.results[c]["xn1c0"]),
                            np.asarray(r3.results[c]["xn1c1"])], 1)
            for c in cores]
    xn1full = np.ascontiguousarray(np.concatenate(xn1c, 1))

    maps4 = []
    for c in cores:
        maps4.append({"xn1full": xn1full, "wq1p": g[f"wq1p_{c}"],
                      "wk1p": g[f"wk1p_{c}"], "wv1": g[f"wv1_{c}"],
                      "ldc": g[f"ld_{c}"], "wg1p": g["wg1p"],
                      "wg2c": g[f"wg2_{c}"], "identb": g["identb"]})
    r4 = _run(build_l4(), maps4)
    o1full = np.zeros((S, D), BF)
    for c in cores:
        oc = np.asarray(r4.results[c]["o1c"])
        for i_h, h in enumerate(heads_of(c)):
            o1full[:, h * 128:(h + 1) * 128] = oc[:, i_h * 128:(i_h + 1) * 128]

    maps6 = []
    for c in cores:
        r0, r1_ = c * ROWS, (c + 1) * ROWS
        maps6.append({"x1fm": x1c[c], "xn1loc0": xn1c[c][:, :512].copy(),
                      "xn1loc1": xn1c[c][:, 512:].copy(),
                      "o1loc": np.ascontiguousarray(o1full[r0:r1_]),
                      "wo1p": g["wo1p"], "nwc": g["nwc"],
                      "w1qp": w1qp, "w2qp": w2qp, "w3qp": w3qp})
    r6 = _run(build_l6(), maps6)
    out = np.concatenate([np.asarray(r6.results[c]["outc"]) for c in cores], 1)
    return np.ascontiguousarray(out.T).reshape(1, S, D)


# version 23
# speedup vs baseline: 1.1012x; 1.1012x over previous
import math
from contextlib import ExitStack

import numpy as np
import ml_dtypes
import concourse.bass as bass
import concourse.mybir as mybir
from concourse.bass_utils import run_bass_kernel_spmd

F32 = mybir.dt.float32
BF16 = mybir.dt.bfloat16
AF = mybir.ActivationFunctionType
ALU = mybir.AluOpType
AX = mybir.AxisListType

NCORES = 8
S = 8192
D = 2048
NH = 16
HD = 128
ROWS = S // NCORES
FT = D // 128
NKT = S // 128
QCH = 512
NQC = S // QCH
MAGIC = float(np.float32(1.5 * 2 ** 23))
SCL_Q = 1.0 / math.sqrt(HD)
SLOPES = [2.0 ** (-k / 2) for k in range(NH)]
LN_EPS = 1e-5
RMS_EPS = 1e-5
Q_EPS = 1e-8
FF0 = 4096
FF1 = 5632
BF = ml_dtypes.bfloat16


def heads_of(c):
    return [c, c + 8]


def pack_lhsT(A):
    A = np.ascontiguousarray(A)
    K, M = A.shape
    return np.ascontiguousarray(
        A.reshape(K // 128, 128, M // 128, 128).transpose(2, 1, 0, 3).reshape(M // 128, 128, K))



def _prep(inputs):
    g = {}
    x = np.ascontiguousarray(np.asarray(inputs["L0_attn_x"], np.float32).reshape(S, D))
    Wqkv = np.asarray(inputs["L0_attn_Wqkv"], np.float32)
    bqkv = np.asarray(inputs["L0_attn_bqkv"], np.float32)
    Wo0 = np.asarray(inputs["L0_attn_Wo"], np.float32)
    bo0 = np.asarray(inputs["L0_attn_bo"], np.float32)
    Wg = np.asarray(inputs["L0_ffn_Wg"], np.float32)
    W0a = np.asarray(inputs["L0_ffn_W0a"], np.float32)
    b0a = np.asarray(inputs["L0_ffn_b0a"], np.float32)
    W0b = np.asarray(inputs["L0_ffn_W0b"], np.float32)
    b0b = np.asarray(inputs["L0_ffn_b0b"], np.float32)
    W1a = np.asarray(inputs["L0_ffn_W1a"], np.float32)
    b1a = np.asarray(inputs["L0_ffn_b1a"], np.float32)
    W1b = np.asarray(inputs["L0_ffn_W1b"], np.float32)
    b1b = np.asarray(inputs["L0_ffn_b1b"], np.float32)
    ld = np.asarray(inputs["L1_attn_log_decay"], np.float32)
    Wq1 = np.asarray(inputs["L1_attn_Wq"], np.float32)
    Wk1 = np.asarray(inputs["L1_attn_Wk"], np.float32)
    Wv1 = np.asarray(inputs["L1_attn_Wv"], np.float32)
    Wo1 = np.asarray(inputs["L1_attn_Wo"], np.float32)
    Wg1 = np.asarray(inputs["L1_attn_Wg1"], np.float32)
    Wg2 = np.asarray(inputs["L1_attn_Wg2"], np.float32)
    nw = np.asarray(inputs["L1_attn_norm_w"], np.float32)
    w1 = np.asarray(inputs["L1_ffn_w1"], np.float32)
    w2 = np.asarray(inputs["L1_ffn_w2"], np.float32)
    w3 = np.asarray(inputs["L1_ffn_w3"], np.float32)

    for c in range(NCORES):
        g[f"xfm_{c}"] = np.ascontiguousarray(x[c * ROWS:(c + 1) * ROWS].T)

    for c in range(NCORES):
        hs = heads_of(c)
        qcols = np.concatenate([Wqkv[:, 0 * D + h * HD:0 * D + (h + 1) * HD] for h in hs], 1)
        kcols = np.concatenate([Wqkv[:, 1 * D + h * HD:1 * D + (h + 1) * HD] for h in hs], 1)
        vcols = np.concatenate([Wqkv[:, 2 * D + h * HD:2 * D + (h + 1) * HD] for h in hs], 1)
        g[f"wqp_{c}"] = pack_lhsT(qcols)
        g[f"wkp_{c}"] = pack_lhsT(kcols)
        g[f"wv_{c}"] = np.ascontiguousarray(vcols)
        bq = np.concatenate([bqkv[0 * D + h * HD:0 * D + (h + 1) * HD] for h in hs])
        bk = np.concatenate([bqkv[1 * D + h * HD:1 * D + (h + 1) * HD] for h in hs])
        bv = np.concatenate([bqkv[2 * D + h * HD:2 * D + (h + 1) * HD] for h in hs])
        g[f"bq_{c}"] = np.ascontiguousarray(bq.reshape(2, 128).T)
        g[f"bk_{c}"] = np.ascontiguousarray(bk.reshape(2, 128).T)
        g[f"bv_{c}"] = np.ascontiguousarray(bv.reshape(1, 256))
        sj = np.zeros((2, 128, NKT), np.float32)
        msh = np.zeros((1, 16), np.float32)
        for i_h, h in enumerate(hs):
            sl = np.float32(SLOPES[h])
            kk = np.arange(128, dtype=np.float64)[:, None]
            jj = np.arange(NKT, dtype=np.float64)[None, :]
            sj[i_h] = (sl * (jj * 128 + kk)).astype(np.float32)
            if i_h == 1:
                msh[0, i_h * 8:(i_h + 1) * 8] = sl * np.arange(8) * 1024.0
        g[f"sjtab_{c}"] = sj
        g[f"mshtab_{c}"] = msh
        g[f"negtab_{c}"] = (-np.float64(SLOPES[hs[0]]) *
                            np.arange(S, dtype=np.float64)).astype(BF).reshape(1, S)
    dm = np.zeros((8, 128, 1024), np.float32)
    kk = np.arange(128)[:, None]
    qq = np.arange(1024)[None, :]
    for t in range(8):
        dm[t] = (qq - kk >= 128 * t).astype(np.float32)
    g["diagmask"] = dm.astype(BF)

    g["wo0p"] = pack_lhsT(Wo0)
    order = [h for r in range(NCORES) for h in (r, r + 8)]
    g["wo0pg"] = pack_lhsT(Wo0.reshape(NH, HD, D)[order].reshape(D, D))
    g["bo0c"] = np.ascontiguousarray(bo0.reshape(FT, 128).T)
    g["w0ap"] = pack_lhsT(np.ascontiguousarray(W0a.T))
    g["w1ap"] = pack_lhsT(np.ascontiguousarray(W1a.T))
    g["w0bp"] = pack_lhsT(np.ascontiguousarray(W0b.T))
    g["w1bp"] = pack_lhsT(np.ascontiguousarray(W1b.T))
    g["b0ac"] = np.ascontiguousarray(b0a.reshape(32, 128).T)
    g["b1ac"] = np.ascontiguousarray(b1a.reshape(32, 128).T)
    g["b0bc"] = np.ascontiguousarray(b0b.reshape(FT, 128).T)
    g["b1bc"] = np.ascontiguousarray(b1b.reshape(FT, 128).T)
    g["wgt"] = np.ascontiguousarray(Wg.T.reshape(FT, 128, 2))

    for c in range(NCORES):
        hs = heads_of(c)
        g[f"wq1p_{c}"] = pack_lhsT(np.concatenate([Wq1[:, h * HD:(h + 1) * HD] for h in hs], 1))
        g[f"wk1p_{c}"] = pack_lhsT(np.concatenate([Wk1[:, h * HD:(h + 1) * HD] for h in hs], 1))
        g[f"wv1_{c}"] = np.ascontiguousarray(
            np.concatenate([Wv1[:, h * HD:(h + 1) * HD] for h in hs], 1))
        g[f"ld_{c}"] = np.ascontiguousarray(ld[hs].reshape(1, 2))
        g[f"wg2_{c}"] = np.ascontiguousarray(
            np.concatenate([Wg2[:, h * HD:(h + 1) * HD] for h in hs],
                           1)).astype(BF)
    g["wg1p"] = pack_lhsT(Wg1)

    g["wo1p"] = pack_lhsT(Wo1)
    g["nwc"] = np.ascontiguousarray(nw.reshape(FT, 128).T)
    nq = FF1 * D // NCORES // 128
    w1tp = pack_lhsT(np.ascontiguousarray(w1.T)).reshape(NCORES, 128, nq)
    w2tp = pack_lhsT(np.ascontiguousarray(w2.T)).reshape(NCORES, 128, nq)
    w3tp = pack_lhsT(np.ascontiguousarray(w3.T)).reshape(NCORES, 128, nq)
    for c in range(NCORES):
        g[f"w1s_{c}"] = np.ascontiguousarray(w1tp[c])
        g[f"w2s_{c}"] = np.ascontiguousarray(w2tp[c])
        g[f"w3s_{c}"] = np.ascontiguousarray(w3tp[c])
    g["identb"] = np.eye(128, dtype=BF)
    return g



class Sem:
    def __init__(self, stack, nc, name):
        self.h = stack.enter_context(nc.semaphore(name))
        self.n = 0

    def bump(self, k=1):
        self.n += k
        return self.n


class Prog:
    ENGS = ["sync", "gpsimd", "scalar", "vector", "tensor"]
    SELF = ("vector", "scalar", "gpsimd")

    def __init__(self, nc, stack):
        self.ops = {e: [] for e in self.ENGS}
        self.selfs = {e: Sem(stack, nc, f"ss_{e}") for e in self.SELF}

    @property
    def vs(self):
        return self.selfs["vector"]

    @property
    def ss(self):
        return self.selfs["scalar"]

    @property
    def gs(self):
        return self.selfs["gpsimd"]

    def op(self, eng, fn, waits=(), inc=None, inck=1, ordered=True, serial=True):
        ss = self.selfs.get(eng) if ordered else None
        assert ss is None or inc is None, "one sem update per instruction"
        if inc is not None and inck == 16 and serial:
            waits = list(waits) + [(inc, inc.n)]
        pre = ss.n if ss is not None else 0
        tgt = None
        if ss is not None:
            tgt = ss.bump(1)
        elif inc is not None:
            tgt = inc.bump(inck)

        def f(e, fn=fn, waits=tuple(waits), ss=ss, pre=pre, inc=inc, inck=inck):
            for s, v in waits:
                e.wait_ge(s.h, v)
            if ss is not None and pre > 0:
                e.wait_ge(ss.h, pre)
            ins = fn(e)
            if ss is not None:
                ins.then_inc(ss.h, 1)
            elif inc is not None:
                ins.then_inc(inc.h, inck)
        self.ops[eng].append(f)
        return tgt

    def run(self, nc):
        with nc.Block() as block:
            for e in self.ENGS:
                if self.ops[e]:
                    getattr(block, e)(lambda eng, ops=tuple(self.ops[e]): [f(eng) for f in ops])



def mk_pool(nc, stack):
    pAB = stack.enter_context(nc.psum_tensor("g_pAB", [128, 1024], F32))
    pCD = stack.enter_context(nc.psum_tensor("g_pCD", [128, 1024], F32))
    pOP = stack.enter_context(nc.psum_tensor("g_pOP", [128, 1024], F32))
    pD1 = stack.enter_context(nc.psum_tensor("g_pD1", [128, 512], F32))
    ptr = stack.enter_context(nc.psum_tensor("g_ptr", [128, 1024], BF16))
    return {
        "pAB": pAB[:, :], "pCD": pCD[:, :], "pOP": pOP[:, :],
        "A0": pAB[:, 0:512], "A1": pAB[:, 512:1024],
        "C0": pCD[:, 0:512], "C1": pCD[:, 512:1024],
        "O0": pOP[:, 0:512], "O1": pOP[:, 512:1024],
        "pD1": pD1[0:1, :], "pDf": pD1[:, :], "ptr": ptr[:, :],
        "pD2v": pOP[0:1, 512:1024], "pGv": pCD[0:1, 512:1024],
    }


class Env:

    def __init__(self, ctx, prefix, psmap):
        self.ctx = ctx
        self.prefix = prefix
        self.psmap = psmap
        if ctx is None:
            self.nc = bass.Bass("TRN2", target_bir_lowering=False)
            self._own_stack = ExitStack()
            self.gst = self._own_stack
            self.P = None
        else:
            self.nc = ctx["nc"]
            self.P = ctx["P"]
            self.gst = ctx["gst"]
        self._scope = None
        self._pool = None

    def dram(self, name, shape, dt, kind):
        if self.ctx is None:
            k = {"in": "ExternalInput", "out": "ExternalOutput",
                 "int": "Internal"}[kind]
            return self.nc.dram_tensor(name, shape, dt, kind=k)
        return self.ctx["ext"][self.prefix + name]

    def scope(self):
        self._scope = ExitStack()
        if self.ctx is None:
            self._own_stack.__enter__()
            self.P = Prog(self.nc, self._own_stack)
        return self._scope

    def sb(self, name, shape, dt):
        return self._scope.enter_context(
            self.nc.sbuf_tensor(self.prefix + name, shape, dt))

    def ps(self, name, shape=None, dt=F32):
        if self.ctx is None:
            if self._pool is None:
                self._pool = mk_pool(self.nc, self._scope)
            return self._pool[self.psmap[name]]
        return self.ctx["pss"][self.psmap[name]]

    def sem(self, name):
        sm = Sem(self.gst, self.nc, self.prefix + name)
        if self.ctx is not None:
            self.ctx["all"].append(sm)
        return sm

    def finish(self):
        self._scope.close()
        if self.ctx is None:
            self.P.run(self.nc)
            self._own_stack.close()
            return self.nc
        return None


def barrier(P, sems):
    snap = [(s, s.n) for s in sems if s.n > 0]
    for eng in Prog.ENGS:
        P.op(eng, lambda e: e.nop(), waits=snap, ordered=False)



def emit_ln_fm(P, bufs, pe, src, dst, ncols, after=(), gate=None,
               norm_after=()):
    ones128f, ones1f = bufs["ones128f"], bufs["ones1f"]
    sq, mu_s, rstd_s = bufs["lnsq"], bufs["ln_mu"], bufs["ln_rstd"]
    mu_r, e2_r, var_r, rstd_r = bufs["ln_r1"], bufs["ln_r2"], bufs["ln_r3"], bufs["ln_r4"]
    pS, pQ, pB1, pB2 = bufs["ln_pS"], bufs["ln_pQ"], bufs["ln_pB1"], bufs["ln_pB2"]
    nch = ncols // 512
    prev_norm_done = P.vs.n
    prev_stats = P.vs.n
    pq_abs = {}
    dg_abs = {}
    last = P.vs.n
    for ci in range(nch):
        first = list(after) if ci == 0 else []
        for kt in range(FT):
            s_in = src[:, kt * ncols + ci * 512: kt * ncols + ci * 512 + 512]
            sl = kt % 2
            s_sq = sq[:, sl * 512:(sl + 1) * 512]
            w = list(first) + ([(P.vs, prev_stats)] if kt == 0 else [])
            P.op("tensor", lambda e, r=s_in, kt=kt: e.matmul(
                pS[:, :], ones128f[:, :], r, start=(kt == 0), stop=(kt == FT - 1)),
                waits=w)
            idx = ci * FT + kt
            w = list(first) + ([(P.vs, prev_norm_done)] if kt == 0 else []) + (
                [(pe, pq_abs[idx - 2])] if idx - 2 in pq_abs else [])
            sc_sq = P.op("scalar", lambda e, a=s_in, b=s_sq: e.activation(b, a, AF.Square),
                         waits=w)
            pq_abs[idx] = P.op("tensor", lambda e, r=s_sq, kt=kt: e.matmul(
                pQ[:, :], ones128f[:, :], r, start=(kt == 0), stop=(kt == FT - 1)),
                waits=[(P.ss, sc_sq)] + ([(P.vs, prev_stats)] if kt == 0 else []),
                inc=pe)
        pe_sums = pq_abs[ci * FT + FT - 1]
        P.op("vector", lambda e: e.tensor_scalar_mul(mu_r[:, :], pS[:, :], 1.0 / D),
             waits=[(pe, pe_sums)])
        P.op("vector", lambda e: e.tensor_scalar_mul(e2_r[:, :], pQ[:, :], 1.0 / D))
        P.op("vector", lambda e: e.tensor_mul(var_r[:, :], mu_r[:, :], mu_r[:, :]))
        P.op("vector", lambda e: e.tensor_sub(var_r[:, :], e2_r[:, :], var_r[:, :]))
        P.op("vector", lambda e: e.tensor_scalar_add(var_r[:, :], var_r[:, :], LN_EPS))
        ve_recip = P.op("vector", lambda e: e.reciprocal(rstd_r[:, :], var_r[:, :]))
        prev_stats = ve_recip
        sc_rstd = P.op("scalar", lambda e: e.activation(rstd_r[:, :], rstd_r[:, :], AF.Sqrt),
                       waits=[(P.vs, ve_recip)])
        P.op("tensor", lambda e: e.matmul(pB1[:, :], ones1f[:, :], mu_r[:, :],
                                          start=True, stop=True),
             waits=[(P.vs, ve_recip)])
        pe_bc = P.op("tensor", lambda e: e.matmul(pB2[:, :], ones1f[:, :], rstd_r[:, :],
                                                  start=True, stop=True),
                     waits=[(P.ss, sc_rstd)], inc=pe)
        P.op("vector", lambda e: e.tensor_copy(mu_s[:, :], pB1[:, :]), waits=[(pe, pe_bc)])
        P.op("vector", lambda e: e.tensor_copy(rstd_s[:, :], pB2[:, :]))
        for kt in range(FT):
            sl = kt % 2
            s_in = src[:, kt * ncols + ci * 512: kt * ncols + ci * 512 + 512]
            s_tmp = sq[:, sl * 512:(sl + 1) * 512]
            s_out = dst[:, kt * ncols + ci * 512: kt * ncols + ci * 512 + 512]
            idx = ci * FT + kt
            w = [(pe, dg_abs[idx - 2])] if (gate and idx - 2 in dg_abs) else []
            if idx == 0:
                w = w + list(norm_after)
            t1 = P.op("vector", lambda e, a=s_in, t=s_tmp: e.tensor_sub(t, a, mu_s[:, :]),
                      waits=w)
            last = P.op("vector",
                        lambda e, t=s_tmp, o=s_out: e.tensor_mul(o, t, rstd_s[:, :]))
            if gate:
                dg_abs[idx] = P.op("tensor", lambda e, t=s_tmp, kt=kt, ci=ci: e.matmul(
                    gate["pG"][0:1, :], gate["wd"][:, kt:kt + 1], t,
                    start=(kt == 0), stop=(kt == FT - 1)),
                    waits=[(P.vs, t1)] + ([(P.vs, gate.get("_mev", -1))] if (kt == 0 and "_mev" in gate) else []),
                    inc=pe)
        if gate:
            gm = dg_abs[ci * FT + FT - 1]
            moff = gate.get("moff", 0)
            gate["_mev"] = P.op("vector", lambda e, ci=ci, moff=moff: e.tensor_scalar(
                gate["mrow"][:, moff + ci * 512: moff + (ci + 1) * 512],
                gate["pG"][0:1, :], 0.0, None,
                ALU.is_ge), waits=[(pe, gm)])
            last = gate["_mev"]
        prev_norm_done = last
    return P.vs, last



def build_l1(ctx=None):
    env = Env(ctx, "p1_", {"ln_pS": "pD1", "ln_pQ": "pD2v", "ln_pB1": "C0",
                           "ln_pB2": "A0", "pG": "pGv"})
    nc = env.nc
    xfm = env.dram("xfm", [D, ROWS], F32, "in")
    wgt = env.dram("wgt", [FT, 128, 2], F32, "in")
    xn0ch = [env.dram("xn0c0", [D, 512], BF16, "out"),
             env.dram("xn0c1", [D, 512], BF16, "out")]
    maskc = env.dram("maskc", [1, ROWS], F32, "out")
    with env.scope():
        sb, ps = env.sb, env.ps
        bufs = {
            "ones128f": sb("ones128f", [128, 1], F32),
            "ones1f": sb("ones1f", [1, 128], F32),
            "lnsq": sb("lnsq", [128, 2 * 512], F32),
            "ln_mu": sb("ln_mu", [128, 512], F32),
            "ln_rstd": sb("ln_rstd", [128, 512], F32),
            "ln_r1": sb("ln_r1", [1, 512], F32),
            "ln_r2": sb("ln_r2", [1, 512], F32),
            "ln_r3": sb("ln_r3", [1, 512], F32),
            "ln_r4": sb("ln_r4", [1, 512], F32),
            "ln_pS": ps("ln_pS", [1, 512]),
            "ln_pQ": ps("ln_pQ", [1, 512]),
            "ln_pB1": ps("ln_pB1", [128, 512]),
            "ln_pB2": ps("ln_pB2", [128, 512]),
        }
        resc = [sb(f"resc{i}", [128, FT * 512], F32) for i in range(2)]
        xnb = sb("xnb", [128, FT * 512], BF16)
        wg_s = sb("wg_s", [128, FT * 2], F32)
        wd_s = sb("wd_s", [128, FT], F32)
        mrow = sb("mrow", [1, ROWS], F32)
        pG = ps("pG", [1, 512])
        P = env.P
        pe = env.sem("pe")
        ld = env.sem("ld")
        lds = [env.sem("ldr0"), env.sem("ldr1")]
        stv = env.sem("stv")
        P.op("gpsimd", lambda e: e.memset(bufs["ones128f"][:, :], 1.0))
        const_ok = P.op("gpsimd", lambda e: e.memset(bufs["ones1f"][:, :], 1.0))
        ld_w = P.op("sync", lambda e: e.dma_start(
            out=wg_s.ap().rearrange("p (t e) -> p t e", e=2),
            in_=wgt.ap().rearrange("t p e -> p t e")), inc=ld, inck=16)
        ldcv = []
        for ch in range(2):
            ldcv.append(P.op("sync", lambda e, ch=ch: e.dma_start(
                out=resc[ch].ap().rearrange("p (t n) -> p t n", n=512),
                in_=xfm[:, ch * 512:(ch + 1) * 512].rearrange(
                    "(t p) n -> p t n", p=128)),
                inc=lds[ch], inck=16, serial=False))
        P.op("vector", lambda e: e.tensor_sub(
            wd_s.ap().rearrange("p (t o) -> p t o", o=1),
            wg_s.ap().rearrange("p (t e) -> p t e", e=2)[:, :, 0:1],
            wg_s.ap().rearrange("p (t e) -> p t e", e=2)[:, :, 1:2]),
            waits=[(ld, ld_w)])
        gate = {"wd": wd_s, "pG": pG}
        xn0_st = []
        st_prev = 0
        gate["mrow"] = mrow
        for ch in range(2):
            gate["moff"] = ch * 512
            vsem, done = emit_ln_fm(
                P, bufs, pe, resc[ch], xnb, 512,
                after=[(lds[ch], ldcv[ch]), (P.gs, const_ok)], gate=gate,
                norm_after=([(stv, st_prev)] if ch else []))
            P.op("sync", lambda e, ch=ch: e.dma_start(
                out=xn0ch[ch].ap().rearrange("(t p) n -> p t n", p=128),
                in_=xnb.ap().rearrange("p (t n) -> p t n", n=512)),
                waits=[(vsem, done)], inc=stv, inck=16)
            st_prev = stv.n
            xn0_st.append((stv, stv.n))
        if ctx is not None:
            ctx["xn0_st"] = xn0_st
        stx = P.op("sync", lambda e: e.dma_start(out=maskc[:, :], in_=mrow[:, :]),
                   waits=[(P.vs, P.vs.n)], inc=stv, inck=16)
        P.op("sync", lambda e: e.nop(), waits=[(stv, stx)])
    return env.finish()



def build_l2(ctx=None):
    env = Env(ctx, "p2_", {"pAB": "pAB", "pCD": "pCD", "pOP": "pOP",
                           "pD": "pD1", "pDf": "pDf", "A0": "A0", "A1": "A1",
                           "C0": "C0", "C1": "C1", "O0": "O0", "O1": "O1"})
    nc = env.nc
    wqp = env.dram("wqp", [2, 128, D], F32, "in")
    wkp = env.dram("wkp", [2, 128, D], F32, "in")
    wv = env.dram("wv", [D, 256], F32, "in")
    bq = env.dram("bq", [128, 2], F32, "in")
    bk = env.dram("bk", [128, 2], F32, "in")
    bv = env.dram("bv", [1, 256], F32, "in")
    sjtab = env.dram("sjtab", [2, 128, NKT], F32, "in")
    mshtab = env.dram("mshtab", [1, 16], F32, "in")
    negtab = env.dram("negtab", [1, S], BF16, "in")
    diagmask = env.dram("diagmask", [8, 128, 1024], BF16, "in")
    if ctx is None:
        xn0full = env.dram("xn0full", [D, S], BF16, "in")
        o0c = env.dram("o0c", [256, S], BF16, "out")
        def xn_src(tc):
            return xn0full[:, tc * 512:(tc + 1) * 512].rearrange(
                "(t p) n -> p t n", p=128)
        def o0_dst(hh, ci):
            return o0c[hh * 128:(hh + 1) * 128, ci * 1024:(ci + 1) * 1024]
    else:
        xn_src = ctx["xn0_src"]
        o0_dst = ctx["o0_dst"]
    with env.scope():
        sb, ps = env.sb, env.ps
        xnb = [sb(f"xnb{i}", [128, FT * 512], BF16) for i in range(2)]
        qf = sb("qf", [128, 2 * S], BF16)
        kf = sb("kf", [128, 2 * S], BF16)
        vt = sb("vt", [128, NKT * 256], BF16)
        wq_s = sb("wq_s", [128, 2 * D], BF16)
        wk_s = sb("wk_s", [128, 2 * D], BF16)
        wv_s = sb("wv_s", [128, FT * 256], BF16)
        bq_s = sb("bq_s", [128, 2], F32)
        bqs = sb("bqs", [128, 2], F32)
        bk_s = sb("bk_s", [128, 2], F32)
        bv_s = sb("bv_s", [1, 256], F32)
        bvt = sb("bvt", [128, 256], F32)
        sj_s = sb("sj_s", [128, 2 * NKT], F32)
        msh_s = sb("msh_s", [1, 16], F32)
        msh2 = sb("msh2", [1, 16], F32)
        ngt_c = [sb(f"ngt_c{i}", [1, 1024], BF16) for i in range(2)]
        ones1b = sb("ones1b", [1, 128], BF16)
        mctab = sb("mctab", [1, 32], F32)
        mc2 = sb("mc2", [1, 16], F32)
        mcol = sb("mcol", [128, 1], F32)
        biasc = sb("biasc", [128, NKT], F32)
        sacc = [sb(f"sacc{i}", [128, 1024], F32) for i in range(2)]
        ones128f = sb("ones128f", [128, 1], F32)
        dg_s = sb("dg_s", [128, 8 * 1024], BF16)
        px = [sb(f"px{i}", [128, 1024], BF16) for i in range(3)]
        pm = [sb(f"pm{i}", [128, 1024], BF16) for i in range(3)]
        clampb = [sb(f"clampb{i}", [128, 1024], F32) for i in range(2)]
        s2b = sb("s2b", [128, 2 * 512], BF16)
        kmrow = sb("kmrow", [1, 16], F32)
        km2 = sb("km2", [1, 1], F32)
        mrow = [clampb[0][0:1, 0:512], clampb[1][0:1, 0:512]]
        obuf = px

        ones128b = sb("ones128b", [128, 1], BF16)
        ones2b = sb("ones2b", [2, 128], BF16)
        ones1f = sb("ones1f", [1, 128], F32)
        o128th = sb("o128th", [128, 128], BF16)
        ngbF = s2b
        pAB = ps("pAB")
        pCD = ps("pCD")
        pOP = ps("pOP")
        pD = ps("pD")
        pDfull = ps("pDf")
        bA0, bA1 = ps("A0"), ps("A1")
        bC0, bC1 = ps("C0"), ps("C1")
        bO0, bO1 = ps("O0"), ps("O1")

        P = env.P
        ld = env.sem("ld")
        ldxs = [env.sem("ldx0"), env.sem("ldx1")]
        auxd = env.sem("auxd")
        peQ = env.sem("peQ")
        peS = env.sem("peS")
        peP = env.sem("peP")
        peO = env.sem("peO")
        peD = env.sem("peD")
        peB = env.sem("peB")
        stv = env.sem("stv")

        P.op("gpsimd", lambda e: e.memset(ones128b[:, :], 1.0))
        P.op("gpsimd", lambda e: e.memset(o128th[:, :], 1.0 / 128.0))
        P.op("gpsimd", lambda e: e.memset(ones1b[:, :], 1.0))
        P.op("gpsimd", lambda e: e.memset(ones128f[:, :], 1.0))
        g_const = P.op("gpsimd", lambda e: e.memset(ones1f[:, :], 1.0))
        def cdma(dst, src):
            return P.op("gpsimd", lambda e, d=dst, s=src: e.dma_start(out=d, in_=s),
                        inc=ld, inck=16, ordered=False, serial=False)
        cdma(wq_s.ap().rearrange("p (m k) -> p m k", k=D),
             wqp.ap().rearrange("m p k -> p m k"))
        cdma(wk_s.ap().rearrange("p (m k) -> p m k", k=D),
             wkp.ap().rearrange("m p k -> p m k"))
        cdma(wv_s.ap().rearrange("p (t f) -> p t f", f=256),
             wv.ap().rearrange("(t p) f -> p t f", p=128))
        cdma(bq_s[:, :], bq[:, :])
        cdma(bk_s[:, :], bk[:, :])
        cdma(bv_s[:, :], bv[:, :])
        cdma(sj_s.ap().rearrange("p (m k) -> p m k", k=NKT),
             sjtab.ap().rearrange("m p k -> p m k"))
        cdma(msh_s[:, :], mshtab[:, :])
        ld_const = cdma(dg_s.ap().rearrange("p (m k) -> p m k", k=1024),
                        diagmask.ap().rearrange("m p k -> p m k"))
        if ctx is not None and "post_warm" in ctx:
            ctx["post_warm"]([(ld, ld_const), (ldxs[0], 16), (ldxs[1], 16)])
        vs_bqs = P.op("vector", lambda e: e.tensor_scalar_mul(bqs[:, :], bq_s[:, :], SCL_Q),
                      waits=[(ld, ld_const)])
        pe_bv = P.op("tensor", lambda e: e.matmul(bO1[:, 0:256], ones1f[:, :], bv_s[:, :],
                                                  start=True, stop=True),
                     waits=[(ld, ld_const), (P.gs, g_const)], inc=peB)
        vs_bvt = P.op("vector", lambda e: e.tensor_copy(bvt[:, :], bO1[:, 0:256]),
                      waits=[(peB, pe_bv)])

        last_v_evac = vs_bvt
        last_k_evac = None
        ev_bank = {}
        qkv_pe = {}
        for tc in range(FT):
            sl_ = tc % 2
            ldw0 = [(peQ, qkv_pe.get(tc - 2, 0))]
            if ctx is not None and tc < 2:
                ldw0.append(ctx["xn0_join"][tc % 2])
            ldx_v = P.op("sync", lambda e, tc=tc, sl_=sl_: e.dma_start(
                out=xnb[sl_].ap().rearrange("p (t n) -> p t n", n=512),
                in_=xn_src(tc)),
                waits=ldw0, inc=ldxs[sl_], inck=16,
                serial=False)
            qkbanks = {("q", 0): bA0, ("q", 1): bA1, ("k", 0): bC0, ("k", 1): bC1}
            for which, wsb, dst, bias in (("q", wq_s, qf, bqs),
                                          ("k", wk_s, kf, bk_s)):
                for hh in range(2):
                    bank = qkbanks[(which, hh)]
                    ev_prev = ev_bank.get((which, hh))
                    for kt in range(FT):
                        w = []
                        if kt == 0:
                            w.append((ldxs[sl_], ldx_v))
                            if ev_prev is not None:
                                w.append((P.ss, ev_prev))
                            else:
                                w.append((P.vs, vs_bvt))
                        P.op("tensor", lambda e, hh=hh, kt=kt, wsb=wsb, bank=bank,
                             sl_=sl_: e.matmul(
                            bank[:, :], wsb[:, hh * D + kt * 128: hh * D + kt * 128 + 128],
                            xnb[sl_][:, kt * 512:(kt + 1) * 512],
                            start=(kt == 0), stop=(kt == FT - 1)),
                            waits=w, inc=peQ if kt == FT - 1 else None)
                    grp = peQ.n
                    ev_bank[(which, hh)] = P.op(
                        "scalar",
                        lambda e, hh=hh, tc=tc, dst=dst, bias=bias, bank=bank: e.activation(
                            dst[:, hh * S + tc * 512: hh * S + tc * 512 + 512],
                            bank[:, :], AF.Identity, bias=bias[:, hh:hh + 1],
                            scale=(SCL_Q if dst is qf else 1.0)),
                        waits=[(peQ, grp), (ld, ld_const)] + ([(P.vs, vs_bqs)] if dst is qf else []))
            last_k_evac = ev_bank[("k", 1)]
            for stt in range(4):
                for kt in range(FT):
                    w = []
                    if kt == 0:
                        w.append((ldxs[sl_], ldx_v))
                        w.append((P.vs, last_v_evac))
                    P.op("tensor", lambda e, stt=stt, kt=kt, sl_=sl_: e.matmul(
                        bO0[:, 0:256],
                        xnb[sl_][:, kt * 512 + stt * 128: kt * 512 + stt * 128 + 128],
                        wv_s[:, kt * 256:(kt + 1) * 256],
                        start=(kt == 0), stop=(kt == FT - 1)),
                        waits=w, inc=peQ if kt == FT - 1 else None)
                grpv = peQ.n
                tt = tc * 4 + stt
                last_v_evac = P.op("vector", lambda e, tt=tt: e.scalar_tensor_tensor(
                    vt[:, tt * 256:(tt + 1) * 256], bO0[:, 0:256], 1.0, bvt[:, :],
                    ALU.mult, ALU.add), waits=[(peQ, grpv)])
            qkv_pe[tc] = peQ.n
        qkv_done_s = last_k_evac
        qkv_done_v = last_v_evac

        vb_ms = {}
        for hh in range(2):
            for tc in range(FT):
                sqs = P.op("scalar", lambda e, hh=hh, tc=tc: e.activation(
                    s2b[:, (tc % 2) * 512:(tc % 2) * 512 + 512],
                    kf[:, hh * S + tc * 512: hh * S + tc * 512 + 512], AF.Square),
                    waits=[(P.ss, qkv_done_s), (P.vs, qkv_done_v)] if tc == 0 else
                          [(peD, peD.n)])
                dv = P.op("tensor", lambda e, tc=tc: e.matmul(
                    pD[:, :], ones128b[:, :], s2b[:, (tc % 2) * 512:(tc % 2) * 512 + 512],
                    start=True, stop=True), waits=[(P.ss, sqs), (P.vs, P.vs.n)],
                    inc=peD)
                P.op("vector", lambda e, tc=tc: e.reduce_max(
                    kmrow[:, tc:tc + 1], pD[:, :], axis=AX.X), waits=[(peD, dv)])
            vkm = P.op("vector", lambda e: e.reduce_max(km2[:, :], kmrow[:, :], axis=AX.X))
            for tc in range(FT):
                pp = tc % 2
                sqs = P.op("scalar", lambda e, hh=hh, tc=tc, pp=pp: e.activation(
                    s2b[:, pp * 512:pp * 512 + 512],
                    qf[:, hh * S + tc * 512: hh * S + tc * 512 + 512], AF.Square),
                    waits=[(peD, peD.n)])
                dv = P.op("tensor", lambda e, pp=pp: e.matmul(
                    pD[:, :], ones128b[:, :], s2b[:, pp * 512:pp * 512 + 512],
                    start=True, stop=True),
                    waits=[(P.ss, sqs), (P.vs, P.vs.n)], inc=peD)
                msw = [(peD, dv), (P.vs, vkm)]
                if (hh, tc - 2) in vb_ms:
                    msw.append((P.vs, vb_ms[(hh, tc - 2)]))
                ms = P.op("scalar", lambda e, pp=pp: e.activation(
                    mrow[pp][:, :], pD[:, :], AF.Sqrt, scale=km2[:, :]), waits=msw)
                vb_ms[(hh, tc)] = P.op("vector", lambda e, hh=hh, tc=tc, pp=pp: e.reduce_max(
                    mctab[:, hh * 16 + tc: hh * 16 + tc + 1], mrow[pp][:, :], axis=AX.X),
                    waits=[(P.ss, ms)])
        P.op("vector", lambda e: e.tensor_max(
            mc2.ap().rearrange("o (c t) -> o c t", t=1),
            mctab.ap().rearrange("o (c t) -> o c t", t=2)[:, :, 0:1],
            mctab.ap().rearrange("o (c t) -> o c t", t=2)[:, :, 1:2]))
        mhat_done = P.op("vector", lambda e: e.tensor_add(msh2[:, :], msh_s[:, :],
                                                          mc2[:, :]),
                         waits=[(ld, ld_const)])

        QW = 1024
        NQW = S // QW
        prev_norm = P.vs.n
        prev_exp = P.ss.n
        prev_pv = peP.n
        prev_sum = peO.n
        spairs = [pAB, pCD]
        for hh in range(2):
            for ci in range(NQW):
                jmax = 8 * ci + 7
                q0 = ci * QW
                idx = hh * 8 + ci
                exp_ms = {}
                pv_ms = {}
                clamp_ms = {}
                sig_ms = {}
                ngl = P.op("gpsimd", lambda e, hh=hh, q0=q0, cb=idx % 2: e.dma_start(
                    out=ngt_c[cb][:, :], in_=negtab[0:1, q0:q0 + QW]),
                    waits=[(peS, peS.n)], inc=auxd, inck=16, ordered=False)
                ngb = ngt_c[idx % 2]
                if hh == 0:
                    ngf_ms = 0
                    for half in range(2):
                        bF = P.op("tensor", lambda e, half=half, ngb=ngb: e.matmul(
                            pDfull[:, :], ones1b[:, :],
                            ngb[:, half * 512:(half + 1) * 512],
                            start=True, stop=True),
                            waits=[(auxd, ngl), (peO, prev_sum),
                                   (P.vs, prev_norm if idx else mhat_done),
                                   (P.vs, ngf_ms)], inc=peB)
                        ngf_ms = P.op("vector", lambda e, half=half: e.tensor_copy(
                            ngbF[:, half * 512:(half + 1) * 512], pDfull[:, :]),
                            waits=[(peB, bF), (peS, peS.n)])
                bcv = P.op("tensor", lambda e, idx=idx: e.matmul(
                    pAB[:, 0:1], ones1f[:, :], msh2[:, idx:idx + 1],
                    start=True, stop=True),
                    waits=[(P.vs, mhat_done if idx == 0 else prev_norm),
                           (P.ss, prev_exp), (peO, prev_sum)], inc=peB)
                vmc = P.op("vector", lambda e: e.tensor_copy(mcol[:, :], pAB[:, 0:1]),
                           waits=[(peB, bcv)])
                vbias = P.op("vector", lambda e, hh=hh, jmax=jmax: e.tensor_scalar(
                    biasc[:, 0:jmax + 1], sj_s[:, hh * NKT: hh * NKT + jmax + 1],
                    mcol[:, :], None, ALU.subtract), waits=[(P.ss, prev_exp)])

                def emit_pv(j):
                    rhsP, pready = pvq[j]
                    w = [pready]
                    if j == 0:
                        w.append((P.vs, prev_norm))
                    pv_ms[j] = P.op("tensor", lambda e, hh=hh, j=j, rhsP=rhsP,
                                    jmax=jmax: e.matmul(
                        pOP[:, 0:512], vt[:, j * 256 + hh * 128: j * 256 + hh * 128 + 128],
                        rhsP[:, 0:512], start=(j == 0), stop=(j == jmax)), waits=w)
                    pv_ms[j] = P.op("tensor", lambda e, hh=hh, j=j, rhsP=rhsP,
                                    jmax=jmax: e.matmul(
                        pOP[:, 512:1024],
                        vt[:, j * 256 + hh * 128: j * 256 + hh * 128 + 128],
                        rhsP[:, 512:1024], start=(j == 0), stop=(j == jmax)),
                        inc=peP)
                    if j == 0:
                        v0 = P.op("vector", lambda e, rhsP=rhsP: e.tensor_copy(
                            sacc[0][:, 0:512], rhsP[:, 0:512]),
                            waits=[pready, (peO, prev_sum), (peB, peB.n)])
                        g0 = P.op("gpsimd", lambda e, rhsP=rhsP: e.tensor_copy(
                            sacc[0][:, 512:1024], rhsP[:, 512:1024]),
                            waits=[pready, (peO, prev_sum), (peB, peB.n)])
                        sig_ms[j] = (v0, g0)
                    else:
                        v0 = P.op("vector", lambda e, j=j, rhsP=rhsP: e.tensor_add(
                            sacc[j % 2][:, 0:512], sacc[(j + 1) % 2][:, 0:512],
                            rhsP[:, 0:512]), waits=[pready])
                        g0 = P.op("gpsimd", lambda e, j=j, rhsP=rhsP: e.tensor_add(
                            sacc[j % 2][:, 512:1024], sacc[(j + 1) % 2][:, 512:1024],
                            rhsP[:, 512:1024]), waits=[pready])
                        sig_ms[j] = (v0, g0)

                pvq = {}
                for j in range(jmax + 1):
                    pair = spairs[j % 2]
                    t = j - 8 * ci
                    diag = t >= 0
                    w = []
                    if j <= 1:
                        w += [(P.ss, prev_exp), (P.vs, vbias)]
                    else:
                        if (j - 2) in clamp_ms:
                            w.append((P.vs, clamp_ms[j - 2]))
                        else:
                            w.append((P.ss, exp_ms[j - 2]))
                    if hh == 0:
                        w2 = [(P.vs, ngf_ms)] if j == 0 else []
                        P.op("tensor", lambda e, hh=hh, j=j, q0=q0, pair=pair: e.matmul(
                            pair[:, 0:512], kf[:, hh * S + j * 128: hh * S + j * 128 + 128],
                            qf[:, hh * S + q0: hh * S + q0 + 512],
                            start=True, stop=False), waits=w)
                        P.op("tensor", lambda e, hh=hh, j=j, q0=q0, pair=pair: e.matmul(
                            pair[:, 512:1024],
                            kf[:, hh * S + j * 128: hh * S + j * 128 + 128],
                            qf[:, hh * S + q0 + 512: hh * S + q0 + 1024],
                            start=True, stop=False))
                        P.op("tensor", lambda e, pair=pair: e.matmul(
                            pair[:, 0:512], o128th[:, :], ngbF[:, 0:512],
                            start=False, stop=True), waits=w2)
                        sv = P.op("tensor", lambda e, pair=pair: e.matmul(
                            pair[:, 512:1024], o128th[:, :], ngbF[:, 512:1024],
                            start=False, stop=True), inc=peS)
                    else:
                        P.op("tensor", lambda e, hh=hh, j=j, q0=q0, pair=pair: e.matmul(
                            pair[:, 0:512], kf[:, hh * S + j * 128: hh * S + j * 128 + 128],
                            qf[:, hh * S + q0: hh * S + q0 + 512],
                            start=True, stop=True), waits=w)
                        sv = P.op("tensor", lambda e, hh=hh, j=j, q0=q0, pair=pair: e.matmul(
                            pair[:, 512:1024],
                            kf[:, hh * S + j * 128: hh * S + j * 128 + 128],
                            qf[:, hh * S + q0 + 512: hh * S + q0 + 1024],
                            start=True, stop=True), inc=peS)
                    if diag:
                        cw = [(peS, sv), (P.vs, vbias)]
                        if (j - 2) in exp_ms:
                            cw.append((P.ss, exp_ms[j - 2]))
                        else:
                            cw.append((P.ss, prev_exp))
                        if j <= 1:
                            cw.append((P.vs, prev_norm))
                        cm = P.op("vector", lambda e, j=j, pair=pair: e.tensor_scalar(
                            clampb[j % 2][:, :], pair[:, :],
                            biasc[:, j:j + 1], 55.0, ALU.add, ALU.min), waits=cw)
                        clamp_ms[j] = cm
                        src = clampb[j % 2]
                        ew = [(P.vs, cm)]
                    else:
                        src = pair
                        ew = [(peS, sv), (P.vs, vbias)]
                    if (j - 3) in pv_ms:
                        ew.append((peP, pv_ms[j - 3]))
                        ew.append((P.vs, sig_ms[j - 3][0]))
                        ew.append((P.gs, sig_ms[j - 3][1]))
                    elif j <= 2:
                        ew.append((peP, prev_pv))
                        ew.append((P.vs, prev_norm))
                        ew.append((P.gs, P.gs.n))
                        ew.append((stv, stv.n))
                    ex = P.op("scalar", lambda e, j=j, src=src, diag=diag: e.activation(
                        px[j % 3][:, :], src[:, :], AF.Exp,
                        bias=(0.0 if diag else biasc[:, j:j + 1])),
                        waits=ew)
                    exp_ms[j] = ex
                    if diag:
                        mw = [(P.ss, ex)]
                        if (j - 3) in pv_ms:
                            mw.append((peP, pv_ms[j - 3]))
                            mw.append((P.gs, sig_ms[j - 3][1]))
                            mw.append((P.vs, sig_ms[j - 3][0]))
                        elif j <= 2:
                            mw.append((peP, prev_pv))
                            mw.append((P.vs, prev_norm))
                            mw.append((P.gs, P.gs.n))
                        mm = P.op("vector", lambda e, t=t, j=j: e.tensor_mul(
                            pm[j % 3][:, :], px[j % 3][:, :],
                            dg_s[:, t * 1024:(t + 1) * 1024]),
                            waits=mw)
                        pvq[j] = (pm[j % 3], (P.vs, mm))
                    else:
                        pvq[j] = (px[j % 3], (P.ss, ex))
                    if j >= 2:
                        emit_pv(j - 2)
                emit_pv(jmax - 1)
                emit_pv(jmax)
                rin = sacc[0][0:1, :]
                rinb = clampb[0]
                s0 = P.op("tensor", lambda e, jmax=jmax: e.matmul(
                    pD[:, :], ones128f[:, :], sacc[jmax % 2][:, 0:512],
                    start=True, stop=True),
                    waits=[(P.vs, sig_ms[jmax][0]), (P.gs, sig_ms[jmax][1])], inc=peO)
                P.op("vector", lambda e, rin=rin: e.reciprocal(rin[:, 0:512], pD[:, :]),
                     waits=[(peO, s0)])
                prev_sum = P.op("tensor", lambda e, jmax=jmax: e.matmul(
                    pD[:, :], ones128f[:, :], sacc[jmax % 2][:, 512:1024],
                    start=True, stop=True), waits=[(P.vs, P.vs.n)], inc=peO)
                vr = P.op("vector", lambda e, rin=rin: e.reciprocal(
                    rin[:, 512:1024], pD[:, :]), waits=[(peO, prev_sum)])
                bc1 = P.op("tensor", lambda e, rin=rin: e.matmul(
                    pAB[:, 0:512], ones1f[:, :], rin[:, 0:512], start=True, stop=True),
                    waits=[(P.vs, vr), (P.ss, exp_ms[jmax])])
                bc = P.op("tensor", lambda e, rin=rin: e.matmul(
                    pAB[:, 512:1024], ones1f[:, :], rin[:, 512:1024],
                    start=True, stop=True), inc=peB)
                P.op("vector", lambda e, rinb=rinb: e.tensor_copy(rinb[:, :], pAB[:, :]),
                     waits=[(peB, bc)])
                ob = obuf[1]
                nm = P.op("vector", lambda e, ob=ob, rinb=rinb: e.tensor_mul(
                    ob[:, :], pOP[:, :], rinb[:, :]),
                    waits=[(peP, pv_ms[jmax]), (stv, stv.n), (P.ss, exp_ms[jmax])])
                prev_norm = nm
                prev_exp = exp_ms[jmax]
                prev_pv = pv_ms[jmax]
                P.op("sync", lambda e, hh=hh, ci=ci, ob=ob: e.dma_start(
                    out=o0_dst(hh, ci), in_=ob[:, :]),
                    waits=[(P.vs, nm)], inc=stv, inck=16)
        P.op("sync", lambda e: e.nop(), waits=[(stv, stv.n)])
    return env.finish()



def build_l3(ctx=None):
    env = Env(ctx, "p3_", {"pA": "A0", "pB": "A1", "pC": "O0", "pE": "C0",
                           "pD": "pD1", "pD2": "pD2v"})
    nc = env.nc
    xfm = env.dram("xfm", [D, ROWS], F32, "in")
    xn0lh = [env.dram("xn0loc0", [D, 512], BF16, "in"),
             env.dram("xn0loc1", [D, 512], BF16, "in")]
    maskr = env.dram("maskr", [1, ROWS], F32, "in")
    wo0p = env.dram("wo0p", [FT, 128, D], F32, "in")
    bo0c = env.dram("bo0c", [128, FT], F32, "in")
    w0ap = env.dram("w0ap", [32, 128, D], F32, "in")
    w1ap = env.dram("w1ap", [32, 128, D], F32, "in")
    w0bp = env.dram("w0bp", [FT, 128, FF0], F32, "in")
    w1bp = env.dram("w1bp", [FT, 128, FF0], F32, "in")
    b0ac = env.dram("b0ac", [128, 32], F32, "in")
    b1ac = env.dram("b1ac", [128, 32], F32, "in")
    b0bc = env.dram("b0bc", [128, FT], F32, "in")
    b1bc = env.dram("b1bc", [128, FT], F32, "in")
    x1fm = env.dram("x1fm", [D, ROWS], F32, "out")
    xn1ch = [env.dram("xn1c0", [D, 512], BF16, "out"),
             env.dram("xn1c1", [D, 512], BF16, "out")]
    def xn1_dst(tch):
        return xn1ch[tch].ap().rearrange("(t p) n -> p t n", p=128)
    if ctx is None:
        o0loc = env.dram("o0loc", [D, ROWS], BF16, "in")
        def o0_src(t0):
            return o0loc[:, t0:t0 + 512].rearrange("(t p) n -> p t n", p=128)
    else:
        o0_src = ctx["o0_src"]
    with env.scope():
        sb, ps = env.sb, env.ps
        res = sb("res", [128, FT * 512], F32)
        xn = sb("xn", [128, FT * 512], BF16)
        o0 = sb("o0", [128, FT * 512], BF16)
        h0 = sb("h0", [128, 32 * 512], BF16)
        h1 = sb("h1", [128, 32 * 512], BF16)
        wbufA = [sb(f"wbufA{i}", [128, D], BF16) for i in range(2)]
        wbufB0 = sb("wbufB0", [128, FF0], BF16)
        wbufB1 = sb("wbufB1", [128, FF0], BF16)
        wbufO = sb("wbufO", [128, D], BF16)
        b0a_s = sb("b0a_s", [128, 32], F32)
        b1a_s = sb("b1a_s", [128, 32], F32)
        b0b_s = sb("b0b_s", [128, FT], F32)
        b1b_s = sb("b1b_s", [128, FT], F32)
        bo0_s = sb("bo0_s", [128, FT], F32)
        mr_s = sb("mr_s", [1, ROWS], F32)
        mbc = sb("mbc", [128, 512], F32)
        s0 = sb("s0", [128, 512], F32)
        s1 = sb("s1", [128, 512], F32)
        s2 = sb("s2", [128, 512], F32)
        d_ = sb("d_", [128, 512], F32)
        e2 = sb("e2", [128, 512], F32)
        x1b = [sb(f"x1b{i}", [128, 512], F32) for i in range(2)]
        xn1b = xn
        bufs = {
            "ones128f": sb("ones128f", [128, 1], F32),
            "ones1f": sb("ones1f", [1, 128], F32),
            "lnsq": sb("lnsq", [128, 2 * 512], F32),
            "ln_mu": sb("ln_mu", [128, 512], F32),
            "ln_rstd": sb("ln_rstd", [128, 512], F32),
            "ln_r1": sb("ln_r1", [1, 512], F32),
            "ln_r2": sb("ln_r2", [1, 512], F32),
            "ln_r3": sb("ln_r3", [1, 512], F32),
            "ln_r4": sb("ln_r4", [1, 512], F32),
        }
        pA = ps("pA", [128, 512])
        pB = ps("pB", [128, 512])
        pC = ps("pC", [128, 512])
        pE = ps("pE", [128, 512])
        pD = ps("pD", [1, 512])
        pD2 = ps("pD2", [1, 512])
        bufs.update({"ln_pS": pD, "ln_pQ": pD2, "ln_pB1": pE, "ln_pB2": pA})

        P = env.P
        ld = env.sem("ld")
        wsa = [env.sem("wsa0"), env.sem("wsa1")]
        wso = env.sem("wso")
        wsb0 = env.sem("wsb0")
        wsb1 = env.sem("wsb1")
        peH = env.sem("peH")
        peM = env.sem("peM")
        peB3 = env.sem("peB3")
        pe = env.sem("pe")
        stx = env.sem("stx")
        stn = env.sem("stn")

        P.op("gpsimd", lambda e: e.memset(bufs["ones128f"][:, :], 1.0), ordered=False)
        g_const = P.op("gpsimd", lambda e: e.memset(bufs["ones1f"][:, :], 1.0))
        def sdma(dst, src):
            return P.op("sync", lambda e, d=dst, s=src: e.dma_start(out=d, in_=s),
                        inc=ld, inck=16, serial=False)
        sdma(b0a_s[:, :], b0ac[:, :])
        sdma(b1a_s[:, :], b1ac[:, :])
        sdma(b0b_s[:, :], b0bc[:, :])
        sdma(b1b_s[:, :], b1bc[:, :])
        sdma(bo0_s[:, :], bo0c[:, :])
        ld_const = sdma(mr_s[:, :], maskr[:, :])

        last_x1 = 0
        for tch in range(2):
            t0 = tch * 512
            lw = [(peH, peH.n), (peM, peM.n), (P.vs, last_x1)]
            P.op("sync", lambda e, t0=t0: e.dma_start(
                out=res.ap().rearrange("p (t n) -> p t n", n=512),
                in_=xfm[:, t0:t0 + 512].rearrange("(t p) n -> p t n", p=128)),
                waits=lw, inc=ld, inck=16, serial=False)
            ld_rx = P.op("sync", lambda e, tch=tch: e.dma_start(
                out=xn.ap().rearrange("p (t n) -> p t n", n=512),
                in_=xn0lh[tch].ap().rearrange("(t p) n -> p t n", p=128)),
                inc=ld, inck=16, serial=False)
            ld_chunk = P.op("sync", lambda e, t0=t0: e.dma_start(
                out=o0.ap().rearrange("p (t n) -> p t n", n=512),
                in_=o0_src(t0)),
                waits=([ctx["o0_join"]] if ctx is not None else []),
                inc=ld, inck=16, serial=False)
            hgrp = {}
            peH_start = peH.n
            for ex, (wpk, hbuf, bcol, actf) in enumerate(
                    ((w0ap, h0, b0a_s, AF.Gelu), (w1ap, h1, b1a_s, AF.Relu))):
                for mc in range(32):
                    gi = ex * 32 + mc
                    wl = P.op("gpsimd", lambda e, wpk=wpk, mc=mc, gi=gi: e.dma_start(
                        out=wbufA[gi % 2][:, :], in_=wpk[mc]),
                        waits=[(peH, hgrp.get(gi - 2, peH_start))],
                        inc=wsa[gi % 2], inck=16, ordered=False, serial=False)
                    bank = pB if gi % 2 else pA
                    for kt in range(FT):
                        w = [(wsa[gi % 2], wl), (ld, ld_rx)] if kt == 0 else []
                        if kt == 0:
                            w.append((P.ss, ev_h[gi - 2]) if gi >= 2
                                     else (P.ss, P.ss.n))
                        P.op("tensor", lambda e, gi=gi, kt=kt, bank=bank: e.matmul(
                            bank[:, :], wbufA[gi % 2][:, kt * 128:(kt + 1) * 128],
                            xn[:, kt * 512:(kt + 1) * 512],
                            start=(kt == 0), stop=(kt == FT - 1)),
                            waits=w, inc=peH if kt == FT - 1 else None)
                    hgrp[gi] = peH.n
                    if gi == 0:
                        ev_h = {}
                    ev_h[gi] = P.op("scalar", lambda e, mc=mc, hbuf=hbuf, bcol=bcol,
                                    actf=actf, bank=bank: e.activation(
                        hbuf[:, mc * 512:(mc + 1) * 512], bank[:, :], actf,
                        bias=bcol[:, mc:mc + 1]),
                        waits=[(peH, hgrp[gi]), (ld, ld_chunk)])
            ha_done = ev_h[63]
            bcv = P.op("tensor", lambda e, t0=t0: e.matmul(
                pE[:, :], bufs["ones1f"][:, :], mr_s[:, t0:t0 + 512],
                start=True, stop=True),
                waits=[(ld, ld_chunk), (P.gs, g_const), (P.vs, last_x1)], inc=peB3)
            mb = P.op("vector", lambda e: e.tensor_copy(mbc[:, :], pE[:, :]),
                      waits=[(peB3, bcv)])
            for mc in range(FT):
                wlo = P.op("gpsimd", lambda e, mc=mc: e.dma_start(
                    out=wbufO[:, :], in_=wo0p[mc]), waits=[(peM, peM.n)],
                    inc=wso, inck=16, ordered=False, serial=False)
                wl0 = P.op("gpsimd", lambda e, mc=mc: e.dma_start(
                    out=wbufB0[:, :], in_=w0bp[mc]), waits=[(peM, peM.n)],
                    inc=wsb0, inck=16, ordered=False, serial=False)
                wl1 = P.op("gpsimd", lambda e, mc=mc: e.dma_start(
                    out=wbufB1[:, :], in_=w1bp[mc]), waits=[(peM, peM.n)],
                    inc=wsb1, inck=16, ordered=False, serial=False)
                for kt in range(FT):
                    w = [(wso, wlo), (ld, ld_chunk)] if kt == 0 else []
                    if kt == 0:
                        w.append((P.ss, ev_bl.get("s2", ha_done) if mc else ha_done))
                    P.op("tensor", lambda e, kt=kt: e.matmul(
                        pC[:, :], wbufO[:, kt * 128:(kt + 1) * 128],
                        o0[:, kt * 512:(kt + 1) * 512],
                        start=(kt == 0), stop=(kt == FT - 1)),
                        waits=w, inc=peM if kt == FT - 1 else None)
                g_ao = peM.n
                for bank, wb, wlx, hbuf, evk, wsx in ((pA, wbufB0, wl0, h0, "s0", wsb0),
                                                       (pB, wbufB1, wl1, h1, "s1", wsb1)):
                    for ft in range(32):
                        w = [(wsx, wlx)] if ft == 0 else []
                        if ft == 0:
                            w.append((P.ss, ev_bl.get(evk, ha_done) if mc else ha_done))
                        P.op("tensor", lambda e, ft=ft, bank=bank, wb=wb, hbuf=hbuf: e.matmul(
                            bank[:, :], wb[:, ft * 128:(ft + 1) * 128],
                            hbuf[:, ft * 512:(ft + 1) * 512],
                            start=(ft == 0), stop=(ft == 31)),
                            waits=w, inc=peM if ft == 31 else None)
                g_out1 = peM.n
                if mc == 0:
                    ev_bl = {}
                ev_bl["s2"] = P.op("scalar", lambda e, mc=mc: e.activation(
                    s2[:, :], pC[:, :], AF.Identity, bias=bo0_s[:, mc:mc + 1]),
                    waits=[(peM, g_ao), (P.vs, last_x1)])
                ev_bl["s0"] = P.op("scalar", lambda e, mc=mc: e.activation(
                    s0[:, :], pA[:, :], AF.Identity, bias=b0b_s[:, mc:mc + 1]),
                    waits=[(peM, g_out1)])
                ev_bl["s1"] = P.op("scalar", lambda e, mc=mc: e.activation(
                    s1[:, :], pB[:, :], AF.Identity, bias=b1b_s[:, mc:mc + 1]))
                xb = x1b[mc % 2]
                P.op("vector", lambda e: e.tensor_sub(d_[:, :], s0[:, :], s1[:, :]),
                     waits=[(P.ss, ev_bl["s1"]), (P.vs, mb)])
                P.op("vector", lambda e: e.tensor_mul(e2[:, :], d_[:, :], mbc[:, :]))
                P.op("vector", lambda e: e.tensor_add(d_[:, :], s1[:, :], e2[:, :]))
                P.op("vector", lambda e: e.tensor_add(e2[:, :], d_[:, :], s2[:, :]))
                nx = P.op("vector", lambda e, mc=mc, xb=xb: e.tensor_add(
                    xb[:, :], e2[:, :], res[:, mc * 512:(mc + 1) * 512]),
                    waits=[(stx, stx.n)])
                last_x1 = nx
                P.op("sync", lambda e, mc=mc, t0=t0, xb=xb: e.dma_start(
                    out=x1fm[mc * 128:(mc + 1) * 128, t0:t0 + 512], in_=xb[:, :]),
                    waits=[(P.vs, nx)], inc=stx, inck=16)
        xn1_st = []
        xrl1 = h0.ap().bitcast(F32)
        ldrs = [env.sem("ldr0"), env.sem("ldr1")]
        st_mark = 0
        for tch in range(2):
            t0 = tch * 512
            src_buf = res.ap() if tch == 0 else xrl1
            lw = [(stx, stx.n)] + ([(pe, pe.n)] if tch else [])
            ldr = P.op("sync", lambda e, t0=t0, src_buf=src_buf: e.dma_start(
                out=src_buf.rearrange("p (t n) -> p t n", n=512),
                in_=x1fm[:, t0:t0 + 512].rearrange("(t p) n -> p t n", p=128)),
                waits=lw, inc=ldrs[tch], inck=16, serial=False)
            vsem, done = emit_ln_fm(
                P, bufs, pe, src_buf, xn1b, 512, after=[(ldrs[tch], ldr)],
                norm_after=([(stn, st_mark)] if tch else []))
            P.op("sync", lambda e, tch=tch: e.dma_start(
                out=xn1_dst(tch),
                in_=xn1b.ap().rearrange("p (t n) -> p t n", n=512)),
                waits=[(vsem, done)], inc=stn, inck=16)
            st_mark = stn.n
            xn1_st.append((stn, stn.n))
        if ctx is not None:
            ctx["xn1_st"] = xn1_st
        P.op("sync", lambda e: e.nop(), waits=[(stn, stn.n), (stx, stx.n)])
    return env.finish()



def build_l4(ctx=None):
    env = Env(ctx, "p4_", {"pA0": "A0", "pA1": "A1", "pC0": "C0", "pC1": "C1",
                           "pO0": "O0", "pO1": "O1", "pSc": "pDf", "ptr": "ptr"})
    nc = env.nc
    wq1p = env.dram("wq1p", [2, 128, D], F32, "in")
    wk1p = env.dram("wk1p", [2, 128, D], F32, "in")
    wv1 = env.dram("wv1", [D, 256], F32, "in")
    ldc = env.dram("ldc", [1, 2], F32, "in")
    wg1p = env.dram("wg1p", [1, 128, D], F32, "in")
    wg2c = env.dram("wg2c", [128, 256], BF16, "in")
    identb = env.dram("identb", [128, 128], BF16, "in")
    if ctx is None:
        xn1full = env.dram("xn1full", [D, S], BF16, "in")
        o1c = env.dram("o1c", [S, 256], BF16, "out")
        def xn_src(tc):
            return xn1full[:, tc * 512:(tc + 1) * 512].rearrange(
                "(t p) n -> p t n", p=128)
        def o1_dst(t):
            return o1c[t * 128:(t + 1) * 128, :]
    else:
        xn_src = ctx["xn1_src"]
        o1_dst = ctx["o1_dst"]
    with env.scope():
        sb, ps = env.sb, env.ps
        xnb = [sb(f"xnb{i}", [128, FT * 512], BF16) for i in range(2)]
        q1f = sb("q1f", [128, 2 * S], BF16)
        k1f = sb("k1f", [128, 2 * S], BF16)
        v1t = sb("v1t", [128, NKT * 256], BF16)
        sgr = sb("sgr", [128, 8 * 256], F32)
        g1tmp = sb("g1tmp", [128, 2 * 512], BF16)
        sigt = [sb(f"sigt{i}", [128, 512], F32) for i in range(2)]
        pre4 = ctx.get("pre4") if ctx is not None else None
        wq_s = sb("wq_s", [128, 2 * D], BF16)
        wk_s = sb("wk_s", [128, 2 * D], BF16)
        wv_s = sb("wv_s", [128, FT * 256], BF16)
        if pre4 is not None:
            wg1_s, wg2_s, ld_s, id_s = pre4[0]
        else:
            wg1_s = sb("wg1_s", [128, D], BF16)
            wg2_s = sb("wg2_s", [128, 256], BF16)
            ld_s = sb("ld_s", [1, 2], F32)
            id_s = sb("id_s", [128, 128], BF16)
        ldcol = sb("ldcol", [128, 2], F32)
        qdcol = sb("qdcol", [128, 2], F32)
        kdcol = sb("kdcol", [128, 2], F32)
        cdcol = sb("cdcol", [128, 2], F32)
        iot = sb("iot", [128, 128], F32)
        iotm = sb("iotm", [128, 128], F32)
        dts = [sb(f"dts{i}", [128, 128], BF16) for i in range(2)]
        S_sb = [[sb(f"Ssb{h}_{i}", [128, 128], F32) for i in range(2)] for h in range(2)]
        S_bf = [sb(f"Sbf{h}", [128, 128], BF16) for h in range(2)]
        ktb = [sb(f"ktb{h}", [128, 128], BF16) for h in range(2)]
        scb = [sb(f"scb{h}", [128, 128], BF16) for h in range(2)]
        ob = [sb(f"ob{h}", [128, 128], F32) for h in range(2)]
        tib = sb("tib", [128, 128], F32)
        ogr = sb("ogr", [128, 8 * 128], F32)
        o2b = sb("o2b", [128, 128], BF16)
        msbr = sb("msbr", [128, 8], F32)
        msb2r = sb("msb2r", [128, 8], F32)
        lnb8 = sb("lnb8", [128, 8], F32)
        rstd8 = sb("rstd8", [128, 8], F32)
        onb = sb("onb", [128, 4 * 256], BF16)
        ones1f = sb("ones1f", [1, 128], F32)
        pA = [ps("pA0", [128, 512]), ps("pA1", [128, 512])]
        pC = [ps("pC0", [128, 512]), ps("pC1", [128, 512])]
        pO0 = ps("pO0", [128, 512])
        pO1 = ps("pO1", [128, 512])
        pSc = ps("pSc", [128, 512])
        ptr = ps("ptr", [128, 1024], BF16)

        P = env.P
        ldxs = [env.sem("ldx0"), env.sem("ldx1")]
        peQ = env.sem("peQ")
        peL = env.sem("peL")
        peB4 = env.sem("peB4")
        stvs = [env.sem(f"stv{i}") for i in range(4)]

        g_const = P.op("gpsimd", lambda e: e.memset(ones1f[:, :], 1.0))
        ld = env.sem("ld")
        def cdma(dst, src):
            return P.op("gpsimd", lambda e, d=dst, s=src: e.dma_start(
                out=d, in_=s), inc=ld, inck=16, ordered=False, serial=False)
        cdma(wq_s.ap().rearrange("p (m k) -> p m k", k=D),
             wq1p.ap().rearrange("m p k -> p m k"))
        cdma(wk_s.ap().rearrange("p (m k) -> p m k", k=D),
             wk1p.ap().rearrange("m p k -> p m k"))
        cdma(wv_s.ap().rearrange("p (t f) -> p t f", f=256),
             wv1.ap().rearrange("(t p) f -> p t f", p=128))
        if pre4 is not None:
            ldc_sem, ld_const = pre4[1], pre4[2]
        else:
            cdma(wg1_s[:, :], wg1p[0])
            cdma(wg2_s[:, :], wg2c[:, :])
            cdma(ld_s[:, :], ldc[:, :])
            ld_const = cdma(id_s[:, :], identb[:, :])
            ldc_sem = ld
        ld_big = ld.n

        g_iot = P.op("gpsimd", lambda e: e.iota(
            iot[:, :], pattern=[[1, 128]], base=0, channel_multiplier=-1,
            allow_small_or_imprecise_dtypes=True))
        dts_ms = {}
        s0init = {}
        for hh in range(2):
            bc = P.op("tensor", lambda e, hh=hh: e.matmul(
                pA[0][:, 0:1], ones1f[:, :], ld_s[:, hh:hh + 1], start=True, stop=True),
                waits=[(ldc_sem, ld_const), (P.gs, g_const)] +
                      ([(P.vs, ldc_ms)] if hh else []), inc=peB4)
            ldc_ms = P.op("vector", lambda e, hh=hh: e.tensor_copy(
                ldcol[:, hh:hh + 1], pA[0][:, 0:1]), waits=[(peB4, bc)])
            tm1 = P.op("vector", lambda e, hh=hh: e.tensor_scalar_mul(
                iotm[:, :], iot[:, :], ldcol[:, hh:hh + 1]),
                waits=[(P.gs, g_iot)] +
                      ([(P.gs, g_asel), (P.ss, sq3)] if hh else []))
            g_asel = P.op("gpsimd", lambda e: e.affine_select(
                iotm[:, :], iotm[:, :], pattern=[[1, 128]], compare_op=ALU.is_ge,
                fill=-100.0, base=0, channel_multiplier=-1), waits=[(P.vs, tm1)])
            dts_ms[hh] = P.op("scalar", lambda e, hh=hh: e.activation(
                dts[hh][:, :], iotm[:, :], AF.Exp), waits=[(P.gs, g_asel)])
            qi = P.op("gpsimd", lambda e, hh=hh: e.iota(
                iotm[:, 0:1], pattern=[[1, 1]], base=1, channel_multiplier=1,
                allow_small_or_imprecise_dtypes=True), waits=[(P.ss, dts_ms[hh])])
            t2 = P.op("vector", lambda e, hh=hh: e.tensor_mul(
                iotm[:, 1:2], iotm[:, 0:1], ldcol[:, hh:hh + 1]), waits=[(P.gs, qi)])
            sq1 = P.op("scalar", lambda e, hh=hh: e.activation(
                qdcol[:, hh:hh + 1], iotm[:, 1:2], AF.Exp), waits=[(P.vs, t2)])
            ki = P.op("gpsimd", lambda e, hh=hh: e.iota(
                iotm[:, 2:3], pattern=[[1, 1]], base=127, channel_multiplier=-1,
                allow_small_or_imprecise_dtypes=True))
            t3 = P.op("vector", lambda e, hh=hh: e.tensor_mul(
                iotm[:, 3:4], iotm[:, 2:3], ldcol[:, hh:hh + 1]), waits=[(P.gs, ki)])
            sq2 = P.op("scalar", lambda e, hh=hh: e.activation(
                kdcol[:, hh:hh + 1], iotm[:, 3:4], AF.Exp), waits=[(P.vs, t3)])
            t4 = P.op("vector", lambda e, hh=hh: e.tensor_scalar_mul(
                iotm[:, 4:5], ldcol[:, hh:hh + 1], 128.0))
            sq3 = P.op("scalar", lambda e, hh=hh: e.activation(
                cdcol[:, hh:hh + 1], iotm[:, 4:5], AF.Exp), waits=[(P.vs, t4)])
            s0init[hh] = P.op("gpsimd", lambda e, hh=hh: e.memset(S_sb[hh][0][:, :], 0.0))
        dec_done_s = P.ss.n
        dec_done_v = P.vs.n

        bankw = {"A0": [(P.vs, ldc_ms)]}
        sig_war = {}
        sgr_war = {}
        g1_war = {}
        sig_slot = [0]
        PB = {"A0": pA[0], "A1": pA[1], "C0": pC[0], "C1": pC[1],
              "O0": pO0, "O1": pO1, "Sc": pSc}

        def silu_evac(bkey, grp, dst, ncols):
            sl2 = sig_slot[0] % 2
            sig_slot[0] += 1
            s_ms = P.op("scalar", lambda e, bkey=bkey, sl2=sl2, ncols=ncols:
                        e.activation(sigt[sl2][:, 0:ncols], PB[bkey][:, 0:ncols],
                                     AF.Sigmoid),
                        waits=[(peQ, grp)] + sig_war.pop(sl2, []))
            m_ms = P.op("vector", lambda e, bkey=bkey, sl2=sl2, dst=dst, ncols=ncols:
                        e.tensor_mul(dst, sigt[sl2][:, 0:ncols],
                                     PB[bkey][:, 0:ncols]),
                        waits=[(P.ss, s_ms)])
            sig_war[sl2] = [(P.vs, m_ms)]
            bankw[bkey] = [(P.ss, s_ms), (P.vs, m_ms)]
            return m_ms

        ev_q = {}
        ev_k = {}
        ev_v = {}
        sg_ms = {}
        sup_ms = {}
        cast_ms = {}
        ointer_ms = {}
        ointra_ms = {}
        kv_ms = {}
        scm_ms = {}
        om_ms = {}
        og_ms = {}
        sqa_ms = {}
        ktb_ms = {}
        trp_ms = {}
        st_ms = {}
        qkv_pe = {}
        msn_prev = [None]
        lv_prev = None
        on_last_prev = None
        sqa_q = []

        def emit_trp(u):
            t, hh = u // 2, u % 2
            tcu = t // 4
            trp_ms[u] = P.op("tensor", lambda e, t=t, hh=hh: e.transpose(
                ptr[:, 0:128], k1f[:, hh * S + t * 128: hh * S + t * 128 + 128],
                id_s[:, :]),
                waits=[(P.vs, ev_k[(tcu, hh)]), (ldc_sem, ld_const)] +
                      bankw.pop("ptr", []), inc=peL)

        def emit_ktb(u):
            t, hh = u // 2, u % 2
            w = [(peL, trp_ms[u])]
            if (t - 1, hh) in kv_ms:
                w.append((peL, kv_ms[(t - 1, hh)]))
            if t == 0:
                w.append((P.ss, dec_done_s))
            ktb_ms[u] = P.op("vector", lambda e, hh=hh: e.tensor_scalar_mul(
                ktb[hh][:, :], ptr[:, 0:128], kdcol[:, hh:hh + 1]), waits=w)
            bankw["ptr"] = [(P.vs, ktb_ms[u])]

        def emit_sqa(u):
            w = [(P.gs, og_ms[u])]
            if msn_prev[0] is not None:
                w.append((P.gs, msn_prev[0]))
            sqa_ms[u] = P.op("scalar", lambda e, u=u: e.activation(
                o2b[:, :], ogr[:, (u % 8) * 128:(u % 8) * 128 + 128], AF.Square,
                accum_out=msbr[:, u % 8:u % 8 + 1]), waits=w)

        for tc in range(FT):
            sl_ = tc % 2
            ldw = [(peQ, qkv_pe.get(tc - 2, 0))]
            if ctx is not None and tc < 2:
                ldw.append(ctx["xn1_join"][tc % 2])
            ldx_v = P.op("sync", lambda e, tc=tc, sl_=sl_: e.dma_start(
                out=xnb[sl_].ap().rearrange("p (t n) -> p t n", n=512),
                in_=xn_src(tc)),
                waits=ldw, inc=ldxs[sl_], inck=16,
                serial=False)
            first = ([(P.ss, dec_done_s), (P.vs, dec_done_v), (ld, ld_big)]
                     if tc == 0 else [])

            def qk_group(bkey, wsb, hh):
                for kt in range(FT):
                    w = []
                    if kt == 0:
                        w = [(ldxs[sl_], ldx_v)] + bankw.pop(bkey, []) + first
                    P.op("tensor", lambda e, hh=hh, kt=kt, wsb=wsb, bkey=bkey,
                         sl_=sl_: e.matmul(
                        PB[bkey][:, :],
                        wsb[:, hh * D + kt * 128: hh * D + kt * 128 + 128],
                        xnb[sl_][:, kt * 512:(kt + 1) * 512],
                        start=(kt == 0), stop=(kt == FT - 1)),
                        waits=w, inc=peQ if kt == FT - 1 else None)
                return peQ.n

            def v_group(stt):
                for kt in range(FT):
                    w = []
                    if kt == 0:
                        w = [(ldxs[sl_], ldx_v)] + bankw.pop("O0", []) + first
                    P.op("tensor", lambda e, stt=stt, kt=kt, sl_=sl_: e.matmul(
                        pO0[:, 0:256],
                        xnb[sl_][:, kt * 512 + stt * 128: kt * 512 + stt * 128 + 128],
                        wv_s[:, kt * 256:(kt + 1) * 256],
                        start=(kt == 0), stop=(kt == FT - 1)),
                        waits=w, inc=peQ if kt == FT - 1 else None)
                return peQ.n

            g = qk_group("A0", wq_s, 0)
            ev_q[(tc, 0)] = silu_evac(
                "A0", g, q1f[:, 0 * S + tc * 512: 0 * S + tc * 512 + 512], 512)
            g = v_group(0)
            ev_v[tc * 4 + 0] = silu_evac(
                "O0", g, v1t[:, (tc * 4 + 0) * 256:(tc * 4 + 0) * 256 + 256], 256)
            g = qk_group("A1", wq_s, 1)
            ev_q[(tc, 1)] = silu_evac(
                "A1", g, q1f[:, 1 * S + tc * 512: 1 * S + tc * 512 + 512], 512)
            g = v_group(1)
            ev_v[tc * 4 + 1] = silu_evac(
                "O0", g, v1t[:, (tc * 4 + 1) * 256:(tc * 4 + 1) * 256 + 256], 256)
            g = qk_group("C0", wk_s, 0)
            ev_k[(tc, 0)] = silu_evac(
                "C0", g, k1f[:, 0 * S + tc * 512: 0 * S + tc * 512 + 512], 512)
            g = v_group(2)
            ev_v[tc * 4 + 2] = silu_evac(
                "O0", g, v1t[:, (tc * 4 + 2) * 256:(tc * 4 + 2) * 256 + 256], 256)
            g = qk_group("C1", wk_s, 1)
            ev_k[(tc, 1)] = silu_evac(
                "C1", g, k1f[:, 1 * S + tc * 512: 1 * S + tc * 512 + 512], 512)
            g = v_group(3)
            ev_v[tc * 4 + 3] = silu_evac(
                "O0", g, v1t[:, (tc * 4 + 3) * 256:(tc * 4 + 3) * 256 + 256], 256)
            for kt in range(FT):
                w = []
                if kt == 0:
                    w = [(ldxs[sl_], ldx_v)] + bankw.pop("C0", [])
                P.op("tensor", lambda e, kt=kt, sl_=sl_: e.matmul(
                    pC[0][:, :], wg1_s[:, kt * 128:(kt + 1) * 128],
                    xnb[sl_][:, kt * 512:(kt + 1) * 512],
                    start=(kt == 0), stop=(kt == FT - 1)),
                    waits=w, inc=peQ if kt == FT - 1 else None)
            g_g1 = peQ.n
            g1c_ms = P.op("scalar", lambda e, sl_=sl_: e.activation(
                g1tmp[:, sl_ * 512: sl_ * 512 + 512], pC[0][:, :], AF.Copy),
                waits=[(peQ, g_g1)] + g1_war.pop(sl_, []))
            bankw["C0"] = [(P.ss, g1c_ms)]
            qkv_pe[tc] = peQ.n

            emit_trp(8 * tc)
            emit_ktb(8 * tc)
            for t in range(tc * 4, tc * 4 + 4):
                st = t % 4
                P.op("tensor", lambda e, sl_=sl_, st=st: e.matmul(
                    pC[1][:, 0:256], g1tmp[:, sl_ * 512 + st * 128: sl_ * 512 + st * 128 + 128],
                    wg2_s[:, :], start=True, stop=True),
                    waits=[(P.ss, g1c_ms)] + bankw.pop("C1", []), inc=peQ)
                g_g2 = peQ.n
                g1_war.setdefault(sl_, []).append((peQ, g_g2))
                sg_ms[t] = P.op("scalar", lambda e, t=t: e.activation(
                    sgr[:, (t % 8) * 256:(t % 8) * 256 + 256], pC[1][:, 0:256],
                    AF.Sigmoid), waits=[(peQ, g_g2)] + sgr_war.pop(t % 8, []))
                bankw["C1"] = [(P.ss, sg_ms[t])]
                for hh in range(2):
                    cw = ([(P.vs, sup_ms[(t - 1, hh)])] if t else
                          [(P.gs, s0init[hh])])
                    if (t - 1, hh) in ointer_ms:
                        cw.append((peL, ointer_ms[(t - 1, hh)]))
                    cast_ms[(t, hh)] = P.op("scalar", lambda e, hh=hh, t=t:
                                            e.activation(
                        S_bf[hh][:, :], S_sb[hh][t % 2][:, :], AF.Copy), waits=cw)
                while sqa_q and sqa_q[0] < 2 * t - 1:
                    emit_sqa(sqa_q.pop(0))
                for hh in range(2):
                    u = 2 * t + hh
                    w = [(P.ss, cast_ms[(t, hh)]), (P.vs, ev_q[(tc, hh)])] + \
                        bankw.pop("A%d" % hh, [])
                    ointer_ms[(t, hh)] = P.op("tensor", lambda e, hh=hh, t=t:
                                              e.matmul(
                        pA[hh][:, 0:128], q1f[:, hh * S + t * 128: hh * S + t * 128 + 128],
                        S_bf[hh][:, :], start=True, stop=True), waits=w, inc=peL)
                    w = [(P.vs, ktb_ms[u]), (P.vs, ev_v[t])] + \
                        bankw.pop("O1", [])
                    kv_ms[(t, hh)] = P.op("tensor", lambda e, hh=hh, t=t: e.matmul(
                        pO1[:, 0:128], ktb[hh][:, :],
                        v1t[:, t * 256 + hh * 128: t * 256 + hh * 128 + 128],
                        start=True, stop=True), waits=w, inc=peL)
                    w = [(P.vs, ev_k[(tc, hh)]), (P.vs, ev_q[(tc, hh)])] + \
                        bankw.pop("Sc", [])
                    sc_v = P.op("tensor", lambda e, hh=hh, t=t: e.matmul(
                        pSc[:, 0:128], k1f[:, hh * S + t * 128: hh * S + t * 128 + 128],
                        q1f[:, hh * S + t * 128: hh * S + t * 128 + 128],
                        start=True, stop=True), waits=w, inc=peL)
                    w = [(peL, sc_v), (P.ss, dts_ms[hh])]
                    if (t - 1, hh) in ointra_ms:
                        w.append((peL, ointra_ms[(t - 1, hh)]))
                    scm_ms[u] = P.op("vector", lambda e, hh=hh: e.tensor_mul(
                        scb[hh][:, :], pSc[:, 0:128], dts[hh][:, :]), waits=w)
                    bankw["Sc"] = [(P.vs, scm_ms[u])]
                    w = [(P.vs, scm_ms[u]), (P.vs, ev_v[t])] + bankw.pop("Sc", [])
                    ointra_ms[(t, hh)] = P.op("tensor", lambda e, hh=hh, t=t:
                                              e.matmul(
                        pSc[:, 128:256], scb[hh][:, :],
                        v1t[:, t * 256 + hh * 128: t * 256 + hh * 128 + 128],
                        start=True, stop=True), waits=w, inc=peL)
                    if u + 1 < 8 * (tc + 1):
                        emit_trp(u + 1)
                        emit_ktb(u + 1)
                    w = [(peL, ointer_ms[(t, hh)])]
                    if t == 0:
                        w.append((P.ss, dec_done_s))
                    ti_ms = P.op("vector", lambda e, hh=hh: e.tensor_scalar_mul(
                        tib[:, :], pA[hh][:, 0:128], qdcol[:, hh:hh + 1]),
                        waits=w)
                    bankw["A%d" % hh] = [(P.vs, ti_ms)]
                    w = [(peL, ointra_ms[(t, hh)])]
                    if (t - 1, hh) in om_ms:
                        w.append((P.gs, og_ms[2 * (t - 1) + hh]))
                    om_ms[(t, hh)] = P.op("vector", lambda e, hh=hh:
                                          e.tensor_add(
                        ob[hh][:, :], tib[:, :], pSc[:, 128:256]), waits=w)
                    bankw["Sc"] = [(P.vs, om_ms[(t, hh)])]
                    w = [(peL, kv_ms[(t, hh)]), (P.ss, cast_ms[(t, hh)])]
                    if t == 0:
                        w.append((P.ss, dec_done_s))
                    sup_ms[(t, hh)] = P.op("vector", lambda e, hh=hh, t=t:
                                           e.scalar_tensor_tensor(
                        S_sb[hh][(t + 1) % 2][:, :], S_sb[hh][t % 2][:, :],
                        cdcol[:, hh:hh + 1], pO1[:, 0:128], ALU.mult, ALU.add),
                        waits=w)
                    bankw["O1"] = [(P.vs, sup_ms[(t, hh)])]
                    w = [(P.vs, om_ms[(t, hh)]), (P.ss, sg_ms[t])]
                    if u - 8 in sqa_ms:
                        w.append((P.ss, sqa_ms[u - 8]))
                    og_ms[u] = P.op("gpsimd", lambda e, hh=hh, t=t, u=u:
                                    e.tensor_mul(
                        ogr[:, (u % 8) * 128:(u % 8) * 128 + 128], ob[hh][:, :],
                        sgr[:, (t % 8) * 256 + hh * 128:(t % 8) * 256 + hh * 128 + 128]),
                        waits=w)
                    sqa_q.append(u)
                sgr_war[t % 8] = [(P.gs, og_ms[2 * t + 1])]

            while sqa_q:
                emit_sqa(sqa_q.pop(0))
            w = [(P.ss, sqa_ms[8 * tc + 7])]
            if lv_prev is not None:
                w.append((P.ss, lv_prev))
            msn_prev[0] = P.op("gpsimd", lambda e: e.tensor_scalar(
                msb2r[:, :], msbr[:, :], 1.0 / HD, RMS_EPS, ALU.mult, ALU.add),
                waits=w)
            lv_prev = P.op("scalar", lambda e: e.activation(
                lnb8[:, :], msb2r[:, :], AF.Ln), waits=[(P.gs, msn_prev[0])])
            w = []
            if on_last_prev is not None:
                w.append((P.gs, on_last_prev))
            rst_ms = P.op("scalar", lambda e: e.activation(
                rstd8[:, :], lnb8[:, :], AF.Exp, scale=-0.5), waits=w)
            for t in range(tc * 4, tc * 4 + 4):
                for hh in range(2):
                    u = 2 * t + hh
                    w = [(P.ss, rst_ms)]
                    if t - 4 in st_ms:
                        w.append((stvs[t % 4], st_ms[t - 4]))
                    on_last_prev = P.op("gpsimd", lambda e, t=t, hh=hh, u=u:
                                        e.tensor_scalar_mul(
                        onb[:, (t % 4) * 256 + hh * 128:(t % 4) * 256 + hh * 128 + 128],
                        ogr[:, (u % 8) * 128:(u % 8) * 128 + 128],
                        rstd8[:, u % 8:u % 8 + 1]), waits=w)
                P.op("gpsimd", lambda e, t=t: e.dma_start(
                    out=o1_dst(t),
                    in_=onb[:, (t % 4) * 256:(t % 4) * 256 + 256]),
                    waits=[(P.gs, on_last_prev)],
                    inc=stvs[t % 4], inck=16, ordered=False, serial=False)
                st_ms[t] = stvs[t % 4].n
        P.op("sync", lambda e: e.nop(),
             waits=[(sv, sv.n) for sv in stvs if sv.n > 0])
    return env.finish()



QN = FF1 * D
QSH = QN // NCORES // 128


def build_l5(ctx=None):
    env = Env(ctx, "p5_", {"pD": "O0", "pE": "A1"})
    nc = env.nc
    wins = [env.dram(f"w{i}s", [128, QSH], F32, "in") for i in (1, 2, 3)]
    wouts = [env.dram(f"w{i}q", [128, QSH], BF16, "out") for i in (1, 2, 3)]
    arin = env.dram("arin", [1, 4], F32, "int")
    arout = env.dram("arout", [1, 4], F32, "int")
    with env.scope():
        sb, ps = env.sb, env.ps
        pre5 = ctx.get("pre5") if ctx is not None else None
        if pre5 is not None:
            wbuf = [pre5[0][0], pre5[0][1], sb("wbuf2", [128, QSH], F32)]
        else:
            wbuf = [sb(f"wbuf{i}", [128, QSH], F32) for i in range(3)]
        qtmp = sb("qtmp", [128, QSH], F32)
        absb = qtmp.ap().rearrange("p n -> p n") and qtmp
        qout = sb("qout", [128, QSH], BF16)
        asum = sb("asum", [128, 1], F32)
        gsum = sb("gsum", [1, 4], F32)
        gtot = sb("gtot", [1, 4], F32)
        rg = sb("rg", [1, 4], F32)
        rgc = sb("rgc", [128, 4], F32)
        ones128f = sb("ones128f", [128, 1], F32)
        ones1f = sb("ones1f", [1, 128], F32)
        pD = ps("pD", [128, 512])[0:1, :]
        pE = ps("pE", [128, 512])

        P = env.P
        ld = env.sem("ld")
        pe = env.sem("pe")
        ccs = env.sem("ccs")
        stv = env.sem("stv")
        ard = env.sem("ard")

        guard = ([(sm, sm.n) for sm in ctx["all"] if sm.n > 0] +
                 [(P.ss, P.ss.n), (P.vs, P.vs.n)]) if ctx is not None else []
        P.op("gpsimd", lambda e: e.memset(ones128f[:, :], 1.0), waits=guard)
        g_const = P.op("gpsimd", lambda e: e.memset(ones1f[:, :], 1.0))
        if pre5 is not None:
            lv3 = P.op("sync", lambda e: e.dma_start(
                out=wbuf[2][:, :], in_=wins[2][:, :]), inc=ld, inck=16,
                serial=False)
            ldw = [[(pre5[1], pre5[2])], [(pre5[1], pre5[2])], [(ld, lv3)]]
        else:
            lvs = []
            for i in range(3):
                lvs.append(P.op("sync", lambda e, i=i: e.dma_start(
                    out=wbuf[i][:, :], in_=wins[i][:, :]), inc=ld, inck=16,
                    serial=False))
            ldw = [[(ld, lvs[2])]] * 3
        prev_pe = 0
        for i in range(3):
            sa = P.op("scalar", lambda e, i=i: e.activation(
                absb[:, :], wbuf[i][:, :], AF.Abs, accum_out=asum[:, :]),
                waits=ldw[i] + [(pe, prev_pe)] + (guard if i == 0 else []))
            prev_pe = P.op("tensor", lambda e: e.matmul(
                pD[:, 0:1], asum[:, :], ones128f[:, :], start=True, stop=True),
                waits=[(P.ss, sa), (P.vs, P.vs.n), (P.gs, g_const)], inc=pe)
            P.op("vector", lambda e, i=i: e.tensor_copy(gsum[:, i:i + 1], pD[:, 0:1]),
                 waits=[(pe, prev_pe)])
        z = P.op("vector", lambda e: e.tensor_scalar_mul(gsum[:, 3:4], gsum[:, 0:1], 0.0))
        wr = P.op("gpsimd", lambda e: e.dma_start(out=arin[:, :], in_=gsum[:, :]),
                  waits=[(P.vs, z)], inc=ard, inck=16, ordered=False)
        cc = P.op("gpsimd", lambda e: e.collective_compute(
            "AllReduce", ALU.add, replica_groups=[list(range(NCORES))],
            ins=[arin.ap().opt()], outs=[arout.ap().opt()]),
            waits=[(ard, wr)], inc=ccs, inck=1, ordered=False)
        rd = P.op("gpsimd", lambda e: e.dma_start(out=gtot[:, :], in_=arout[:, :]),
                  waits=[(ccs, cc)], inc=ard, inck=16, ordered=False)
        P.op("vector", lambda e: e.tensor_scalar(
            rg[:, :], gtot[:, :], 1.0 / QN, Q_EPS, ALU.mult, ALU.add),
            waits=[(ard, rd)])
        vrg = P.op("vector", lambda e: e.reciprocal(rg[:, :], rg[:, :]))
        bc = P.op("tensor", lambda e: e.matmul(
            pE[:, 0:4], ones1f[:, :], rg[:, :], start=True, stop=True),
            waits=[(P.vs, vrg), (P.gs, g_const)], inc=pe)
        vb = P.op("vector", lambda e: e.tensor_copy(rgc[:, :], pE[:, 0:4]),
                  waits=[(pe, bc)])
        prev_st = 0
        for i in range(3):
            P.op("vector", lambda e, i=i: e.tensor_scalar(
                qtmp[:, :], wbuf[i][:, :], rgc[:, i:i + 1], MAGIC, ALU.mult, ALU.add),
                waits=[(P.vs, vb), (stv, prev_st)])
            P.op("vector", lambda e, i=i: e.tensor_scalar(
                wbuf[i][:, :], qtmp[:, :], -MAGIC, 1.0, ALU.add, ALU.min))
            qv = P.op("vector", lambda e, i=i: e.tensor_scalar_max(
                qout[:, :], wbuf[i][:, :], -1.0))
            prev_st = P.op("sync", lambda e, i=i: e.dma_start(
                out=wouts[i][:, :], in_=qout[:, :]), waits=[(P.vs, qv)],
                inc=stv, inck=16)
            if ctx is not None:
                ctx.setdefault("w5_st", []).append((stv, prev_st))
        P.op("sync", lambda e: e.nop(), waits=[(stv, prev_st)])
    return env.finish()



def build_l6(ctx=None):
    env = Env(ctx, "p6_", {"pA": "A0", "pB": "A1", "pC": "C0", "pD": "C1",
                           "pE": "O0"})
    nc = env.nc
    x1fm = env.dram("x1fm", [D, ROWS], F32, "in")
    xn1lh = [env.dram("xn1loc0", [D, 512], BF16, "in"),
             env.dram("xn1loc1", [D, 512], BF16, "in")]
    wo1p = env.dram("wo1p", [FT, 128, D], F32, "in")
    nwc = env.dram("nwc", [128, FT], F32, "in")
    outc = env.dram("outc", [D, ROWS], F32, "out")
    if ctx is None:
        o1loc = env.dram("o1loc", [ROWS, D], BF16, "in")
        w1qp = env.dram("w1qp", [44, 128, D], BF16, "in")
        w2qp = env.dram("w2qp", [44, 128, D], BF16, "in")
        w3qp = env.dram("w3qp", [FT, 128, FF1], BF16, "in")
        def o1_src(ft, t0):
            return o1loc[t0:t0 + 512, ft * 128:(ft + 1) * 128]
        def w12_blk(i, fc):
            return (w1qp if i == 0 else w2qp)[fc]
        def w3_blk(mc):
            return w3qp[mc]
    else:
        o1_src = ctx["o1_src"]
        w12_blk = ctx["w12_blk"]
        w3_blk = ctx["w3_blk"]
    with env.scope():
        sb, ps = env.sb, env.ps
        res = sb("res", [128, FT * 512], F32)
        xn = sb("xn", [128, FT * 512], BF16)
        o1b = sb("o1b", [128, FT * 512], BF16)
        hid = sb("hid", [128, 44 * 512], BF16)
        w12b = [sb(f"w12b{i}", [128, 2 * D], BF16) for i in range(2)]
        w3b = [sb(f"w3b{i}", [128, FF1], BF16) for i in range(2)]
        wofb = sb("wofb", [128, D], F32)
        wob = [sb(f"wob{i}", [128, D], BF16) for i in range(2)]
        nw_s = sb("nw_s", [128, FT], F32)
        s1b = sb("s1b", [128, 512], BF16)
        s2b = sb("s2b", [128, 512], BF16)
        e1b = sb("e1b", [128, 512], F32)
        e2b = sb("e2b", [128, 512], F32)
        t1b = sb("t1b", [128, 512], F32)
        outb = [sb(f"outb{i}", [128, 512], F32) for i in range(2)]
        pA = ps("pA", [128, 512])
        pB = ps("pB", [128, 512])
        pC = ps("pC", [128, 512])
        pD = ps("pD", [128, 512])
        pE = ps("pE", [128, 512])

        P = env.P
        ld = env.sem("ld")
        o1s = env.sem("o1s")
        w12s = [env.sem("w12s0"), env.sem("w12s1")]
        w3s = [env.sem("w3s0"), env.sem("w3s1")]
        wso = env.sem("wso")
        peF = env.sem("peF")
        stv = env.sem("stv")

        lw_nw = P.op("sync", lambda e: e.dma_start(out=nw_s[:, :], in_=nwc[:, :]),
                     inc=ld, inck=16, serial=False)
        last_blend = 0
        for tch in range(2):
            t0 = tch * 512
            P.op("sync", lambda e, t0=t0: e.dma_start(
                out=res.ap().rearrange("p (t n) -> p t n", n=512),
                in_=x1fm[:, t0:t0 + 512].rearrange("(t p) n -> p t n", p=128)),
                waits=[(peF, peF.n), (P.vs, last_blend), (P.ss, P.ss.n)],
                inc=ld, inck=16, serial=False)
            ld_rx = P.op("sync", lambda e, tch=tch: e.dma_start(
                out=xn.ap().rearrange("p (t n) -> p t n", n=512),
                in_=xn1lh[tch].ap().rearrange("(t p) n -> p t n", p=128)),
                inc=ld, inck=16, serial=False)
            for ft in range(FT):
                P.op("sync", lambda e, t0=t0, ft=ft: e.dma_start(
                    out=o1b[:, ft * 512:(ft + 1) * 512],
                    in_=o1_src(ft, t0), transpose=True),
                    waits=([ctx["o1_join"]] if ctx is not None and ft == 0 else []),
                    inc=ld, inck=16, serial=False)
            ld_chunk = ld.n
            mm_grp = {}
            peF_start = peF.n
            ev1 = {}
            ev2 = {}
            hv_ms = {}
            for fc in range(44):
                wla = P.op("gpsimd", lambda e, fc=fc: e.dma_start(
                    out=w12b[fc % 2][:, 0:D], in_=w12_blk(0, fc)),
                    waits=[(peF, mm_grp.get(fc - 2, peF_start))],
                    inc=w12s[fc % 2], inck=16, ordered=False, serial=False)
                wlb = P.op("gpsimd", lambda e, fc=fc: e.dma_start(
                    out=w12b[fc % 2][:, D:2 * D], in_=w12_blk(1, fc)),
                    inc=w12s[fc % 2], inck=16, ordered=False, serial=False)
                bs = pA if fc % 2 == 0 else pC
                bt = pB if fc % 2 == 0 else pD
                for kt in range(FT):
                    w = []
                    if kt == 0:
                        w = [(w12s[fc % 2], wlb), (ld, ld_rx),
                             (P.ss, ev1.get(fc - 2, P.ss.n if fc < 2 else 0))]
                    P.op("tensor", lambda e, fc=fc, kt=kt, bs=bs: e.matmul(
                        bs[:, :], w12b[fc % 2][:, kt * 128:(kt + 1) * 128],
                        xn[:, kt * 512:(kt + 1) * 512],
                        start=(kt == 0), stop=(kt == FT - 1)), waits=w)
                for kt in range(FT):
                    w = [(P.ss, ev2.get(fc - 2, P.ss.n if fc < 2 else 0))] if kt == 0 else []
                    P.op("tensor", lambda e, fc=fc, kt=kt, bt=bt: e.matmul(
                        bt[:, :], w12b[fc % 2][:, D + kt * 128: D + kt * 128 + 128],
                        xn[:, kt * 512:(kt + 1) * 512],
                        start=(kt == 0), stop=(kt == FT - 1)),
                        waits=w, inc=peF if kt == FT - 1 else None)
                mm_grp[fc] = peF.n
                ev1[fc] = P.op("scalar", lambda e, bs=bs: e.activation(
                    s1b[:, :], bs[:, :], AF.Sigmoid),
                    waits=[(peF, mm_grp[fc])] +
                          ([(P.vs, hv_ms[fc - 1])] if fc else [(P.vs, last_blend)]))
                ev2[fc] = P.op("scalar", lambda e, bt=bt: e.activation(
                    s2b[:, :], bt[:, :], AF.Copy))
                hv_ms[fc] = P.op("vector", lambda e, fc=fc: e.tensor_mul(
                    hid[:, fc * 512:(fc + 1) * 512], s1b[:, :], s2b[:, :]),
                    waits=[(P.ss, ev2[fc])])
            hid_done = hv_ms[43]
            ph1_sc = ev2[43]
            last_wosc = 0
            last_aoev = ph1_sc
            f_ev = {}
            f_grp = {}
            peF_start2 = peF.n
            for mc in range(FT):
                wlo = P.op("gpsimd", lambda e, mc=mc: e.dma_start(
                    out=wofb[:, :], in_=wo1p[mc]),
                    waits=[(P.vs, last_wosc)],
                    inc=wso, inck=16, ordered=False, serial=False)
                for kt in range(FT):
                    w = [(wso, wlo), (ld, ld_chunk)] if kt == 0 else []
                    if kt == 0 and not (tch == 0 and mc < 2):
                        w.append((peF, f_grp.get(mc - 2, 0) if mc >= 2 else peF.n))
                    P.op("vector", lambda e, kt=kt, mc=mc: e.tensor_scalar_mul(
                        wob[mc % 2][:, kt * 128:(kt + 1) * 128],
                        wofb[:, kt * 128:(kt + 1) * 128], nw_s[:, kt:kt + 1]),
                        waits=w)
                last_wosc = P.vs.n
                wl3 = P.op("gpsimd", lambda e, mc=mc: e.dma_start(
                    out=w3b[mc % 2][:, :], in_=w3_blk(mc)),
                    waits=[(peF, f_grp.get(mc - 2, peF_start2))],
                    inc=w3s[mc % 2], inck=16, ordered=False, serial=False)
                for kt in range(FT):
                    w = []
                    if kt == 0:
                        w = [(P.vs, last_wosc), (ld, ld_chunk), (P.ss, last_aoev)]
                    P.op("tensor", lambda e, kt=kt, mc=mc: e.matmul(
                        pE[:, :], wob[mc % 2][:, kt * 128:(kt + 1) * 128],
                        o1b[:, kt * 512:(kt + 1) * 512],
                        start=(kt == 0), stop=(kt == FT - 1)),
                        waits=w, inc=peF if kt == FT - 1 else None)
                g_ao = peF.n
                bf_ = pA if mc % 2 == 0 else pC
                for fc in range(44):
                    w = []
                    if fc == 0:
                        w = [(w3s[mc % 2], wl3), (P.vs, hid_done)]
                        w.append((P.ss, f_ev.get(mc - 2, ph1_sc)))
                    P.op("tensor", lambda e, fc=fc, mc=mc, bf_=bf_: e.matmul(
                        bf_[:, :], w3b[mc % 2][:, fc * 128:(fc + 1) * 128],
                        hid[:, fc * 512:(fc + 1) * 512],
                        start=(fc == 0), stop=(fc == 43)),
                        waits=w, inc=peF if fc == 43 else None)
                f_grp[mc] = peF.n
                last_aoev = P.op("scalar", lambda e, mc=mc: e.activation(
                    e2b[:, :], pE[:, :], AF.Copy),
                    waits=[(peF, g_ao)] + ([(P.vs, last_blend)] if (tch or mc) else []))
                f_ev[mc] = P.op("scalar", lambda e, mc=mc, bf_=bf_: e.activation(
                    e1b[:, :], bf_[:, :], AF.Copy), waits=[(peF, f_grp[mc])])
                ob = outb[mc % 2]
                P.op("vector", lambda e, mc=mc: e.tensor_add(
                    t1b[:, :], res[:, mc * 512:(mc + 1) * 512], e2b[:, :]),
                    waits=[(P.ss, f_ev[mc])])
                nx = P.op("vector", lambda e, ob=ob: e.tensor_add(
                    ob[:, :], t1b[:, :], e1b[:, :]), waits=[(stv, stv.n)])
                last_blend = nx
                P.op("sync", lambda e, mc=mc, t0=t0, ob=ob: e.dma_start(
                    out=outc[mc * 128:(mc + 1) * 128, t0:t0 + 512], in_=ob[:, :]),
                    waits=[(P.vs, nx)], inc=stv, inck=16)
        P.op("sync", lambda e: e.nop(), waits=[(stv, stv.n)])
    return env.finish()



def build_fused():
    nc = bass.Bass("TRN2", target_bir_lowering=False)
    ext = {}

    def din(name, shape, dt):
        ext[name] = nc.dram_tensor(name, shape, dt, kind="ExternalInput")

    def dint(name, shape, dt):
        ext[name] = nc.dram_tensor(name, shape, dt, kind="Internal")

    def dout(name, shape, dt):
        ext[name] = nc.dram_tensor(name, shape, dt, kind="ExternalOutput")

    din("p1_xfm", [D, ROWS], F32)
    din("p1_wgt", [FT, 128, 2], F32)
    dint("p1_xn0c0", [D, 512], BF16)
    dint("p1_xn0c1", [D, 512], BF16)
    dint("p1_maskc", [1, ROWS], F32)
    for nm, shp, dt in (("wqp", [2, 128, D], F32), ("wkp", [2, 128, D], F32),
                        ("wv", [D, 256], F32), ("bq", [128, 2], F32),
                        ("bk", [128, 2], F32), ("bv", [1, 256], F32),
                        ("sjtab", [2, 128, NKT], F32), ("mshtab", [1, 16], F32),
                        ("negtab", [1, S], BF16),
                        ("diagmask", [8, 128, 1024], BF16)):
        din("p2_" + nm, shp, dt)
    dint("xn0ag0", [NCORES * D, 512], BF16)
    dint("xn0ag1", [NCORES * D, 512], BF16)
    dint("o0li", [D, ROWS], BF16)
    dint("o0rc", [D, ROWS], BF16)
    ext["p3_xfm"] = ext["p1_xfm"]
    ext["p3_xn0loc0"] = ext["p1_xn0c0"]
    ext["p3_xn0loc1"] = ext["p1_xn0c1"]
    ext["p3_maskr"] = ext["p1_maskc"]
    for nm, shp in (("wo0p", [FT, 128, D]), ("bo0c", [128, FT]),
                    ("w0ap", [32, 128, D]), ("w1ap", [32, 128, D]),
                    ("w0bp", [FT, 128, FF0]), ("w1bp", [FT, 128, FF0]),
                    ("b0ac", [128, 32]), ("b1ac", [128, 32]),
                    ("b0bc", [128, FT]), ("b1bc", [128, FT])):
        din("p3_" + nm, shp, F32)
    dint("p3_x1fm", [D, ROWS], F32)
    dint("p3_xn1c0", [D, 512], BF16)
    dint("p3_xn1c1", [D, 512], BF16)
    dint("xn1ag0", [NCORES * D, 512], BF16)
    dint("xn1ag1", [NCORES * D, 512], BF16)
    for nm, shp, dt in (("wq1p", [2, 128, D], F32), ("wk1p", [2, 128, D], F32),
                        ("wv1", [D, 256], F32), ("ldc", [1, 2], F32),
                        ("wg1p", [1, 128, D], F32), ("wg2c", [128, 256], BF16),
                        ("identb", [128, 128], BF16)):
        din("p4_" + nm, shp, dt)
    dint("o1li", [S, 256], BF16)
    dint("o1rc", [S, 256], BF16)
    for i in (1, 2, 3):
        din(f"p5_w{i}s", [128, QSH], F32)
        dint(f"p5_w{i}q", [128, QSH], BF16)
        dint(f"w{i}qa", [NCORES * 128, QSH], BF16)
    dint("p5_arin", [1, 4], F32)
    dint("p5_arout", [1, 4], F32)
    ext["p6_x1fm"] = ext["p3_x1fm"]
    ext["p6_xn1loc0"] = ext["p3_xn1c0"]
    ext["p6_xn1loc1"] = ext["p3_xn1c1"]
    din("p6_wo1p", [FT, 128, D], F32)
    din("p6_nwc", [128, FT], F32)
    dout("p6_outc", [D, ROWS], F32)

    o0li, o0rc = ext["o0li"], ext["o0rc"]
    o1li, o1rc = ext["o1li"], ext["o1rc"]
    xn0agh = [ext["xn0ag0"], ext["xn0ag1"]]
    xn1agh = [ext["xn1ag0"], ext["xn1ag1"]]

    def xn0_src(tc):
        h, r = tc % 2, tc // 2
        return xn0agh[h][r * D:(r + 1) * D, :].rearrange(
            "(t p) n -> p t n", p=128)

    def xn1_src(tc):
        h, r = tc % 2, tc // 2
        return xn1agh[h][r * D:(r + 1) * D, :].rearrange(
            "(t p) n -> p t n", p=128)

    def o0_dst(hh, ci):
        rb = ci * 256 + hh * 128
        return o0li[rb:rb + 128, :]

    def o0_src(t0):
        return o0rc[:, t0:t0 + 512].rearrange("(t p) n -> p t n", p=128)

    def o1_dst(t):
        rb = (t // 8) * ROWS + (t % 8) * 128
        return o1li[rb:rb + 128, :]

    def o1_src(ft, t0):
        rb = (ft % 8) * ROWS + t0
        cb = (ft // 8) * 128
        return o1rc[rb:rb + 512, cb:cb + 128]

    def w12_blk(i, fc):
        h = ext["w1qa"] if i == 0 else ext["w2qa"]
        return bass.AP(h, fc * 128 * D, [[D, 128], [1, D]])

    def w3_blk(mc):
        return bass.AP(ext["w3qa"], mc * 128 * FF1, [[FF1, 128], [1, FF1]])

    with ExitStack() as gst:
        pss = mk_pool(nc, gst)
        P = Prog(nc, gst)
        ctx = {"nc": nc, "P": P, "gst": gst, "pss": pss, "ext": ext, "all": [],
               "xn0_src": xn0_src, "o0_dst": o0_dst, "o0_src": o0_src,
               "xn1_src": xn1_src, "o1_dst": o1_dst, "o1_src": o1_src,
               "w12_blk": w12_blk, "w3_blk": w3_blk}

        def bar():
            barrier(P, ctx["all"] + [P.vs, P.ss, P.gs])

        def cc(kind, op, ins, outs, waits=()):
            cs = Sem(gst, nc, f"cc{len(ctx.setdefault('_cc', []))}")
            ctx["_cc"].append(cs)
            P.op("gpsimd", lambda e: e.collective_compute(
                kind, op, replica_groups=[list(range(NCORES))],
                ins=[ins.ap().opt()], outs=[outs.ap().opt()]),
                waits=waits, inc=cs, inck=1, ordered=False)
            return cs

        def ccjoin(cs, engs=None):
            for eng in (engs or Prog.ENGS):
                P.op(eng, lambda e, cs=cs: e.nop(), waits=[(cs, 1)], ordered=False)

        pre5_stack = gst.enter_context(ExitStack())
        w5t = [pre5_stack.enter_context(
            nc.sbuf_tensor(f"pre5_w{i}", [128, QSH], F32)) for i in range(2)]
        ld5 = Sem(gst, nc, "pre5_ld")
        ctx["all"].append(ld5)
        for i in range(2):
            P.op("sync", lambda e, i=i: e.dma_start(
                out=w5t[i][:, :], in_=ext[f"p5_w{i + 1}s"][:, :]),
                inc=ld5, inck=16, ordered=False, serial=False)
        ctx["pre5"] = (w5t, ld5, ld5.n)
        build_l1(ctx)
        cc0a = cc("AllGather", ALU.bypass, ext["p1_xn0c0"], xn0agh[0],
                  waits=[ctx["xn0_st"][0]])
        build_l5(ctx)
        cc0b = cc("AllGather", ALU.bypass, ext["p1_xn0c1"], xn0agh[1],
                  waits=[ctx["xn0_st"][1]])
        ctx["xn0_join"] = [(cc0a, 1), (cc0b, 1)]
        pre5_stack.close()
        bar()
        ccw = []

        def post_warm(l2_warm):
            for i in (1, 2, 3):
                ccw.append(cc("AllGather", ALU.bypass, ext[f"p5_w{i}q"],
                              ext[f"w{i}qa"],
                              waits=[ctx["w5_st"][i - 1]] + l2_warm))
        ctx["post_warm"] = post_warm
        build_l2(ctx)
        bar()
        cc5 = cc("AllToAll", ALU.bypass, o0li, o0rc)
        ctx["o0_join"] = (cc5, 1)
        pre4_stack = gst.enter_context(ExitStack())
        p4t = {}
        for nm, shp, dt in (("wg1_s", [128, D], BF16),
                            ("wg2_s", [128, 256], BF16),
                            ("ld_s", [1, 2], F32),
                            ("id_s", [128, 128], BF16)):
            p4t[nm] = pre4_stack.enter_context(
                nc.sbuf_tensor("pre4_" + nm, shp, dt))
        ld4 = Sem(gst, nc, "pre4_ld")
        ctx["all"].append(ld4)
        def cdma4(dst, src):
            return P.op("gpsimd", lambda e, d=dst, s=src: e.dma_start(
                out=d, in_=s), inc=ld4, inck=16, ordered=False, serial=False)
        cdma4(p4t["wg1_s"][:, :], ext["p4_wg1p"][0])
        cdma4(p4t["wg2_s"][:, :], ext["p4_wg2c"][:, :])
        cdma4(p4t["ld_s"][:, :], ext["p4_ldc"][:, :])
        ld4_const = cdma4(p4t["id_s"][:, :], ext["p4_identb"][:, :])
        ctx["pre4"] = ((p4t["wg1_s"], p4t["wg2_s"], p4t["ld_s"], p4t["id_s"]),
                       ld4, ld4_const)
        build_l3(ctx)
        cc6h = [cc("AllGather", ALU.bypass, ext[f"p3_xn1c{h}"], xn1agh[h],
                   waits=[ctx["xn1_st"][h]]) for h in range(2)]
        ctx["xn1_join"] = [(cc6h[0], 1), (cc6h[1], 1)]
        bar()
        build_l4(ctx)
        pre4_stack.close()
        bar()
        cc7 = cc("AllToAll", ALU.bypass, o1li, o1rc)
        ctx["o1_join"] = (cc7, 1)
        for cs in ccw:
            ccjoin(cs, ["gpsimd"])
        build_l6(ctx)
        from concourse.aot_env import allow_ici_all_to_all
        with allow_ici_all_to_all():
            P.run(nc)
    return nc


def kernel_fused(inputs):
    g = _prep(inputs)
    nc = build_fused()
    maps = []
    for c in range(NCORES):
        m = {"p1_xfm": g[f"xfm_{c}"], "p1_wgt": g["wgt"]}
        for nm in ["wqp", "wkp", "wv", "bq", "bk", "bv", "sjtab", "mshtab", "negtab"]:
            m["p2_" + nm] = g[f"{nm}_{c}"]
        m["p2_diagmask"] = g["diagmask"]
        m["p3_wo0p"] = g["wo0pg"]
        for nm in ["bo0c", "w0ap", "w1ap", "w0bp", "w1bp",
                   "b0ac", "b1ac", "b0bc", "b1bc"]:
            m["p3_" + nm] = g[nm]
        m["p4_wq1p"] = g[f"wq1p_{c}"]
        m["p4_wk1p"] = g[f"wk1p_{c}"]
        m["p4_wv1"] = g[f"wv1_{c}"]
        m["p4_ldc"] = g[f"ld_{c}"]
        m["p4_wg1p"] = g["wg1p"]
        m["p4_wg2c"] = g[f"wg2_{c}"]
        m["p4_identb"] = g["identb"]
        for i in (1, 2, 3):
            m[f"p5_w{i}s"] = g[f"w{i}s_{c}"]
        m["p6_wo1p"] = g["wo1p"]
        m["p6_nwc"] = g["nwc"]
        maps.append(m)
    res = run_bass_kernel_spmd(nc, maps, core_ids=list(range(NCORES)))
    out = np.concatenate([np.asarray(res.results[c]["p6_outc"])
                          for c in range(NCORES)], 1)
    return np.ascontiguousarray(out.T).reshape(1, S, D)



def _run(nc, maps):
    return run_bass_kernel_spmd(nc, maps, core_ids=list(range(NCORES)))


def kernel(**inputs):
    return kernel_fused(inputs)


def kernel_unfused(**inputs):
    g = _prep(inputs)
    cores = list(range(NCORES))

    r1 = _run(build_l1(), [{"xfm": g[f"xfm_{c}"], "wgt": g["wgt"]} for c in cores])
    xn0c = [np.concatenate([np.asarray(r1.results[c]["xn0c0"]),
                            np.asarray(r1.results[c]["xn0c1"])], 1)
            for c in cores]
    maskc = [np.asarray(r1.results[c]["maskc"]) for c in cores]
    xn0full = np.ascontiguousarray(np.concatenate(xn0c, 1))

    r5 = _run(build_l5(), [{"w1s": g[f"w1s_{c}"], "w2s": g[f"w2s_{c}"],
                            "w3s": g[f"w3s_{c}"]} for c in cores])
    w1qp = np.ascontiguousarray(np.stack(
        [np.asarray(r5.results[c]["w1q"]) for c in cores]).reshape(44, 128, D))
    w2qp = np.ascontiguousarray(np.stack(
        [np.asarray(r5.results[c]["w2q"]) for c in cores]).reshape(44, 128, D))
    w3qp = np.ascontiguousarray(np.stack(
        [np.asarray(r5.results[c]["w3q"]) for c in cores]).reshape(FT, 128, FF1))

    maps2 = []
    for c in cores:
        maps2.append({"xn0full": xn0full, "wqp": g[f"wqp_{c}"], "wkp": g[f"wkp_{c}"],
                      "wv": g[f"wv_{c}"], "bq": g[f"bq_{c}"], "bk": g[f"bk_{c}"],
                      "bv": g[f"bv_{c}"], "sjtab": g[f"sjtab_{c}"],
                      "mshtab": g[f"mshtab_{c}"], "negtab": g[f"negtab_{c}"],
                      "diagmask": g["diagmask"]})
    r2 = _run(build_l2(), maps2)
    O0 = np.zeros((D, S), BF)
    for c in cores:
        oc = np.asarray(r2.results[c]["o0c"])
        for i_h, h in enumerate(heads_of(c)):
            O0[h * 128:(h + 1) * 128] = oc[i_h * 128:(i_h + 1) * 128]

    maps3 = []
    for c in cores:
        r0, r1_ = c * ROWS, (c + 1) * ROWS
        m = {"xfm": g[f"xfm_{c}"], "xn0loc0": xn0c[c][:, :512].copy(),
             "xn0loc1": xn0c[c][:, 512:].copy(),
             "o0loc": np.ascontiguousarray(O0[:, r0:r1_]), "maskr": maskc[c]}
        for nm in ["wo0p", "bo0c", "w0ap", "w1ap", "w0bp", "w1bp",
                   "b0ac", "b1ac", "b0bc", "b1bc"]:
            m[nm] = g[nm]
        maps3.append(m)
    r3 = _run(build_l3(), maps3)
    x1c = [np.asarray(r3.results[c]["x1fm"]) for c in cores]
    xn1c = [np.concatenate([np.asarray(r3.results[c]["xn1c0"]),
                            np.asarray(r3.results[c]["xn1c1"])], 1)
            for c in cores]
    xn1full = np.ascontiguousarray(np.concatenate(xn1c, 1))

    maps4 = []
    for c in cores:
        maps4.append({"xn1full": xn1full, "wq1p": g[f"wq1p_{c}"],
                      "wk1p": g[f"wk1p_{c}"], "wv1": g[f"wv1_{c}"],
                      "ldc": g[f"ld_{c}"], "wg1p": g["wg1p"],
                      "wg2c": g[f"wg2_{c}"], "identb": g["identb"]})
    r4 = _run(build_l4(), maps4)
    o1full = np.zeros((S, D), BF)
    for c in cores:
        oc = np.asarray(r4.results[c]["o1c"])
        for i_h, h in enumerate(heads_of(c)):
            o1full[:, h * 128:(h + 1) * 128] = oc[:, i_h * 128:(i_h + 1) * 128]

    maps6 = []
    for c in cores:
        r0, r1_ = c * ROWS, (c + 1) * ROWS
        maps6.append({"x1fm": x1c[c], "xn1loc0": xn1c[c][:, :512].copy(),
                      "xn1loc1": xn1c[c][:, 512:].copy(),
                      "o1loc": np.ascontiguousarray(o1full[r0:r1_]),
                      "wo1p": g["wo1p"], "nwc": g["nwc"],
                      "w1qp": w1qp, "w2qp": w2qp, "w3qp": w3qp})
    r6 = _run(build_l6(), maps6)
    out = np.concatenate([np.asarray(r6.results[c]["outc"]) for c in cores], 1)
    return np.ascontiguousarray(out.T).reshape(1, S, D)
